# version 04 (final)
# baseline (speedup 1.0000x reference)
import numpy as np
import ml_dtypes
import concourse.bass as bass
import concourse.mybir as mybir
from concourse.bass_utils import run_bass_kernel_spmd
from contextlib import ExitStack

F32 = mybir.dt.float32
BF16 = mybir.dt.bfloat16
I32 = mybir.dt.int32
AF = mybir.ActivationFunctionType
ALU = mybir.AluOpType
AX = mybir.AxisListType.X
BF = ml_dtypes.bfloat16

ENGINES = ("tensor", "vector", "scalar", "gpsimd", "sync")


class Tok:
    __slots__ = ("sem_key", "cnt")

    def __init__(self, sem_key, cnt):
        self.sem_key = sem_key
        self.cnt = cnt


class Prog:

    def __init__(self, nc, ctx):
        self.nc = nc
        self.ctx = ctx
        self.sems = {e: ctx.enter_context(nc.semaphore(f"cs_{e}")) for e in ENGINES}
        self.ccnt = {e: 0 for e in ENGINES}
        self.dstream = {}
        self.pending = {e: False for e in ENGINES}
        self.stream = {e: [] for e in ENGINES}
        self.wm = {e: {} for e in ENGINES}
        self.last_w = {}
        self.readers = {}
        self.nops = 0

    def _wait(self, eng, tok):
        if tok is None:
            return
        if isinstance(tok.sem_key, tuple) and tok.sem_key[0] == "dma":
            tok = Tok(tok.sem_key, self.dstream[tok.sem_key])
        if tok.sem_key == eng and tok.cnt > self.ccnt[eng]:
            assert eng == "tensor", f"pending same-engine dep on {eng}"
            return
        if self.wm[eng].get(tok.sem_key, 0) >= tok.cnt:
            return
        self.wm[eng][tok.sem_key] = tok.cnt
        sem = self.sems[tok.sem_key]
        cnt = tok.cnt
        self.stream[eng].append(lambda e, sem=sem, cnt=cnt: e.wait_ge(sem, cnt))

    def _deps(self, eng, r, w, extra=()):
        toks = [t for t in extra if t is not None]
        for k in r:
            t = self.last_w.get(k)
            if t is not None:
                toks.append(t)
        for k in w:
            t = self.last_w.get(k)
            if t is not None:
                toks.append(t)
            toks.extend(self.readers.get(k, ()))
        best = {}
        for t in toks:
            if best.get(t.sem_key) is None or best[t.sem_key].cnt < t.cnt:
                best[t.sem_key] = t
        for t in best.values():
            self._wait(eng, t)

    def _book(self, tok, r, w):
        for k in w:
            self.last_w[k] = tok
            self.readers[k] = []
        for k in r:
            self.readers.setdefault(k, []).append(tok)

    def op(self, eng, fn, r=(), w=(), inc=True, deps=()):
        self._deps(eng, r, w, deps)
        if inc:
            self.ccnt[eng] += 1
            sem = self.sems[eng]
            self.stream[eng].append(lambda e, fn=fn, sem=sem: fn(e).then_inc(sem, 1))
            self.pending[eng] = False
            tok = Tok(eng, self.ccnt[eng])
        else:
            assert eng == "tensor", "inc=False only for PE accumulation groups"
            self.stream[eng].append(fn)
            self.pending[eng] = True
            tok = Tok(eng, self.ccnt[eng] + 1)
        self.nops += 1
        self._book(tok, r, w)
        return tok

    def dma(self, eng, fn, r=(), w=(), stream=None, deps=()):
        if stream is None:
            stream = "d_" + eng
        skey = ("dma", eng, stream)
        if skey not in self.sems:
            self.sems[skey] = self.ctx.enter_context(
                self.nc.semaphore(f"d_{eng}_{stream}"))
            self.dstream[skey] = 0
        self._deps(eng, r, w, deps)
        if self.dstream[skey] > 0:
            self._wait(eng, Tok(skey, self.dstream[skey]))
        self.dstream[skey] += 16
        sem = self.sems[skey]
        self.stream[eng].append(lambda e, fn=fn, sem=sem: fn(e).then_inc(sem, 16))
        tok = Tok(skey, self.dstream[skey])
        self.nops += 1
        self._book(tok, r, w)
        return tok

    def cc(self, fn, r=(), w=()):
        skey = ("cc", len([k for k in self.sems if isinstance(k, tuple) and k[0] == "cc"]))
        self.sems[skey] = self.ctx.enter_context(self.nc.semaphore(f"cc{skey[1]}"))
        self._deps("gpsimd", r, w)
        sem = self.sems[skey]
        self.stream["gpsimd"].append(lambda e, fn=fn, sem=sem: fn(e).then_inc(sem, 1))
        tok = Tok(skey, 1)
        self.nops += 1
        self._book(tok, r, w)
        return tok

    def key_toks(self, key):
        toks = []
        t = self.last_w.get(key)
        if t is not None:
            toks.append(t)
        toks.extend(self.readers.get(key, ()))
        return toks

    def key_toks_prefix(self, prefix):
        toks = []
        pc = prefix + ":"
        for k, t in self.last_w.items():
            if k == prefix or k.startswith(pc):
                toks.append(t)
        for k, rl in self.readers.items():
            if k == prefix or k.startswith(pc):
                toks.extend(rl)
        return toks

    def close_group(self, eng):
        if self.pending[eng]:
            self.ccnt[eng] += 1
            sem = self.sems[eng]
            self.stream[eng].append(
                lambda e, sem=sem:
                    getattr(e, "engine_nop", getattr(e, "nop", None))().then_inc(sem, 1))
            self.pending[eng] = False

    def wait_all(self, eng):
        for f in ENGINES:
            if f != eng and self.ccnt[f] > 0:
                self.close_group(f)
                self._wait(eng, Tok(f, self.ccnt[f]))
        for skey, cnt in self.dstream.items():
            if cnt > 0:
                self._wait(eng, Tok(skey, cnt))

    def fence(self):
        for e in ENGINES:
            self.close_group(e)
        for e in ENGINES:
            if self.stream[e]:
                self.wait_all(e)

    def build(self):
        for e in ENGINES:
            assert not self.pending[e], f"engine {e} has uncovered group ops"
        block = self.ctx.enter_context(self.nc.Block())
        for ename in ENGINES:
            closures = self.stream[ename]
            if not closures:
                continue

            def section(eng_obj, closures=closures):
                for c in closures:
                    c(eng_obj)

            getattr(block, ename)(section)
        return block


class Arena:

    def __init__(self, P, tensor, key, n):
        self.P = P
        self.t = tensor
        self.key = key
        self.n = n
        self.cur = 0
        self.phase = 0
        self.live = []

    def pin(self, datakey):
        self.live = [(s, e, k, True if k == datakey else p, al, ph) for (s, e, k, p, al, ph) in self.live]

    def unpin(self, datakey):
        self.live = [(s, e, k, False if k == datakey else p, al, ph) for (s, e, k, p, al, ph) in self.live]

    def begin_phase(self):
        self.phase += 1

    def reset(self):
        self.cur = 0

    def alloc(self, name, n, parts=128, aliases=(), at=None):
        if at is not None:
            start, end = at, at + n
            assert end <= self.n
            deps = []
            keep = []
            for (s, e, k, p, al, ph) in self.live:
                if s < end and e > start:
                    assert not p and ph < self.phase, (
                        f"arena {self.key}: {name}@{at} overlaps live {k}")
                    deps.extend(self.P.key_toks_prefix(k))
                    for a in al:
                        deps.extend(self.P.key_toks_prefix(a))
                else:
                    keep.append((s, e, k, p, al, ph))
            dk = f"{self.key}:{name}"
            keep.append((start, end, dk, False, tuple(aliases), self.phase))
            self.live = keep
            return self.t[0:parts, start:end], dk, deps
        tries = 0
        while True:
            tries += 1
            assert tries < 200, f"arena {self.key}: cannot place {name}({n})"
            if self.cur + n > self.n:
                self.cur = 0
            start, end = self.cur, self.cur + n
            conflict = None
            for (s, e, k, p, al, ph) in self.live:
                if (p or ph == self.phase) and s < end and e > start:
                    conflict = e
                    break
            if conflict is None:
                break
            self.cur = conflict
        self.cur = end
        deps = []
        keep = []
        for (s, e, k, p, al, ph) in self.live:
            if s < end and e > start:
                assert not p, f"arena {self.key}: {name} overlaps pinned {k}"
                assert ph < self.phase, (
                    f"arena {self.key}: {name} overlaps same-phase region {k}")
                deps.extend(self.P.key_toks_prefix(k))
                for a in al:
                    deps.extend(self.P.key_toks_prefix(a))
            else:
                keep.append((s, e, k, p, al, ph))
        dk = f"{self.key}:{name}"
        keep.append((start, end, dk, False, tuple(aliases), self.phase))
        self.live = keep
        return self.t[0:parts, start:end], dk, deps


HID = 2048
NH = 16
NOPE = 128
ROPE = 64
VD = 128
QC = 1536
KVC = 512
SM_SCALE = float((NOPE + ROPE) ** -0.5)
LN_EPS = 1e-5
RMS_EPS = 1e-6
nCC = HID // 128
nKV = KVC // 128
nQC = QC // 128


def _rope_tables(S):
    inv = 1.0 / (10000.0 ** (np.arange(0, ROPE, 2, dtype=np.float64) / ROPE))
    t = np.arange(S, dtype=np.float64)
    freqs = t[:, None] * inv[None, :]
    emb = np.concatenate([freqs, freqs], axis=-1)
    return np.cos(emb).astype(np.float32), np.sin(emb).astype(np.float32)


def _rot_mat():
    RT = np.zeros((128, 128), dtype=np.float32)
    for m in range(128):
        if (m % 64) < 32:
            RT[m + 32, m] = -1.0
        else:
            RT[m - 32, m] = 1.0
    return RT.astype(BF)




class Env:

    def __init__(self, nc, ctx, fused, ab_cols=51200, af_cols=26368, final_outs=("outT",)):
        self.nc = nc
        self.ctx = ctx
        self.fused = fused
        self.wiring = {}
        self.final_outs = final_outs
        self.P = Prog(nc, ctx)
        sb = lambda n, shp, dt: ctx.enter_context(nc.sbuf_tensor(n, shp, dt))
        ps = lambda n, shp, dt: ctx.enter_context(nc.psum_tensor(n, shp, dt))
        self.AB = Arena(self.P, sb("ab", [128, ab_cols], BF16), "ab", ab_cols)
        self.AF = Arena(self.P, sb("af", [128, af_cols], F32), "af", af_cols)
        self.pb = [ps(f"pb{i}", [128, 512], F32) for i in range(4)]
        self.pt = [ps(f"pt{i}", [128, 1024], BF16) for i in range(2)]
        self.px = [ps(f"px{i}", [128, 512], F32) for i in range(2)]
        self.onesf_r = sb("onesf_r", [1, 128], F32)
        self.ones_c = sb("ones_c", [128, 1], F32)
        self.onesb_c = sb("onesb_c", [128, 1], BF16)
        self.idf = sb("idf", [128, 128], F32)
        self.idb = sb("idb", [128, 128], BF16)
        self._dram = {}
        P = self.P
        P.op("gpsimd", lambda e: e.memset(self.onesf_r[:], 1.0), w=["onesf_r"])
        P.op("gpsimd", lambda e: e.memset(self.ones_c[:], 1.0), w=["ones_c"])
        P.op("gpsimd", lambda e: e.memset(self.onesb_c[:], 1.0), w=["onesb_c"])
        identf = self.inp("identf", [128, 128], F32)
        identb = self.inp("identb", [128, 128], BF16)
        P.dma("sync", lambda e: e.dma_start(out=self.idf[:], in_=identf), w=["idf"], stream="cst")
        P.dma("sync", lambda e: e.dma_start(out=self.idb[:], in_=identb), w=["idb"], stream="cst")

    def inp(self, name, shape, dtype):
        if name in self.wiring:
            return self.wiring[name]
        if name not in self._dram:
            self._dram[name] = self.nc.dram_tensor(name, shape, dtype, kind="ExternalInput").ap()
        return self._dram[name]

    def out(self, name, shape, dtype):
        if name not in self._dram:
            kind = "ExternalOutput" if (not self.fused or name in self.final_outs) else "Internal"
            self._dram[name] = self.nc.dram_tensor(name, shape, dtype, kind=kind).ap()
        return self._dram[name]

    def scratch(self, name, shape, dtype):
        if name not in self._dram:
            self._dram[name] = self.nc.dram_tensor(name, shape, dtype, kind="Internal").ap()
        return self._dram[name]

    def finish(self):
        self.P.wait_all("sync")
        self.P.build()


def _standalone(emit_fn, **kw):
    nc = bass.Bass("TRN2", target_bir_lowering=False)
    ctx = ExitStack()
    with ctx:
        E = Env(nc, ctx, fused=False)
        emit_fn(E, **kw)
        E.finish()
    return nc


def _wload(P, w_sb, key, deps, hbm_ap, nchunk, width, stream):
    P.dma("gpsimd", lambda e: e.dma_start(
        out=w_sb.rearrange("p (cc f) -> p cc f", cc=nchunk),
        in_=hbm_ap.rearrange("(cc p) f -> p cc f", p=128)),
        w=[key], deps=deps, stream=stream)


def emit_a(E, S=4096):
    So = S // 2
    nTT = S // 128
    nTTo = So // 128
    nTB = S // 512
    nTBo = So // 512
    nKC = S // 512
    nKC128 = S // 128
    nQT = So // 128

    nc, P, AB, AF_ = E.nc, E.P, E.AB, E.AF
    AB.begin_phase()
    AF_.begin_phase()
    pb, pt = E.pb, E.pt
    pav, pmisc = E.px[0], E.px[1]
    idf, idb = E.idf, E.idb
    onesf_r, ones_c = E.onesf_r, E.ones_c
    x_tm = E.inp("x_tm", [S, HID], F32)
    wqdT = E.inp("wqdT", [HID, QC], F32)
    wqnT = E.inp("wqnT", [QC, NH * NOPE], F32)
    wqrT = E.inp("wqrT", [QC, NH * ROPE], F32)
    wkvdT = E.inp("wkvdT", [HID, KVC], F32)
    wknT = E.inp("wknT", [KVC, NH * NOPE], F32)
    wvuT = E.inp("wvuT", [KVC, NH * VD], F32)
    wkrT = E.inp("wkrT", [HID, ROPE], F32)
    woT = E.inp("woT", [NH * VD, HID], F32)
    qnw_r = E.inp("qnw_r", [nQC, 128], F32)
    kvnw_r = E.inp("kvnw_r", [nKV, 128], F32)
    cosq2 = E.inp("cosq2", [128, So], BF16)
    sinq2 = E.inp("sinq2", [128, So], BF16)
    cosk = E.inp("cosk", [ROPE, S], BF16)
    sink = E.inp("sink", [ROPE, S], BF16)
    rotm = E.inp("rotm", [128, 128], BF16)
    xn_out = E.out("xn_out", [So, HID], BF16)
    attn_outT = E.out("attn_outT", [HID, So], F32)
    xn_all = E.scratch("xn_all", [S, HID], BF16)
    ckv_a = E.scratch("ckv_a", [KVC, S], BF16)
    krope_a = E.scratch("krope_a", [ROPE, S], BF16)
    kn_a = E.scratch("kn_a", [NH * NOPE, S], BF16)
    v_tm = E.scratch("v_tm", [S, NH * VD], BF16)
    cq_a = E.scratch("cq_a", [QC, So], BF16)
    qn_a = E.scratch("qn_a", [NH * NOPE, So], BF16)
    qr_a = E.scratch("qr_a", [NH * ROPE, So], BF16)
    attf = E.scratch("attf", [NH * VD, So], BF16)
    if True:
        xnf, xnfk = [], []
        for i in range(2):
            s_, k_, d_ = AB.alloc(f"xnf{i}", nCC * 512)
            xnf.append(s_)
            xnfk.append(k_)
            AB.pin(k_)
        rotm_s, _, _ = AB.alloc("rotm_s", 128, aliases=["rotm"], at=AB.n - 128)
        AB.pin("ab:rotm_s")
        qnw_c, _, _ = AF_.alloc("qnw_c", nQC, aliases=["qnw_c"])
        AF_.pin("af:qnw_c")
        kvnw_c, _, _ = AF_.alloc("kvnw_c", nKV, aliases=["kvnw_c"])
        AF_.pin("af:kvnw_c")
        cosq_s, _, _ = AB.alloc("cosq_s", So, aliases=["cosq"], at=AB.n - 128 - So)
        AB.pin("ab:cosq_s")
        sinq_s, _, _ = AB.alloc("sinq_s", So, aliases=["sinq"], at=AB.n - 128 - 2 * So)
        AB.pin("ab:sinq_s")
        cosk_s, cosk_k, _ = AB.alloc("cosk_s", S, parts=ROPE, aliases=["cosk"])
        AB.pin("ab:cosk_s")
        sink_s, sink_k, _ = AB.alloc("sink_s", S, parts=ROPE, aliases=["sink"])
        AB.pin("ab:sink_s")
        stat, _, _ = AF_.alloc("stat", 16, aliases=["st0", "st1", "st2", "st4", "st5", "st6", "st7", "st8"])
        AF_.pin("af:stat")
        rrow, _, _ = AF_.alloc("rrow", 512, parts=1, aliases=["rrow"])
        AF_.pin("af:rrow")
        rbc, _, _ = AF_.alloc("rbc", 512, aliases=["rbc"])
        AF_.pin("af:rbc")
        mrow, _, _ = AF_.alloc("mrow", 4, aliases=["mr", "nmax"])
        AF_.pin("af:mrow")
        rsum, _, _ = AF_.alloc("rsum", 2, aliases=["rsum", "rrec"])
        AF_.pin("af:rsum")
        for name, dst, k in [("cosq2", cosq_s, "cosq"), ("sinq2", sinq_s, "sinq"),
                             ("cosk", cosk_s, "cosk"), ("sink", sink_s, "sink"),
                             ("rotm", rotm_s, "rotm")]:
            src = {"cosq2": cosq2, "sinq2": sinq2, "cosk": cosk, "sink": sink,
                   "rotm": rotm}[name]
            P.dma("sync", lambda e, dst=dst, src=src: e.dma_start(out=dst[0:src.tensor.shape[0], :], in_=src),
                  w=[k], stream="cst")
        nwr, _, _ = AF_.alloc("nwr", 128, parts=nQC, aliases=["nwr"])
        AF_.pin("af:nwr")
        P.dma("sync", lambda e: e.dma_start(out=nwr[0:nQC, :], in_=qnw_r), w=["nwr"], stream="cst")
        P.op("tensor", lambda t: t.transpose(pmisc[:, 0:nQC], nwr[0:nQC, :], idf[0:nQC, 0:nQC]),
             r=["nwr", "idf"], w=["px1"])
        P.op("vector", lambda v: v.tensor_copy(qnw_c[:], pmisc[:, 0:nQC]),
             r=["px1"], w=["qnw_c"])
        P.dma("sync", lambda e: e.dma_start(out=nwr[0:nKV, :], in_=kvnw_r),
              r=["nwr"], w=["nwr"], stream="cst")
        P.op("tensor", lambda t: t.transpose(pmisc[:, 0:nKV], nwr[0:nKV, :], idf[0:nKV, 0:nKV]),
             r=["nwr", "idf", "qnw_c"], w=["px1"])
        P.op("vector", lambda v: v.tensor_copy(kvnw_c[:], pmisc[:, 0:nKV]),
             r=["px1"], w=["kvnw_c"])

        xa, xak, dp = zip(*[AF_.alloc(f"xa{i}", HID) for i in range(2)])
        assert not dp[0] and not dp[1]
        dumpt, dumpk, _ = AF_.alloc("dump", HID)
        xnt, xntk = [], []
        for i in range(2):
            s_, k_, d_ = AB.alloc(f"xnt{i}", HID)
            xnt.append(s_)
            xntk.append(k_)
        AB.pin("ab:xnt0"), AB.pin("ab:xnt1")
        for tt in range(nTT):
            bi = tt % 2
            P.dma("sync", lambda e, tt=tt, bi=bi: e.dma_start(
                out=xa[bi][:], in_=x_tm[tt * 128:(tt + 1) * 128, :]),
                w=[xak[bi]], stream=f"x{bi}")
            P.op("scalar", lambda s, bi=bi: s.activation(
                dumpt[:], xa[bi][:], AF.Square, accum_out=stat[:, 0:1]),
                r=[xak[bi]], w=[dumpk, "st0"])
            P.op("vector", lambda v, bi=bi: v.reduce_sum(stat[:, 1:2], xa[bi][:], axis=AX),
                 r=[xak[bi]], w=["st1"])
            P.op("vector", lambda v: v.tensor_scalar(stat[:, 2:3], stat[:, 1:2], 1.0 / HID, None, ALU.mult),
                 r=["st1"], w=["st2"])
            P.op("vector", lambda v: v.tensor_tensor(stat[:, 4:5], stat[:, 2:3], stat[:, 2:3], ALU.mult),
                 r=["st2"], w=["st4"])
            P.op("vector", lambda v: v.scalar_tensor_tensor(
                stat[:, 5:6], stat[:, 0:1], 1.0 / HID, stat[:, 4:5], ALU.mult, ALU.subtract),
                r=["st0", "st4"], w=["st5"])
            P.op("vector", lambda v: v.tensor_scalar(stat[:, 8:9], stat[:, 5:6], LN_EPS, None, ALU.add),
                 r=["st5"], w=["st8"])
            P.op("scalar", lambda s: s.activation(stat[:, 6:7], stat[:, 8:9], AF.Sqrt),
                 r=["st8"], w=["st6"])
            P.op("vector", lambda v: v.reciprocal(stat[:, 7:8], stat[:, 6:7]),
                 r=["st6"], w=["st7"])
            P.op("vector", lambda v, bi=bi: v.tensor_scalar(
                xnt[bi][:], xa[bi][:], stat[:, 2:3], stat[:, 7:8], ALU.subtract, ALU.mult),
                r=[xak[bi], "st2", "st7"], w=[xntk[bi]])
            P.dma("gpsimd", lambda e, tt=tt, bi=bi: e.dma_start(
                out=xn_all[tt * 128:(tt + 1) * 128, :], in_=xnt[bi][:]),
                r=[xntk[bi]], w=[f"xn:{tt}"], stream=f"st{bi}")
            if tt < nTTo:
                P.dma("gpsimd", lambda e, tt=tt, bi=bi: e.dma_start(
                    out=xn_out[tt * 128:(tt + 1) * 128, :], in_=xnt[bi][:]),
                    r=[xntk[bi]], stream=f"st{bi}")
        AB.unpin("ab:xnt0"), AB.unpin("ab:xnt1")
        xn_tok = [f"xn:{t}" for t in range(nTT)]


        def load_xnf(tb, sl, extra_deps=()):
            ks = []
            for cc in range(nCC):
                t = P.dma("scalar", lambda e, tb=tb, cc=cc, sl=sl: e.dma_start(
                    out=xnf[sl][:, cc * 512:(cc + 1) * 512],
                    in_=xn_all[tb * 512:(tb + 1) * 512, cc * 128:(cc + 1) * 128],
                    transpose=True),
                    r=[xn_tok[tb * 4 + j] for j in range(4)],
                    w=[f"{xnfk[sl]}:{cc}"], stream=f"xnf{sl}", deps=extra_deps)
                ks.append(f"{xnfk[sl]}:{cc}")
            return ks

        wkvd_s, wkvd_k, dp = AB.alloc("wkvd", nCC * KVC)
        _wload(P, wkvd_s, wkvd_k, dp, wkvdT, nCC, KVC, "w0")
        wkr_s, wkr_k, dp = AB.alloc("wkr", nCC * ROPE)
        _wload(P, wkr_s, wkr_k, dp, wkrT, nCC, ROPE, "w1")
        ckvb, ckvbk, _ = AB.alloc("ckvb", nKV * 512)
        AB.pin("ab:ckvb")
        krb, krbk, _ = AB.alloc("krb", 512)
        AB.pin("ab:krb")
        krsb, krsbk, _ = AB.alloc("krsb", 512)
        AB.pin("ab:krsb")
        evt, evk = [], []
        for i in range(4):
            s_, k_, _ = AF_.alloc(f"ev{i}", 512)
            evt.append(s_)
            evk.append(k_)
        sqt, sqk, _ = AF_.alloc("sq", 512)
        krt, krtk, _ = AF_.alloc("krt", 512)

        def rms_norm_store(tb, n_f, inv_n, gcol, gkey, out_hbm, out_pref, psum_keys, st):
            for fc in range(n_f):
                P.op("scalar", lambda s, fc=fc: s.activation(evt[fc][:], pb[fc][:], AF.Square),
                     r=[psum_keys[fc]], w=[evk[fc]])
                P.op("tensor", lambda t, fc=fc: t.matmul(
                    pmisc[0:1, :], ones_c[:], evt[fc][:],
                    start=(fc == 0), stop=(fc == n_f - 1)),
                    r=[evk[fc], "ones_c"], w=["px1"], inc=(fc == n_f - 1))
            P.op("vector", lambda v: v.tensor_scalar(rrow[:], pmisc[0:1, :], inv_n, RMS_EPS, ALU.mult, ALU.add),
                 r=["px1"], w=["rrow"])
            P.op("scalar", lambda s: s.activation(rrow[:], rrow[:], AF.Sqrt),
                 r=["rrow"], w=["rrow"])
            P.op("vector", lambda v: v.reciprocal(rrow[:], rrow[:]), r=["rrow"], w=["rrow"])
            P.op("tensor", lambda t: t.matmul(pmisc[:, :], onesf_r[:], rrow[:],
                                              start=True, stop=True),
                 r=["rrow", "onesf_r", "px1"], w=["px1"])
            P.op("vector", lambda v: v.tensor_copy(rbc[:], pmisc[:, :]), r=["px1"], w=["rbc"])
            outks = []
            for fc in range(n_f):
                P.op("vector", lambda v, fc=fc: v.scalar_tensor_tensor(
                    ckvb[:, fc * 512:(fc + 1) * 512], pb[fc][:], gcol[:, fc:fc + 1],
                    rbc[:], ALU.mult, ALU.mult),
                    r=[psum_keys[fc], gkey, "rbc"], w=[f"{ckvbk}:{fc}"])
                ok = f"{out_pref}:{tb}:{fc}"
                P.dma("gpsimd", lambda e, fc=fc, tb=tb: e.dma_start(
                    out=out_hbm[fc * 128:(fc + 1) * 128, tb * 512:(tb + 1) * 512],
                    in_=ckvb[:, fc * 512:(fc + 1) * 512]),
                    r=[f"{ckvbk}:{fc}"], w=[ok], stream=f"st{fc % 2}")
                outks.append(ok)
            return outks

        for tb in range(nTB):
            sl = tb % 2
            xk = load_xnf(tb, sl)
            for fc in range(nKV):
                for cc in range(nCC):
                    P.op("tensor", lambda t, fc=fc, cc=cc, sl=sl: t.matmul(
                        pb[fc][:], wkvd_s[:, cc * KVC + fc * 128: cc * KVC + fc * 128 + 128],
                        xnf[sl][:, cc * 512:(cc + 1) * 512],
                        start=(cc == 0), stop=(cc == nCC - 1)),
                        r=[wkvd_k, xk[cc]], w=[f"pb{fc}"], inc=(cc == nCC - 1))
            rms_norm_store(tb, nKV, 1.0 / KVC, kvnw_c, "kvnw_c", ckv_a, "ckv",
                           [f"pb{i}" for i in range(nKV)], "st")
            for cc in range(nCC):
                P.op("tensor", lambda t, cc=cc, sl=sl: t.matmul(
                    pav[0:ROPE, :], wkr_s[:, cc * ROPE:(cc + 1) * ROPE],
                    xnf[sl][:, cc * 512:(cc + 1) * 512],
                    start=(cc == 0), stop=(cc == nCC - 1)),
                    r=[wkr_k, xk[cc]], w=["px0"], inc=(cc == nCC - 1))
            P.op("vector", lambda v: v.tensor_copy(krsb[0:64, :], pav[0:ROPE, :]),
                 r=["px0"], w=[f"{krsbk}"])
            P.op("tensor", lambda t: t.matmul(pmisc[0:64, :], rotm_s[0:64, 0:64],
                                              krsb[0:64, :], start=True, stop=True),
                 r=[krsbk, "rotm"], w=["px1"])
            P.op("vector", lambda v, tb=tb: v.tensor_tensor(
                krt[0:64, :], krsb[0:64, :], cosk_s[:, tb * 512:(tb + 1) * 512], ALU.mult),
                r=[krsbk, "cosk"], w=[krtk])
            P.op("vector", lambda v, tb=tb: v.tensor_tensor(
                krb[0:64, :], pmisc[0:64, :], sink_s[:, tb * 512:(tb + 1) * 512], ALU.mult),
                r=["px1", "sink"], w=[krbk])
            P.op("vector", lambda v: v.tensor_tensor(
                krb[0:64, :], krb[0:64, :], krt[0:64, :], ALU.add),
                r=[krbk, krtk], w=[krbk])
            P.dma("gpsimd", lambda e, tb=tb: e.dma_start(
                out=krope_a[:, tb * 512:(tb + 1) * 512], in_=krb[0:64, :]),
                r=[krbk], w=[f"kro:{tb}"], stream="st2")

        AB.unpin("ab:ckvb")
        AB.unpin("ab:krb")
        AB.unpin("ab:krsb")
        AB.unpin("ab:cosk_s")
        AB.unpin("ab:sink_s")
        AB.begin_phase()
        AF_.begin_phase()
        wkn_s, wkn_k, dp = AB.alloc("wkn", nKV * NH * NOPE)
        _wload(P, wkn_s, wkn_k, dp, wknT, nKV, NH * NOPE, "w0")
        wvu_s, wvu_k, dp = AB.alloc("wvu", nKV * NH * VD)
        _wload(P, wvu_s, wvu_k, dp, wvuT, nKV, NH * VD, "w1")
        ckvf, ckvfk = [], []
        for i in range(2):
            s_, k_, d_ = AB.alloc(f"ckvf{i}", nKV * 512)
            ckvf.append(s_)
            ckvfk.append(k_)
            assert not d_ or True
        knb, knbk, _ = AB.alloc("knb", 512)
        vtt, vttk, _ = AB.alloc("vtt", NH * VD)
        for tb in range(nTB):
            sl = tb % 2
            P.dma("sync", lambda e, tb=tb, sl=sl: e.dma_start(
                out=ckvf[sl].rearrange("p (cc t) -> p cc t", cc=nKV),
                in_=ckv_a[:, tb * 512:(tb + 1) * 512].rearrange("(cc p) t -> p cc t", p=128)),
                r=[f"ckv:{tb}:{fc}" for fc in range(nKV)], w=[ckvfk[sl]], stream=f"cf{sl}")
            for h in range(NH):
                pk = f"pb{h % 2}"
                for cc in range(nKV):
                    P.op("tensor", lambda t, h=h, cc=cc, sl=sl: t.matmul(
                        pb[h % 2][:], wkn_s[:, cc * NH * NOPE + h * 128: cc * NH * NOPE + h * 128 + 128],
                        ckvf[sl][:, cc * 512:(cc + 1) * 512],
                        start=(cc == 0), stop=(cc == nKV - 1)),
                        r=[wkn_k, ckvfk[sl]], w=[pk], inc=(cc == nKV - 1))
                P.op("vector", lambda v, h=h: v.tensor_copy(knb[:], pb[h % 2][:]),
                     r=[pk], w=[knbk])
                P.dma("gpsimd", lambda e, h=h, tb=tb: e.dma_start(
                    out=kn_a[h * 128:(h + 1) * 128, tb * 512:(tb + 1) * 512], in_=knb[:]),
                    r=[knbk], w=[f"kn:{tb}:{h}"], stream="st0")
            for t4 in range(4):
                for h in range(NH):
                    pk = f"pb{2 + h % 2}"
                    for cc in range(nKV):
                        P.op("tensor", lambda t, h=h, cc=cc, t4=t4, sl=sl: t.matmul(
                            pb[2 + h % 2][:, 0:128],
                            ckvf[sl][:, cc * 512 + t4 * 128: cc * 512 + t4 * 128 + 128],
                            wvu_s[:, cc * NH * VD + h * 128: cc * NH * VD + h * 128 + 128],
                            start=(cc == 0), stop=(cc == nKV - 1)),
                            r=[wvu_k, ckvfk[sl]], w=[pk], inc=(cc == nKV - 1))
                    P.op("vector", lambda v, h=h: v.tensor_copy(
                        vtt[:, h * 128:(h + 1) * 128], pb[2 + h % 2][:, 0:128]),
                        r=[pk], w=[f"{vttk}:{h}"])
                P.dma("gpsimd", lambda e, tb=tb, t4=t4: e.dma_start(
                    out=v_tm[tb * 512 + t4 * 128: tb * 512 + (t4 + 1) * 128, :],
                    in_=vtt[:]),
                    r=[f"{vttk}:{h}" for h in range(NH)], w=[f"v:{tb}:{t4}"], stream="st1")

        AB.begin_phase()
        AF_.begin_phase()
        wqd_s, wqd_k, dp = AB.alloc("wqd", nCC * QC)
        _wload(P, wqd_s, wqd_k, dp, wqdT, nCC, QC, "w0")
        cqr, cqrk, _ = AF_.alloc("cqr", nQC * 512)
        cqb, cqbk, _ = AB.alloc("cqb", 512)
        for tb in range(nTBo):
            sl = tb % 2
            xk = load_xnf(tb, sl)
            for fc in range(nQC):
                pk = f"pb{fc % 4}"
                for cc in range(nCC):
                    P.op("tensor", lambda t, fc=fc, cc=cc, sl=sl: t.matmul(
                        pb[fc % 4][:], wqd_s[:, cc * QC + fc * 128: cc * QC + fc * 128 + 128],
                        xnf[sl][:, cc * 512:(cc + 1) * 512],
                        start=(cc == 0), stop=(cc == nCC - 1)),
                        r=[wqd_k, xk[cc]], w=[pk], inc=(cc == nCC - 1))
                P.op("vector", lambda v, fc=fc: v.tensor_copy(
                    cqr[:, fc * 512:(fc + 1) * 512], pb[fc % 4][:]),
                    r=[pk], w=[f"{cqrk}:{fc}"])
                P.op("scalar", lambda s, fc=fc: s.activation(
                    sqt[:], cqr[:, fc * 512:(fc + 1) * 512], AF.Square),
                    r=[f"{cqrk}:{fc}"], w=[sqk])
                P.op("tensor", lambda t, fc=fc: t.matmul(
                    pmisc[0:1, :], ones_c[:], sqt[:],
                    start=(fc == 0), stop=(fc == nQC - 1)),
                    r=[sqk, "ones_c"], w=["px1"], inc=(fc == nQC - 1))
            P.op("vector", lambda v: v.tensor_scalar(rrow[:], pmisc[0:1, :], 1.0 / QC, RMS_EPS, ALU.mult, ALU.add),
                 r=["px1"], w=["rrow"])
            P.op("scalar", lambda s: s.activation(rrow[:], rrow[:], AF.Sqrt),
                 r=["rrow"], w=["rrow"])
            P.op("vector", lambda v: v.reciprocal(rrow[:], rrow[:]), r=["rrow"], w=["rrow"])
            P.op("tensor", lambda t: t.matmul(pmisc[:, :], onesf_r[:], rrow[:],
                                              start=True, stop=True),
                 r=["rrow", "onesf_r", "px1"], w=["px1"])
            P.op("vector", lambda v: v.tensor_copy(rbc[:], pmisc[:, :]), r=["px1"], w=["rbc"])
            for fc in range(nQC):
                P.op("vector", lambda v, fc=fc: v.scalar_tensor_tensor(
                    cqb[:], cqr[:, fc * 512:(fc + 1) * 512], qnw_c[:, fc:fc + 1],
                    rbc[:], ALU.mult, ALU.mult),
                    r=[f"{cqrk}:{fc}", "qnw_c", "rbc"], w=[cqbk])
                P.dma("gpsimd", lambda e, fc=fc, tb=tb: e.dma_start(
                    out=cq_a[fc * 128:(fc + 1) * 128, tb * 512:(tb + 1) * 512], in_=cqb[:]),
                    r=[cqbk], w=[f"cq:{tb}:{fc}"], stream="st0")

        AB.unpin(xnfk[0])
        AB.unpin(xnfk[1])
        AB.begin_phase()
        AF_.begin_phase()
        wqn_s, wqn_k, dp = AB.alloc("wqn", nQC * NH * NOPE)
        _wload(P, wqn_s, wqn_k, dp, wqnT, nQC, NH * NOPE, "w0")
        wqr_s, wqr_k, dp = AB.alloc("wqr", nQC * NH * ROPE)
        _wload(P, wqr_s, wqr_k, dp, wqrT, nQC, NH * ROPE, "w1")
        cqf, cqfk = [], []
        for i in range(1):
            s_, k_, d_ = AB.alloc(f"cqf{i}", nQC * 512)
            cqf.append(s_)
            cqfk.append(k_)
            if d_:
                pass
            AB.pin(k_)
        qnb, qnbk, _ = AB.alloc("qnb", 512)
        krs, krsk, _ = AB.alloc("krs", 512)
        qrbt, qrbk, _ = AB.alloc("qrb", 512)
        qrt, qrtk, _ = AF_.alloc("qrt", 512)
        for tb in range(nTBo):
            sl = 0
            P.dma("sync", lambda e, tb=tb, sl=sl: e.dma_start(
                out=cqf[sl].rearrange("p (cc t) -> p cc t", cc=nQC),
                in_=cq_a[:, tb * 512:(tb + 1) * 512].rearrange("(cc p) t -> p cc t", p=128)),
                r=[f"cq:{tb}:{fc}" for fc in range(nQC)], w=[cqfk[sl]], stream=f"qf{sl}")
            for h in range(NH):
                pk = f"pb{h % 2}"
                for cc in range(nQC):
                    P.op("tensor", lambda t, h=h, cc=cc, sl=sl: t.matmul(
                        pb[h % 2][:], wqn_s[:, cc * NH * NOPE + h * 128: cc * NH * NOPE + h * 128 + 128],
                        cqf[sl][:, cc * 512:(cc + 1) * 512],
                        start=(cc == 0), stop=(cc == nQC - 1)),
                        r=[wqn_k, cqfk[sl]], w=[pk], inc=(cc == nQC - 1))
                P.op("vector", lambda v, h=h: v.tensor_copy(qnb[:], pb[h % 2][:]),
                     r=[pk], w=[qnbk])
                P.dma("gpsimd", lambda e, h=h, tb=tb: e.dma_start(
                    out=qn_a[h * 128:(h + 1) * 128, tb * 512:(tb + 1) * 512], in_=qnb[:]),
                    r=[qnbk], w=[f"qn:{tb}:{h}"], stream="st0")
            for hc in range(NH * ROPE // 128):
                for cc in range(nQC):
                    P.op("tensor", lambda t, hc=hc, cc=cc, sl=sl: t.matmul(
                        pb[2][:], wqr_s[:, cc * NH * ROPE + hc * 128: cc * NH * ROPE + hc * 128 + 128],
                        cqf[sl][:, cc * 512:(cc + 1) * 512],
                        start=(cc == 0), stop=(cc == nQC - 1)),
                        r=[wqr_k, cqfk[sl]], w=["pb2"], inc=(cc == nQC - 1))
                P.op("vector", lambda v: v.tensor_copy(krs[:], pb[2][:]), r=["pb2"], w=[krsk])
                P.op("tensor", lambda t: t.matmul(pav[:, 0:512], rotm_s[:], krs[:],
                                                  start=True, stop=True),
                     r=[krsk, "rotm"], w=["px0"])
                P.op("vector", lambda v, tb=tb: v.tensor_tensor(
                    qrt[:], krs[:], cosq_s[:, tb * 512:(tb + 1) * 512], ALU.mult),
                    r=[krsk, "cosq"], w=[qrtk])
                P.op("vector", lambda v, tb=tb: v.tensor_tensor(
                    qrbt[:], pav[:, 0:512], sinq_s[:, tb * 512:(tb + 1) * 512], ALU.mult),
                    r=["px0", "sinq"], w=[qrbk])
                P.op("vector", lambda v: v.tensor_tensor(
                    qrbt[:], qrbt[:], qrt[:], ALU.add),
                    r=[qrbk, qrtk], w=[qrbk])
                P.dma("gpsimd", lambda e, hc=hc, tb=tb: e.dma_start(
                    out=qr_a[hc * 128:(hc + 1) * 128, tb * 512:(tb + 1) * 512], in_=qrbt[:]),
                    r=[qrbk], w=[f"qr:{tb}:{hc}"], stream="st1")
        AB.unpin(cqfk[0])

        AB.unpin("ab:cosq_s")
        AB.unpin("ab:sinq_s")
        AB.unpin("ab:rotm_s")
        AB.begin_phase()
        AF_.begin_phase()
        kro_s, _, _ = AB.alloc("kro_s", S, parts=ROPE, aliases=["kro_s"])
        AB.pin("ab:kro_s")
        P.dma("sync", lambda e: e.dma_start(out=kro_s[0:ROPE, :], in_=krope_a),
              r=[f"kro:{tb}" for tb in range(nTB)], w=["kro_s"], stream="cst")
        ksb, ksbk = [], []
        vsb, vsbk = [], []
        qnh, qnhk = [], []
        qrh, qrhk = [], []
        pbf, pbfk = [], []
        ptsb, ptsbk = [], []
        for i in range(2):
            s_, k_, d_ = AB.alloc(f"ksb{i}", S)
            ksb.append(s_), ksbk.append(k_)
            s_, k_, d_ = AB.alloc(f"vsb{i}", S)
            vsb.append(s_), vsbk.append(k_)
            s_, k_, d_ = AB.alloc(f"qnh{i}", So)
            qnh.append(s_), qnhk.append(k_)
            s_, k_, d_ = AB.alloc(f"qrh{i}", So)
            qrh.append(s_), qrhk.append(k_)
            s_, k_, d_ = AB.alloc(f"pbf{i}", S)
            pbf.append(s_), pbfk.append(k_)
            s_, k_, d_ = AB.alloc(f"ptsb{i}", 1024)
            ptsb.append(s_), ptsbk.append(k_)
        avb, avbk, _ = AB.alloc("avb", 128)
        attb, attbk, _ = AB.alloc("attb", 128)
        scs, scsk = [], []
        for i in range(2):
            s_, k_, d_ = AF_.alloc(f"scs{i}", S)
            scs.append(s_), scsk.append(k_)

        for h in range(NH):
            sl = h % 2
            P.dma("sync", lambda e, h=h, sl=sl: e.dma_start(
                out=ksb[sl][:], in_=kn_a[h * 128:(h + 1) * 128, :]),
                r=[f"kn:{tb}:{h}" for tb in range(nTB)], w=[ksbk[sl]], stream=f"k{sl}")
            P.dma("sync", lambda e, h=h, sl=sl: e.dma_start(
                out=vsb[sl].rearrange("p (kc d) -> p kc d", d=VD),
                in_=v_tm.rearrange("(kc p) d -> p kc d", p=128)[:, :, h * VD:(h + 1) * VD]),
                r=[f"v:{tb}:{t4}" for tb in range(nTB) for t4 in range(4)],
                w=[vsbk[sl]], stream=f"v{sl}")
            P.dma("sync", lambda e, h=h, sl=sl: e.dma_start(
                out=qnh[sl][:], in_=qn_a[h * 128:(h + 1) * 128, :]),
                r=[f"qn:{tb}:{h}" for tb in range(nTBo)], w=[qnhk[sl]], stream=f"qn{sl}")
            P.dma("sync", lambda e, h=h, sl=sl: e.dma_start(
                out=qrh[sl][0:ROPE, :],
                in_=qr_a.rearrange("(hc p) t -> hc p t", p=128)[h // 2, (h % 2) * 64:(h % 2) * 64 + 64, :]),
                r=[f"qr:{tb}:{h // 2}" for tb in range(nTBo)], w=[qrhk[sl]], stream=f"qr{sl}")
            for qt in range(nQT):
                qsl = qt % 2
                for kc in range(nKC):
                    pk = f"pb{kc % 4}"
                    P.op("tensor", lambda t, kc=kc, qt=qt, sl=sl: t.matmul(
                        pb[kc % 4][:], qnh[sl][:, qt * 128:(qt + 1) * 128],
                        ksb[sl][:, kc * 512:(kc + 1) * 512], start=True, stop=False),
                        r=[qnhk[sl], ksbk[sl]], w=[pk], inc=False)
                    P.op("tensor", lambda t, kc=kc, qt=qt, sl=sl: t.matmul(
                        pb[kc % 4][:], qrh[sl][0:ROPE, qt * 128:(qt + 1) * 128],
                        kro_s[:, kc * 512:(kc + 1) * 512], start=False, stop=True),
                        r=[qrhk[sl], "kro_s"], w=[pk])
                    m_prev = mrow[:, (kc - 1) % 2: (kc - 1) % 2 + 1] if kc > 0 else None
                    m_cur = mrow[:, kc % 2: kc % 2 + 1]
                    P.op("vector", lambda v, kc=kc, qsl=qsl, m_prev=m_prev, m_cur=m_cur: v.tensor_scalar(
                        scs[qsl][:, kc * 512:(kc + 1) * 512], pb[kc % 4][:],
                        SM_SCALE, m_prev, ALU.mult, ALU.max, accum_out=m_cur),
                        r=[pk] + (["mr"] if kc > 0 else []), w=[f"{scsk[qsl]}:{kc}", "mr"])
                P.op("vector", lambda v: v.tensor_scalar(
                    mrow[:, 2:3], mrow[:, (nKC - 1) % 2:(nKC - 1) % 2 + 1], -1.0, None, ALU.mult),
                    r=["mr"], w=["nmax"])
                P.op("scalar", lambda s, qsl=qsl: s.activation(
                    pbf[qsl][:], scs[qsl][:], AF.Exp, bias=mrow[:, 2:3],
                    accum_out=rsum[:, 0:1]),
                    r=[f"{scsk[qsl]}:{kc}" for kc in range(nKC)] + ["nmax"],
                    w=[pbfk[qsl], "rsum"])
                P.op("vector", lambda v: v.reciprocal(rsum[:, 1:2], rsum[:, 0:1]),
                     r=["rsum"], w=["rrec"])
                for kc in range(nKC128):
                    g = (kc // 8) % 2
                    P.op("tensor", lambda t, kc=kc, qsl=qsl, g=g: t.transpose(
                        pt[g][:, (kc % 8) * 128:(kc % 8 + 1) * 128],
                        pbf[qsl][:, kc * 128:(kc + 1) * 128], idb[:]),
                        r=[pbfk[qsl], "idb", f"ptsb_done{g}"], w=[f"pt{g}:{kc % 8}"])
                    if kc % 8 == 7:
                        P.op("vector", lambda v, g=g: v.tensor_copy(ptsb[g][:], pt[g][:]),
                             r=[f"pt{g}:{j}" for j in range(8)], w=[ptsbk[g], f"ptsb_done{g}"])
                        for j in range(8):
                            kcj = (kc // 8) * 8 + j
                            P.op("tensor", lambda t, kcj=kcj, j=j, g=g, sl=sl: t.matmul(
                                pav[:, 0:128], ptsb[g][:, j * 128:(j + 1) * 128],
                                vsb[sl][:, kcj * 128:(kcj + 1) * 128],
                                start=(kcj == 0), stop=(kcj == nKC128 - 1)),
                                r=[ptsbk[g], vsbk[sl]], w=["px0"], inc=(kcj == nKC128 - 1))
                P.op("vector", lambda v: v.tensor_scalar(
                    avb[:], pav[:, 0:128], rsum[:, 1:2], None, ALU.mult),
                    r=["px0", "rrec"], w=[avbk])
                P.op("tensor", lambda t: t.transpose(pt[0][:, 0:128], avb[:], idb[:]),
                     r=[avbk, "idb"], w=["pt0:0"])
                P.op("vector", lambda v: v.tensor_copy(attb[:], pt[0][:, 0:128]),
                     r=["pt0:0"], w=[attbk])
                P.dma("gpsimd", lambda e, h=h, qt=qt: e.dma_start(
                    out=attf[h * 128:(h + 1) * 128, qt * 128:(qt + 1) * 128], in_=attb[:]),
                    r=[attbk], w=[f"att:{h}:{qt}"], stream="st0")

        AB.unpin("ab:kro_s")
        AB.begin_phase()
        AF_.begin_phase()
        wo_s, wo_k, dp = AB.alloc("wo", nCC * HID)
        _wload(P, wo_s, wo_k, dp, woT, nCC, HID, "w0")
        atf, atfk = [], []
        for i in range(1):
            s_, k_, d_ = AB.alloc(f"atf{i}", nCC * 512)
            atf.append(s_), atfk.append(k_)
        oev, oevk = [], []
        for i in range(2):
            s_, k_, d_ = AF_.alloc(f"oev{i}", 512)
            oev.append(s_), oevk.append(k_)
        for tb in range(nTBo):
            sl = 0
            P.dma("sync", lambda e, tb=tb, sl=sl: e.dma_start(
                out=atf[sl].rearrange("p (cc t) -> p cc t", cc=nCC),
                in_=attf[:, tb * 512:(tb + 1) * 512].rearrange("(cc p) t -> p cc t", p=128)),
                r=[f"att:{h}:{qt}" for h in range(NH) for qt in range(tb * 4, tb * 4 + 4)],
                w=[atfk[sl]], stream="at0")
            for fc in range(nCC):
                pk = f"pb{fc % 4}"
                for cc in range(nCC):
                    P.op("tensor", lambda t, fc=fc, cc=cc, sl=sl: t.matmul(
                        pb[fc % 4][:], wo_s[:, cc * HID + fc * 128: cc * HID + fc * 128 + 128],
                        atf[sl][:, cc * 512:(cc + 1) * 512],
                        start=(cc == 0), stop=(cc == nCC - 1)),
                        r=[wo_k, atfk[sl]], w=[pk], inc=(cc == nCC - 1))
                osl = fc % 2
                P.op("vector", lambda v, fc=fc, osl=osl: v.tensor_copy(oev[osl][:], pb[fc % 4][:]),
                     r=[pk], w=[oevk[osl]])
                P.dma("gpsimd", lambda e, fc=fc, tb=tb, osl=osl: e.dma_start(
                    out=attn_outT[fc * 128:(fc + 1) * 128, tb * 512:(tb + 1) * 512],
                    in_=oev[osl][:]),
                    r=[oevk[osl]], stream=f"o{osl}")




        for k in ["af:qnw_c", "af:kvnw_c", "af:stat", "af:rrow", "af:rbc",
                  "af:mrow", "af:rsum", "af:nwr"]:
            AF_.unpin(k)



def build_launch_a(S=4096):
    return _standalone(emit_a, S=S)


def stage_a(inputs, S=4096):
    So = S // 2
    x = np.asarray(inputs["L0_attn_x"])
    Bb = x.shape[0]
    tr = lambda w: np.ascontiguousarray(np.asarray(w).T)
    shared = dict(
        wqdT=tr(inputs["L0_attn_w_q_down"]),
        wqnT=tr(inputs["L0_attn_w_q_nope"]),
        wqrT=tr(inputs["L0_attn_w_q_rope"]),
        wkvdT=tr(inputs["L0_attn_w_kv_down"]),
        wknT=tr(inputs["L0_attn_w_k_nope"]),
        wvuT=tr(inputs["L0_attn_w_v_up"]),
        wkrT=tr(inputs["L0_attn_w_k_rope"]),
        woT=tr(inputs["L0_attn_w_o"]),
        qnw_r=np.ascontiguousarray(
            np.asarray(inputs["L0_attn_q_norm_w"]).reshape(nQC, 128)),
        kvnw_r=np.ascontiguousarray(
            np.asarray(inputs["L0_attn_kv_norm_w"]).reshape(nKV, 128)),
        identf=np.eye(128, dtype=np.float32),
        identb=np.eye(128, dtype=BF),
        rotm=_rot_mat(),
    )
    cos, sin = _rope_tables(S)
    in_maps = []
    for c in range(8):
        b, hh = c // 2, c % 2
        xb = x[b].reshape(2, So, HID)
        x_rot = np.ascontiguousarray(
            np.concatenate([xb[hh], xb[1 - hh]], axis=0))
        pos_own = hh * So + np.arange(So)
        pos_rot = np.concatenate([pos_own, (1 - hh) * So + np.arange(So)])
        cq2 = np.ascontiguousarray(np.tile(cos[pos_own].T, (2, 1))).astype(BF)
        sq2 = np.ascontiguousarray(np.tile(sin[pos_own].T, (2, 1))).astype(BF)
        ck = np.ascontiguousarray(cos[pos_rot].T).astype(BF)
        sk = np.ascontiguousarray(sin[pos_rot].T).astype(BF)
        m = dict(shared)
        m.update(x_tm=x_rot, cosq2=cq2, sinq2=sq2, cosk=ck, sink=sk)
        in_maps.append(m)
    return in_maps


def unstage_a(results, S=4096):
    So = S // 2
    attn = np.zeros((4, S, HID), np.float32)
    xn = np.zeros((4, S, HID), BF)
    for c in range(8):
        b, hh = c // 2, c % 2
        attn[b, hh * So:(hh + 1) * So] = results[c]["attn_outT"].T
        xn[b, hh * So:(hh + 1) * So] = results[c]["xn_out"]
    return attn, xn





NEXP = 8
FME = 1408
nFE = FME // 128
FSH = 2816
nFS = FSH // 128


def emit_b(E, T=2048):
    nTT = T // 128
    nTB = T // 512
    nTH = 2
    TH = T // nTH
    nSB = TH // 512
    nc, P, AB, AF_ = E.nc, E.P, E.AB, E.AF
    AB.begin_phase()
    AF_.begin_phase()
    pb = E.pb
    pg, pu = E.pb[2], E.pb[3]
    pmisc, pmisc2 = E.px[0], E.px[1]
    idf = E.idf
    onesf_r = E.onesf_r
    xn_tm = E.inp("xn_tm", [T, HID], BF16)
    x_fm = E.inp("x_fm", [HID, T], F32)
    attn_fm = E.inp("attn_fm", [HID, T], F32)
    gwT = E.inp("gwT", [HID, NEXP], F32)
    guT = E.inp("guT", [NEXP, HID, 2 * FME], F32)
    dwT = E.inp("dwT", [NEXP, FME, HID], F32)
    sguT = E.inp("sguT", [HID, 2 * FSH], F32)
    sdT = E.inp("sdT", [FSH, HID], F32)
    x1T = E.out("x1T", [HID, T], F32)
    x1_tm = E.out("x1_tm", [T, HID], F32)
    cmb_d = E.scratch("cmb_d", [NEXP, T], F32)
    if True:
        gw_s, _, _ = AB.alloc("gw_s", nCC * NEXP, aliases=["gw_s"])
        AB.pin("ab:gw_s")
        st8, _, _ = AF_.alloc("st8", 12, aliases=["s0", "s1", "s2", "s3", "s4", "s5", "s6", "s7", "s8"])
        AF_.pin("af:st8")
        crow, _, _ = AF_.alloc("crow", TH, parts=1, aliases=["crow"])
        AF_.pin("af:crow")
        cbc, _, _ = AF_.alloc("cbc", TH, aliases=["cbc"])
        AF_.pin("af:cbc")
        _wload(P, gw_s, "gw_s", (), gwT, nCC, NEXP, "w0")
        xnf, xnfk, _ = AB.alloc("xnfb", nCC * TH)
        AB.pin(xnfk)
        xnfks = [f"{xnfk}:{cc}" for cc in range(nCC)]
        Y, Yk, _ = AF_.alloc("Y", nCC * TH)
        AF_.pin(Yk)
        act, actk, _ = AB.alloc("act", nFE * TH)
        AB.pin(actk)
        lfm, lfmk, _ = AF_.alloc("lfm", TH)
        cmbf, cmbfk, _ = AF_.alloc("cmbf", TH)
        ltm, ltmk, _ = AF_.alloc("ltm", 16)
        ptm, ptmk, _ = AF_.alloc("ptm", 16)
        sgt, sgtk, _ = AF_.alloc("sgt", 512)
        t2t, t2k, _ = AF_.alloc("t2t", 512)
        xre, xrek, _ = AF_.alloc("xre", TH)
        xtmt, _, _ = AF_.alloc("xtmt", HID, aliases=["xtmt"])
        AF_.pin("af:xtmt")
        are, arek, _ = AF_.alloc("are", TH)
        gbuf, ubuf, dwbuf = [], [], []
        for i in range(2):
            s_, k_, _ = AB.alloc(f"gb{i}", nCC * 128)
            gbuf.append((s_, k_))
            s_, k_, _ = AB.alloc(f"ub{i}", nCC * 128)
            ubuf.append((s_, k_))
            s_, k_, _ = AB.alloc(f"dwb{i}", nFS * 128)
            dwbuf.append((s_, k_))

        def gate_up_down(th, guT_e, dwT_e, nF, has_cmb, first):
            npass = nF // nFE
            for pas in range(npass):
                for fi in range(nFE):
                    fc = pas * nFE + fi
                    gs, gk = gbuf[fi % 2]
                    us, uk = ubuf[fi % 2]
                    P.dma("gpsimd", lambda e_, fc=fc, gs=gs: e_.dma_start(
                        out=gs.rearrange("p (cc f) -> p cc f", cc=nCC),
                        in_=guT_e[:, fc * 128:(fc + 1) * 128].rearrange(
                            "(cc p) f -> p cc f", p=128)),
                        w=[gk], stream=f"g{fi % 2}")
                    P.dma("gpsimd", lambda e_, fc=fc, us=us: e_.dma_start(
                        out=us.rearrange("p (cc f) -> p cc f", cc=nCC),
                        in_=guT_e[:, (nF + fc) * 128:(nF + fc + 1) * 128].rearrange(
                            "(cc p) f -> p cc f", p=128)),
                        w=[uk], stream=f"u{fi % 2}")
                    for sbk in range(nSB):
                        for cc in range(nCC):
                            P.op("tensor", lambda t, cc=cc, sbk=sbk, gs=gs: t.matmul(
                                pg[:], gs[:, cc * 128:(cc + 1) * 128],
                                xnf[:, cc * TH + sbk * 512: cc * TH + sbk * 512 + 512],
                                start=(cc == 0), stop=(cc == nCC - 1)),
                                r=[gk, xnfks[cc]], w=["pb2"], inc=(cc == nCC - 1))
                        for cc in range(nCC):
                            P.op("tensor", lambda t, cc=cc, sbk=sbk, us=us: t.matmul(
                                pu[:], us[:, cc * 128:(cc + 1) * 128],
                                xnf[:, cc * TH + sbk * 512: cc * TH + sbk * 512 + 512],
                                start=(cc == 0), stop=(cc == nCC - 1)),
                                r=[uk, xnfks[cc]], w=["pb3"], inc=(cc == nCC - 1))
                        P.op("scalar", lambda s: s.activation(sgt[:], pg[:], AF.Silu),
                             r=["pb2"], w=[sgtk])
                        ao = act[:, fi * TH + sbk * 512: fi * TH + sbk * 512 + 512]
                        if has_cmb:
                            P.op("vector", lambda v: v.tensor_tensor(t2t[:], sgt[:], pu[:], ALU.mult),
                                 r=[sgtk, "pb3"], w=[t2k])
                            P.op("vector", lambda v, ao=ao, sbk=sbk: v.tensor_tensor(
                                ao, t2t[:], cbc[:, sbk * 512:(sbk + 1) * 512], ALU.mult),
                                r=[t2k, "cbc"], w=[f"{actk}:{fi}"])
                        else:
                            P.op("vector", lambda v, ao=ao: v.tensor_tensor(ao, sgt[:], pu[:], ALU.mult),
                                 r=[sgtk, "pb3"], w=[f"{actk}:{fi}"])
                for hc in range(nCC):
                    ds, dk = dwbuf[hc % 2]
                    P.dma("gpsimd", lambda e_, hc=hc, pas=pas, ds=ds: e_.dma_start(
                        out=ds[:, 0:nFE * 128].rearrange("p (fc f) -> p fc f", fc=nFE),
                        in_=dwT_e[pas * FME:(pas + 1) * FME, hc * 128:(hc + 1) * 128].rearrange(
                            "(fc p) f -> p fc f", p=128)),
                        w=[dk], stream=f"dw{hc % 2}")
                    for sbk in range(nSB):
                        pk = f"pb{sbk % 2}"
                        for fi in range(nFE):
                            P.op("tensor", lambda t, fi=fi, sbk=sbk, ds=ds: t.matmul(
                                pb[sbk % 2][:], ds[:, fi * 128:(fi + 1) * 128],
                                act[:, fi * TH + sbk * 512: fi * TH + sbk * 512 + 512],
                                start=(fi == 0), stop=(fi == nFE - 1)),
                                r=[dk, f"{actk}:{fi}"], w=[pk], inc=(fi == nFE - 1))
                        yo = Y[:, hc * TH + sbk * 512: hc * TH + sbk * 512 + 512]
                        yk2 = f"{Yk}:{hc}:{sbk}"
                        if first and pas == 0:
                            P.op("vector", lambda v, yo=yo, sbk=sbk: v.tensor_copy(yo, pb[sbk % 2][:]),
                                 r=[pk], w=[yk2])
                        else:
                            P.op("vector", lambda v, yo=yo, sbk=sbk: v.tensor_tensor(
                                yo, yo, pb[sbk % 2][:], ALU.add),
                                r=[pk, yk2], w=[yk2])

        for th in range(nTH):
            for cc in range(nCC):
                P.dma("scalar", lambda e, cc=cc, th=th: e.dma_start(
                    out=xnf[:, cc * TH:(cc + 1) * TH],
                    in_=xn_tm[th * TH:(th + 1) * TH, cc * 128:(cc + 1) * 128],
                    transpose=True),
                    w=[xnfks[cc]], stream=f"xnf{cc % 2}")
            for tb in range(nSB):
                for cc in range(nCC):
                    P.op("tensor", lambda t, tb=tb, cc=cc: t.matmul(
                        pmisc[0:NEXP, 0:512], gw_s[:, cc * NEXP:(cc + 1) * NEXP],
                        xnf[:, cc * TH + tb * 512: cc * TH + tb * 512 + 512],
                        start=(cc == 0), stop=(cc == nCC - 1)),
                        r=["gw_s", xnfks[cc]], w=["px0"], inc=(cc == nCC - 1))
                P.op("vector", lambda v, tb=tb: v.tensor_copy(
                    lfm[0:8, tb * 512:(tb + 1) * 512], pmisc[0:NEXP, 0:512]),
                    r=["px0"], w=[f"{lfmk}:{tb}"])
            for tt in range(TH // 128):
                P.op("tensor", lambda t, tt=tt: t.transpose(
                    pmisc[:, 0:8], lfm[0:8, tt * 128:(tt + 1) * 128], idf[0:8, 0:8]),
                    r=[f"{lfmk}:{tt // 4}", "idf"], w=["px0"])
                P.op("vector", lambda v: v.tensor_copy(ltm[:, 0:8], pmisc[:, 0:8]),
                     r=["px0"], w=[ltmk])
                P.op("vector", lambda v: v.reduce_max(st8[:, 0:1], ltm[:, 0:8], axis=AX, negate=True),
                     r=[ltmk], w=["s0"])
                P.op("scalar", lambda s: s.activation(ptm[:, 0:8], ltm[:, 0:8], AF.Exp,
                                                      bias=st8[:, 0:1], accum_out=st8[:, 1:2]),
                     r=[ltmk, "s0"], w=[ptmk, "s1"])
                P.op("vector", lambda v: v.reciprocal(st8[:, 2:3], st8[:, 1:2]),
                     r=["s1"], w=["s2"])
                P.op("vector", lambda v: v.tensor_scalar(ptm[:, 0:8], ptm[:, 0:8], st8[:, 2:3], None, ALU.mult),
                     r=[ptmk, "s2"], w=[ptmk])
                P.op("vector", lambda v: v.reduce_max(st8[:, 3:4], ptm[:, 0:8], axis=AX),
                     r=[ptmk], w=["s3"])
                P.op("vector", lambda v: v.tensor_scalar(ptm[:, 8:16], ptm[:, 0:8], st8[:, 3:4], None, ALU.is_lt),
                     r=[ptmk, "s3"], w=["s4"])
                P.op("vector", lambda v: v.tensor_tensor(ptm[:, 8:16], ptm[:, 0:8], ptm[:, 8:16], ALU.mult),
                     r=[ptmk, "s4"], w=["s5"])
                P.op("vector", lambda v: v.reduce_max(st8[:, 4:5], ptm[:, 8:16], axis=AX),
                     r=["s5"], w=["s6"])
                P.op("vector", lambda v: v.tensor_scalar(ptm[:, 8:16], ptm[:, 0:8], st8[:, 4:5], None, ALU.is_ge),
                     r=[ptmk, "s6"], w=["s7"])
                P.op("vector", lambda v: v.tensor_tensor(ptm[:, 8:16], ptm[:, 0:8], ptm[:, 8:16], ALU.mult),
                     r=[ptmk, "s7"], w=["s8"])
                P.op("tensor", lambda t: t.transpose(pmisc2[0:8, 0:128], ptm[:, 8:16], idf[:]),
                     r=["s8", "idf"], w=["px1"])
                P.op("vector", lambda v, tt=tt: v.tensor_copy(
                    cmbf[0:8, tt * 128:(tt + 1) * 128], pmisc2[0:8, 0:128]),
                    r=["px1"], w=[f"{cmbfk}:{tt}"])
            P.dma("gpsimd", lambda e, th=th: e.dma_start(
                out=cmb_d[:, th * TH:(th + 1) * TH], in_=cmbf[0:8, 0:TH]),
                r=[f"{cmbfk}:{tt}" for tt in range(TH // 128)], w=[f"cmb_d:{th}"], stream="st0")

            for e in range(NEXP):
                P.dma("sync", lambda e_, e=e, th=th: e_.dma_start(
                    out=crow[0:1, :], in_=cmb_d[e:e + 1, th * TH:(th + 1) * TH]),
                    r=[f"cmb_d:{th}"], w=["crow"], stream="crow")
                for sbk in range(nSB):
                    P.op("tensor", lambda t, sbk=sbk: t.matmul(
                        pmisc[:, :], onesf_r[:], crow[0:1, sbk * 512:(sbk + 1) * 512],
                        start=True, stop=True),
                        r=["crow", "onesf_r"], w=["px0"])
                    P.op("vector", lambda v, sbk=sbk: v.tensor_copy(
                        cbc[:, sbk * 512:(sbk + 1) * 512], pmisc[:, :]),
                        r=["px0"], w=["cbc"])
                gate_up_down(th, guT[e], dwT[e], nFE, True, first=(e == 0))
            gate_up_down(th, sguT, sdT, nFS, False, first=False)
            for cc in range(nCC):
                P.dma("sync", lambda e_, cc=cc, th=th: e_.dma_start(
                    out=xre[:], in_=x_fm[cc * 128:(cc + 1) * 128, th * TH:(th + 1) * TH]),
                    w=[xrek], stream="xre")
                P.dma("sync", lambda e_, cc=cc, th=th: e_.dma_start(
                    out=are[:], in_=attn_fm[cc * 128:(cc + 1) * 128, th * TH:(th + 1) * TH]),
                    w=[arek], stream="are")
                for sbk in range(nSB):
                    yo = Y[:, cc * TH + sbk * 512: cc * TH + sbk * 512 + 512]
                    yk2 = f"{Yk}:{cc}:{sbk}"
                    P.op("vector", lambda v, yo=yo, sbk=sbk: v.tensor_tensor(
                        yo, yo, xre[:, sbk * 512:(sbk + 1) * 512], ALU.add),
                        r=[yk2, xrek], w=[yk2])
                    P.op("vector", lambda v, yo=yo, sbk=sbk: v.tensor_tensor(
                        yo, yo, are[:, sbk * 512:(sbk + 1) * 512], ALU.add),
                        r=[yk2, arek], w=[yk2])
                P.dma("gpsimd", lambda e_, cc=cc, th=th: e_.dma_start(
                    out=x1T[cc * 128:(cc + 1) * 128, th * TH:(th + 1) * TH],
                    in_=Y[:, cc * TH:(cc + 1) * TH]),
                    r=[f"{Yk}:{cc}:{sbk}" for sbk in range(nSB)], stream=f"o{cc % 2}")
            for lt in range(TH // 128):
                for cc in range(nCC):
                    P.op("tensor", lambda t, lt=lt, cc=cc: t.matmul(
                        pmisc2[:, 0:128], Y[:, cc * TH + lt * 128: cc * TH + lt * 128 + 128],
                        idf[:], start=True, stop=True),
                        r=[f"{Yk}:{cc}:{(lt * 128) // 512}", "idf"], w=["px1"])
                    P.op("vector", lambda v, cc=cc: v.tensor_copy(
                        xtmt[:, cc * 128:(cc + 1) * 128], pmisc2[:, 0:128]),
                        r=["px1"], w=[f"xtmt:{cc}"])
                P.dma("gpsimd", lambda e_, lt=lt, th=th: e_.dma_start(
                    out=x1_tm[th * TH + lt * 128: th * TH + (lt + 1) * 128, :], in_=xtmt[:]),
                    r=[f"xtmt:{cc}" for cc in range(nCC)], stream=f"o{lt % 2}")

        for k in ["ab:gw_s", "af:st8", "af:crow", "af:cbc", "af:xtmt"]:
            (AB if k.startswith("ab:") else AF_).unpin(k)
        AF_.unpin("af:Y")
        AB.unpin("ab:act")
        AB.unpin(xnfk)




def build_launch_b(T=2048):
    return _standalone(emit_b, T=T)


def stage_b(inputs, xn, attn, T=2048):
    x = np.asarray(inputs["L0_attn_x"])
    S = x.shape[1]
    So = S // 2
    tr = lambda w: np.ascontiguousarray(np.asarray(w).T)
    shared = dict(
        gwT=tr(inputs["L0_ffn_gate_weight"]),
        guT=np.ascontiguousarray(np.asarray(inputs["L0_ffn_gate_up_weights"]).transpose(0, 2, 1)),
        dwT=np.ascontiguousarray(np.asarray(inputs["L0_ffn_down_weights"]).transpose(0, 2, 1)),
        sguT=tr(inputs["L0_ffn_shared_gate_up"]),
        sdT=tr(inputs["L0_ffn_shared_down"]),
        identf=np.eye(128, dtype=np.float32),
        identb=np.eye(128, dtype=BF),
    )
    in_maps = []
    for c in range(8):
        b, hh = c // 2, c % 2
        sl = slice(hh * So, hh * So + T)
        m = dict(shared)
        m.update(
            xn_tm=np.ascontiguousarray(xn[b, sl]),
            x_fm=np.ascontiguousarray(x[b, sl].T),
            attn_fm=np.ascontiguousarray(attn[b, sl].T),
        )
        in_maps.append(m)
    return in_maps


DIN = 8512
ZDIM = 4096
CONVD = 4352
nZC = ZDIM // 128
nXBC = CONVD // 128


def emit_c1(E, T=2048):
    nTT = T // 128
    nTB = T // 512
    nc, P, AB, AF_ = E.nc, E.P, E.AB, E.AF
    AB.begin_phase()
    AF_.begin_phase()
    pb = E.pb
    x1_tm = E.inp("x1_tm", [T, HID], F32)
    winT = E.inp("winT", [HID, DIN], F32)
    xn1_out = E.out("xn1_out", [T, HID], BF16)
    xbc_raw = E.out("xbc_raw", [CONVD, T], F32)
    z_tm = E.out("z_tm", [T, ZDIM], F32)
    dt_raw = E.out("dt_raw", [T, 64], F32)
    if True:
        stat, _, _ = AF_.alloc("statc", 16, aliases=["st0", "st1", "st2", "st4", "st5", "st6", "st7", "st8"])
        AF_.pin("af:statc")
        xa, xak = [], []
        for i in range(2):
            s_, k_, _ = AF_.alloc(f"xa{i}", HID)
            xa.append(s_), xak.append(k_)
        dumpt, dumpk, _ = AF_.alloc("dump", HID)
        xnt, xntk = [], []
        for i in range(2):
            s_, k_, _ = AB.alloc(f"xnt{i}", HID)
            xnt.append(s_), xntk.append(k_)
        for tt in range(nTT):
            bi = tt % 2
            P.dma("sync", lambda e, tt=tt, bi=bi: e.dma_start(
                out=xa[bi][:], in_=x1_tm[tt * 128:(tt + 1) * 128, :]),
                w=[xak[bi]], stream=f"x{bi}")
            P.op("scalar", lambda s, bi=bi: s.activation(
                dumpt[:], xa[bi][:], AF.Square, accum_out=stat[:, 0:1]),
                r=[xak[bi]], w=[dumpk, "st0"])
            P.op("vector", lambda v, bi=bi: v.reduce_sum(stat[:, 1:2], xa[bi][:], axis=AX),
                 r=[xak[bi]], w=["st1"])
            P.op("vector", lambda v: v.tensor_scalar(stat[:, 2:3], stat[:, 1:2], 1.0 / HID, None, ALU.mult),
                 r=["st1"], w=["st2"])
            P.op("vector", lambda v: v.tensor_tensor(stat[:, 4:5], stat[:, 2:3], stat[:, 2:3], ALU.mult),
                 r=["st2"], w=["st4"])
            P.op("vector", lambda v: v.scalar_tensor_tensor(
                stat[:, 5:6], stat[:, 0:1], 1.0 / HID, stat[:, 4:5], ALU.mult, ALU.subtract),
                r=["st0", "st4"], w=["st5"])
            P.op("vector", lambda v: v.tensor_scalar(stat[:, 8:9], stat[:, 5:6], LN_EPS, None, ALU.add),
                 r=["st5"], w=["st8"])
            P.op("scalar", lambda s: s.activation(stat[:, 6:7], stat[:, 8:9], AF.Sqrt),
                 r=["st8"], w=["st6"])
            P.op("vector", lambda v: v.reciprocal(stat[:, 7:8], stat[:, 6:7]),
                 r=["st6"], w=["st7"])
            P.op("vector", lambda v, bi=bi: v.tensor_scalar(
                xnt[bi][:], xa[bi][:], stat[:, 2:3], stat[:, 7:8], ALU.subtract, ALU.mult),
                r=[xak[bi], "st2", "st7"], w=[xntk[bi]])
            P.dma("gpsimd", lambda e, tt=tt, bi=bi: e.dma_start(
                out=xn1_out[tt * 128:(tt + 1) * 128, :], in_=xnt[bi][:]),
                r=[xntk[bi]], w=[f"xn1:{tt}"], stream=f"st{bi}")

        xnf, xnfk, _ = AB.alloc("xnfc", nCC * T)
        AB.pin(xnfk)
        xnfks = [f"{xnfk}:{cc}" for cc in range(nCC)]
        for cc in range(nCC):
            P.dma("scalar", lambda e, cc=cc: e.dma_start(
                out=xnf[:, cc * T:(cc + 1) * T],
                in_=xn1_out[:, cc * 128:(cc + 1) * 128], transpose=True),
                r=[f"xn1:{tt}" for tt in range(nTT)],
                w=[xnfks[cc]], stream=f"xnf{cc % 2}")

        wsl = []
        for i in range(1):
            s_, k_, _ = AB.alloc(f"w{i}", nCC * 512)
            wsl.append((s_, k_))
        ev, evk = [], []
        for i in range(2):
            s_, k_, _ = AF_.alloc(f"ev{i}", 512)
            ev.append(s_), evk.append(k_)

        nFG = (nXBC + 3) // 4
        for fg in range(nFG):
            f0 = fg * 4
            nf = min(4, nXBC - f0)
            ws, wk = wsl[0]
            P.dma("gpsimd", lambda e, f0=f0, nf=nf, ws=ws: e.dma_start(
                out=ws[:, 0:nCC * nf * 128].rearrange("p (cc f) -> p cc f", cc=nCC),
                in_=winT[:, ZDIM + f0 * 128: ZDIM + (f0 + nf) * 128].rearrange(
                    "(cc p) f -> p cc f", p=128)),
                w=[wk], stream="wl0")
            for tb in range(nTB):
                for fi in range(nf):
                    pk = f"pb{fi}"
                    for cc in range(nCC):
                        P.op("tensor", lambda t, fi=fi, cc=cc, tb=tb, ws=ws, nf=nf: t.matmul(
                            pb[fi][:], ws[:, cc * nf * 128 + fi * 128: cc * nf * 128 + fi * 128 + 128],
                            xnf[:, cc * T + tb * 512: cc * T + tb * 512 + 512],
                            start=(cc == 0), stop=(cc == nCC - 1)),
                            r=[wk, xnfks[cc]], w=[pk], inc=(cc == nCC - 1))
                    P.op("vector", lambda v, fi=fi: v.tensor_copy(ev[fi % 2][:], pb[fi][:]),
                         r=[pk], w=[evk[fi % 2]])
                    P.dma("gpsimd", lambda e, f0=f0, fi=fi, tb=tb: e.dma_start(
                        out=xbc_raw[(f0 + fi) * 128:(f0 + fi + 1) * 128, tb * 512:(tb + 1) * 512],
                        in_=ev[fi % 2][:]),
                        r=[evk[fi % 2]], w=[f"xbcs:{f0 + fi}:{tb}"], stream=f"so{fi % 2}")

        for zb in range(ZDIM // 512):
            ws, wk = wsl[0]
            P.dma("gpsimd", lambda e, zb=zb, ws=ws: e.dma_start(
                out=ws.rearrange("p (cc f) -> p cc f", cc=nCC),
                in_=winT[:, zb * 512:(zb + 1) * 512].rearrange("(cc p) f -> p cc f", p=128)),
                w=[wk], stream="wl0")
            for tt in range(nTT):
                pk = f"pb{tt % 4}"
                for cc in range(nCC):
                    P.op("tensor", lambda t, tt=tt, cc=cc, ws=ws: t.matmul(
                        pb[tt % 4][:], xnf[:, cc * T + tt * 128: cc * T + tt * 128 + 128],
                        ws[:, cc * 512:(cc + 1) * 512],
                        start=(cc == 0), stop=(cc == nCC - 1)),
                        r=[wk, xnfks[cc]], w=[pk], inc=(cc == nCC - 1))
                P.op("vector", lambda v, tt=tt: v.tensor_copy(ev[tt % 2][:], pb[tt % 4][:]),
                     r=[pk], w=[evk[tt % 2]])
                P.dma("gpsimd", lambda e, zb=zb, tt=tt: e.dma_start(
                    out=z_tm[tt * 128:(tt + 1) * 128, zb * 512:(zb + 1) * 512],
                    in_=ev[tt % 2][:]),
                    r=[evk[tt % 2]], stream=f"so{tt % 2}")

        wdt, wdtk, _ = AB.alloc("wdt", nCC * 64)
        P.dma("gpsimd", lambda e: e.dma_start(
            out=wdt.rearrange("p (cc f) -> p cc f", cc=nCC),
            in_=winT[:, DIN - 64:DIN].rearrange("(cc p) f -> p cc f", p=128)),
            w=[wdtk], stream="wl0")
        for tt in range(nTT):
            pk = f"pb{tt % 4}"
            for cc in range(nCC):
                P.op("tensor", lambda t, tt=tt, cc=cc: t.matmul(
                    pb[tt % 4][:, 0:64], xnf[:, cc * T + tt * 128: cc * T + tt * 128 + 128],
                    wdt[:, cc * 64:(cc + 1) * 64],
                    start=(cc == 0), stop=(cc == nCC - 1)),
                    r=[wdtk, xnfks[cc]], w=[pk], inc=(cc == nCC - 1))
            P.op("vector", lambda v, tt=tt: v.tensor_copy(ev[tt % 2][:, 0:64], pb[tt % 4][:, 0:64]),
                 r=[pk], w=[evk[tt % 2]])
            P.dma("gpsimd", lambda e, tt=tt: e.dma_start(
                out=dt_raw[tt * 128:(tt + 1) * 128, :], in_=ev[tt % 2][:, 0:64]),
                r=[evk[tt % 2]], stream=f"so{tt % 2}")

        AF_.unpin("af:statc")
        AB.unpin(xnfk)




def build_launch_c1(T=2048):
    return _standalone(emit_c1, T=T)


def stage_c1(inputs, x1, T=2048):
    S = x1.shape[1]
    So = S // 2
    tr = lambda w: np.ascontiguousarray(np.asarray(w).T)
    shared = dict(winT=tr(inputs["L1_attn_W_in"]),
                  identf=np.eye(128, dtype=np.float32),
                  identb=np.eye(128, dtype=BF))
    in_maps = []
    for c in range(8):
        b, hh = c // 2, c % 2
        m = dict(shared)
        m.update(x1_tm=np.ascontiguousarray(x1[b, hh * So: hh * So + T]))
        in_maps.append(m)
    return in_maps


MH = 64
MP = 64
MN = 128


def emit_c2a(E, T=2048, fused_halo=False):
    nCH = T // 128
    nc, P, AB, AF_ = E.nc, E.P, E.AB, E.AF
    AB.begin_phase()
    AF_.begin_phase()
    pb, pt = E.pb, E.pt
    pcb, pyd, pmisc = E.pb[2], E.pb[3], E.px[0]
    ptq0, ptq1 = E.pt[0], E.pt[1]
    idf, idb = E.idf, E.idb
    onesf_r, ones_c = E.onesf_r, E.ones_c
    if fused_halo:
        xbc_m = E.inp("xbc_raw", [CONVD, T], F32)
        halo_ap = E.wiring["cc1_halo"]
        m0col = E.inp("m0col", [128, 1], F32)
    else:
        xbc_h = E.inp("xbc_h", [CONVD, T + 3], F32)
    dtr_d = E.inp("dt_raw", [T, MH], F32)
    cw_d = E.inp("cw_d", [128, nXBC * 4], F32)
    dtb_d = E.inp("dtb_d", [1, MH], F32)
    alog_d = E.inp("alog_d", [1, MH], F32)
    dd_d = E.inp("dd_d", [1, MH], F32)
    tri_d = E.inp("tri_d", [128, 128], F32)
    mnegw_d = E.inp("mnegw_d", [128, MH * 128], BF16)
    y_part = E.out("y_part", [T, ZDIM], F32)
    states_a = E.out("states_a", [nCH, MP, MH * MN], BF16)
    decays_d = E.out("decays_d", [1, nCH * MH], F32)
    sloc = E.out("sloc", [MP, MH * MN], F32)
    c_fm = E.out("c_fm", [MN, T], BF16)
    cum_out = E.out("cum_out", [T, MH], F32)
    xs_a = E.scratch("xs_a", [ZDIM, T], BF16)
    cumt_d = E.scratch("cumt_d", [MH, 128], F32)
    if True:
        tri_s, _, _ = AF_.alloc("tri_s", 128, aliases=["tri"])
        AF_.pin("af:tri_s")
        mnegw, _, _ = AB.alloc("mnegw", MH * 128, aliases=["mnegw"])
        AB.pin("ab:mnegw")
        cwt, _, _ = AF_.alloc("cwt", nXBC * 4, aliases=["cwt"])
        AF_.pin("af:cwt")
        rows, _, _ = AF_.alloc("rows", 4 * MH, parts=1, aliases=["r_dtb", "r_al", "r_dd"])
        AF_.pin("af:rows")
        S_t, _, _ = AF_.alloc("S_t", MH * MN, parts=MP, aliases=["S"])
        AF_.pin("af:S_t")
        Bfm, _, _ = AB.alloc("Bfm", T, aliases=["Bfm"])
        AB.pin("ab:Bfm")
        Cfm, _, _ = AB.alloc("Cfm", T, aliases=["Cfm"])
        AB.pin("ab:Cfm")
        small, _, _ = AF_.alloc("small", 1024, aliases=[
            "dtbbc", "expab", "DBt", "dtc", "lndt", "apos", "cum_sb", "colv",
            "wdecc", "tmp1", "tmp2"])
        AF_.pin("af:small")
        cl_r, _, _ = AF_.alloc("cl_r", MH, parts=1, aliases=["cl_r"])
        AF_.pin("af:cl_r")
        dec_r, _, _ = AF_.alloc("dec_r", MH, parts=1, aliases=["dec_r"])
        AF_.pin("af:dec_r")
        decB, _, _ = AF_.alloc("decBa", MH, parts=MP, aliases=["decB"])
        AF_.pin("af:decBa")
        cumfq, _, _ = AF_.alloc("cumfq", 2048, parts=1, aliases=["cumfq"])
        AF_.pin("af:cumfq")
        cumT, _, _ = AF_.alloc("cumT", 128, parts=MH, aliases=["cumT"])
        AF_.pin("af:cumT")
        dtr_s, _, _ = AF_.alloc("dtr_s", nCH * MH, aliases=["dtr"])
        AF_.pin("af:dtr_s")
        P.op("gpsimd", lambda e: e.memset(S_t[:, :], 0.0), w=["S"])
        for nm, dst, k in [("tri_d", tri_s, "tri"), ("mnegw_d", mnegw, "mnegw"),
                           ("cw_d", cwt, "cwt")]:
            src = {"tri_d": tri_d, "mnegw_d": mnegw_d, "cw_d": cw_d}[nm]
            P.dma("sync", lambda e, dst=dst, src=src: e.dma_start(out=dst[:, :], in_=src),
                  w=[k], stream="cst")
        P.dma("sync", lambda e: e.dma_start(out=rows[0:1, 0:MH], in_=dtb_d), w=["r_dtb"], stream="cst")
        P.dma("sync", lambda e: e.dma_start(out=rows[0:1, MH:2 * MH], in_=alog_d), w=["r_al"], stream="cst")
        P.dma("sync", lambda e: e.dma_start(out=rows[0:1, 2 * MH:3 * MH], in_=dd_d), w=["r_dd"], stream="cst")
        dtbbc = small[:, 0:64]
        expab = small[:, 64:128]
        DBt = small[:, 128:192]
        P.op("tensor", lambda t: t.matmul(pmisc[:, 0:MH], onesf_r[:], rows[0:1, 0:MH],
                                          start=True, stop=True),
             r=["r_dtb", "onesf_r"], w=["px0"])
        P.op("vector", lambda v: v.tensor_copy(dtbbc, pmisc[:, 0:MH]), r=["px0"], w=["dtbbc"])
        P.op("tensor", lambda t: t.matmul(pmisc[:, 0:MH], onesf_r[:], rows[0:1, MH:2 * MH],
                                          start=True, stop=True),
             r=["r_al", "onesf_r", "dtbbc"], w=["px0"])
        P.op("scalar", lambda s: s.activation(expab, pmisc[:, 0:MH], AF.Exp),
             r=["px0"], w=["expab"])
        P.op("tensor", lambda t: t.matmul(pmisc[:, 0:MH], onesf_r[:], rows[0:1, 2 * MH:3 * MH],
                                          start=True, stop=True),
             r=["r_dd", "onesf_r", "expab"], w=["px0"])
        P.op("vector", lambda v: v.tensor_copy(DBt, pmisc[:, 0:MH]), r=["px0"], w=["DBt"])

        xbr, xbrk = [], []
        for i in range(2):
            s_, k_, _ = AF_.alloc(f"xbr{i}", T + 3)
            xbr.append(s_), xbrk.append(k_)
        if fused_halo:
            m0s, _, _ = AF_.alloc("m0s", 1, aliases=["m0s"])
            AF_.pin("af:m0s")
            P.dma("sync", lambda e: e.dma_start(out=m0s[:, 0:1], in_=m0col),
                  w=["m0s"], stream="cst")
        cacc, cacck, _ = AF_.alloc("cacc", T)
        cout, coutk, _ = AB.alloc("cout", T)
        for fc in range(nXBC):
            bi = fc % 2
            if fused_halo:
                P.dma("sync", lambda e, fc=fc, bi=bi: e.dma_start(
                    out=xbr[bi][:, 0:3], in_=halo_ap[fc * 128:(fc + 1) * 128, :]),
                    w=[xbrk[bi]], stream=f"xb{bi}", deps=E.wiring.get("cc1_tok"))
                P.dma("sync", lambda e, fc=fc, bi=bi: e.dma_start(
                    out=xbr[bi][:, 3:T + 3], in_=xbc_m[fc * 128:(fc + 1) * 128, :]),
                    r=[f"xbcs:{fc}"], w=[xbrk[bi]], stream=f"xb{bi}")
                P.op("vector", lambda v, bi=bi: v.tensor_scalar(
                    xbr[bi][:, 0:3], xbr[bi][:, 0:3], m0s[:, 0:1], None, ALU.mult),
                    r=[xbrk[bi], "m0s"], w=[xbrk[bi]])
            else:
                P.dma("sync", lambda e, fc=fc, bi=bi: e.dma_start(
                    out=xbr[bi][:], in_=xbc_h[fc * 128:(fc + 1) * 128, :]),
                    w=[xbrk[bi]], stream=f"xb{bi}")
            P.op("vector", lambda v, fc=fc, bi=bi: v.tensor_scalar(
                cacc[:], xbr[bi][:, 0:T], cwt[:, fc * 4:fc * 4 + 1], None, ALU.mult),
                r=[xbrk[bi], "cwt"], w=[cacck])
            for k in range(1, 4):
                P.op("vector", lambda v, fc=fc, bi=bi, k=k: v.scalar_tensor_tensor(
                    cacc[:], xbr[bi][:, k:k + T], cwt[:, fc * 4 + k:fc * 4 + k + 1],
                    cacc[:], ALU.mult, ALU.add),
                    r=[xbrk[bi], "cwt", cacck], w=[cacck])
            P.op("scalar", lambda s: s.activation(cout[:], cacc[:], AF.Silu),
                 r=[cacck], w=[coutk])
            if fc < 32:
                P.dma("gpsimd", lambda e, fc=fc: e.dma_start(
                    out=xs_a[fc * 128:(fc + 1) * 128, :], in_=cout[:]),
                    r=[coutk], w=[f"xs:{fc}"], stream=f"so{fc % 2}")
            elif fc == 32:
                P.op("vector", lambda v: v.tensor_copy(Bfm[:, :], cout[:]),
                     r=[coutk], w=["Bfm"])
            else:
                P.op("vector", lambda v: v.tensor_copy(Cfm[:, :], cout[:]),
                     r=[coutk], w=["Cfm"])
                P.dma("gpsimd", lambda e: e.dma_start(out=c_fm[:, :], in_=Cfm[:, :]),
                      r=["Cfm"], stream="so0")

        P.dma("sync", lambda e: e.dma_start(
            out=dtr_s.rearrange("p (c h) -> p c h", c=nCH),
            in_=dtr_d.rearrange("(c p) h -> p c h", p=128)),
            w=["dtr"], stream="cst")

        dtc = small[:, 192:256]
        lndt = small[:, 256:320]
        apos = small[:, 320:384]
        cum_sb = small[:, 384:448]
        colv = small[:, 448:512]
        wdecc = small[:, 512:576]
        tmp1 = small[:, 576:640]
        tmp2 = small[:, 640:704]
        xsT_sb, xsTk, _ = AB.alloc("xsT", 32 * 128)
        xsc, xsck = [], []
        for i in range(2):
            s_, k_, _ = AB.alloc(f"xsc{i}", 32 * 128)
            xsc.append(s_), xsck.append(k_)
        btq, btqk, _ = AB.alloc("btq", 128)
        dti, dtik, _ = AF_.alloc("dti", 128)
        edt, edtk, _ = AB.alloc("edt", 128)
        mts, mtsk, _ = AB.alloc("mts", 128)
        wx, wxk, _ = AB.alloc("wx", 128)
        stb, stbk, _ = AB.alloc("stb", 256)
        sts, stsk, _ = AF_.alloc("sts", 128)
        yc, yck, _ = AF_.alloc("yc", ZDIM)

        for c in range(nCH):
            ci = c % 2
            P.dma("sync", lambda e, c=c, ci=ci: e.dma_start(
                out=xsc[ci].rearrange("p (fc t) -> p fc t", fc=32),
                in_=xs_a.rearrange("(fc p) t -> p fc t", p=128)[:, :, c * 128:(c + 1) * 128]),
                r=[f"xs:{fc}" for fc in range(32)], w=[xsck[ci]], stream=f"xs{ci}")
            dk = f"dt:{c}"
            P.op("vector", lambda v, c=c: v.tensor_tensor(
                tmp1[:, 0:64], dtr_s[:, c * MH:(c + 1) * MH], dtbbc, ALU.add),
                r=["dtr", "dtbbc"], w=["tmp1"])
            P.op("scalar", lambda s: s.activation(tmp2[:, 0:64], tmp1[:, 0:64], AF.Abs),
                 r=["tmp1"], w=["tmp2"])
            P.op("scalar", lambda s: s.activation(tmp2[:, 0:64], tmp2[:, 0:64], AF.Exp, scale=-1.0),
                 r=["tmp2"], w=["tmp2"])
            P.op("vector", lambda v: v.tensor_scalar(tmp2[:, 0:64], tmp2[:, 0:64], 1.0, None, ALU.add),
                 r=["tmp2"], w=["tmp2"])
            P.op("scalar", lambda s: s.activation(tmp2[:, 0:64], tmp2[:, 0:64], AF.Ln),
                 r=["tmp2"], w=["tmp2"])
            P.op("vector", lambda v: v.tensor_scalar(tmp1[:, 0:64], tmp1[:, 0:64], 0.0, None, ALU.max),
                 r=["tmp1"], w=["tmp1"])
            P.op("vector", lambda v: v.tensor_tensor(dtc, tmp1[:, 0:64], tmp2[:, 0:64], ALU.add),
                 r=["tmp1", "tmp2"], w=["dtc"])
            P.op("scalar", lambda s: s.activation(lndt, dtc, AF.Ln), r=["dtc"], w=["lndt"])
            P.op("vector", lambda v: v.tensor_tensor(apos, dtc, expab, ALU.mult),
                 r=["dtc", "expab"], w=["apos"])
            P.op("tensor", lambda t: t.matmul(pmisc[:, 0:64], tri_s[:], apos,
                                              start=True, stop=True),
                 r=["tri", "apos"], w=["px0"])
            P.op("vector", lambda v: v.tensor_scalar(cum_sb, pmisc[:, 0:64], -1.0, None, ALU.mult),
                 r=["px0"], w=["cum_sb"])
            P.dma("gpsimd", lambda e, c=c: e.dma_start(
                out=cum_out[c * 128:(c + 1) * 128, :], in_=cum_sb),
                r=["cum_sb"], stream="so1")
            P.op("tensor", lambda t: t.matmul(pmisc[0:1, 64:128], ones_c[:], apos,
                                              start=True, stop=True),
                 r=["ones_c", "apos", "cum_sb"], w=["px0"])
            P.op("vector", lambda v: v.tensor_scalar(cl_r[0:1, :], pmisc[0:1, 64:128], -1.0, None, ALU.mult),
                 r=["px0"], w=["cl_r"])
            P.op("scalar", lambda s: s.activation(dec_r[0:1, :], cl_r[0:1, :], AF.Exp),
                 r=["cl_r"], w=["dec_r"])
            P.dma("gpsimd", lambda e, c=c: e.dma_start(
                out=decays_d[:, c * MH:(c + 1) * MH], in_=dec_r[0:1, :]),
                r=["dec_r"], stream="so1")
            P.op("tensor", lambda t: t.matmul(pmisc[0:MP, 128:192], onesf_r[:, 0:MH].rearrange("o h -> o h"), dec_r[0:1, :],
                                              start=True, stop=True),
                 r=["dec_r", "onesf_r", "cl_r"], w=["px0"])
            P.op("vector", lambda v: v.tensor_copy(decB[0:MP, :], pmisc[0:MP, 128:192]),
                 r=["px0"], w=["decB"])
            P.op("vector", lambda v: v.tensor_tensor(colv, lndt, cum_sb, ALU.subtract),
                 r=["lndt", "cum_sb"], w=["colv"])
            P.op("tensor", lambda t: t.matmul(pmisc[:, 192:256], onesf_r[:], cl_r[0:1, :],
                                              start=True, stop=True),
                 r=["cl_r", "onesf_r", "decB"], w=["px0"])
            P.op("vector", lambda v: v.scalar_tensor_tensor(
                tmp1[:, 0:64], cum_sb, -1.0, pmisc[:, 192:256], ALU.mult, ALU.add),
                r=["cum_sb", "px0"], w=["tmp1"])
            P.op("vector", lambda v: v.tensor_tensor(tmp1[:, 0:64], tmp1[:, 0:64], lndt, ALU.add),
                 r=["tmp1", "lndt"], w=["tmp1"])
            P.op("scalar", lambda s: s.activation(wdecc, tmp1[:, 0:64], AF.Exp),
                 r=["tmp1"], w=["wdecc"])
            P.op("tensor", lambda t: t.transpose(pmisc[0:MH, 256:384], cum_sb, idf[:]),
                 r=["cum_sb", "idf", "tmp1"], w=["px0"])
            P.op("vector", lambda v: v.tensor_copy(cumT[0:MH, :], pmisc[0:MH, 256:384]),
                 r=["px0"], w=["cumT"])
            P.dma("gpsimd", lambda e: e.dma_start(out=cumt_d[:, :], in_=cumT[0:MH, :]),
                  r=["cumT"], w=["cumt_d"], stream="so1")
            P.op("tensor", lambda t, c=c: t.transpose(
                ptq0[:, 0:128], Bfm[:, c * 128:(c + 1) * 128], idb[:]),
                r=["Bfm", "idb"], w=["pt0"])
            P.op("vector", lambda v: v.tensor_copy(btq[:], ptq0[:, 0:128]),
                 r=["pt0"], w=[btqk])
            P.op("tensor", lambda t, c=c: t.matmul(
                pcb[:, 0:128], Bfm[:, c * 128:(c + 1) * 128], Cfm[:, c * 128:(c + 1) * 128],
                start=True, stop=True),
                r=["Bfm", "Cfm"], w=["pb2"])
            for fc in range(32):
                g = fc % 2
                pt_ = ptq0 if g == 0 else ptq1
                ptk = f"pt{g}"
                P.op("tensor", lambda t, fc=fc, ci=ci, pt_=pt_: t.transpose(
                    pt_[:, 0:128],
                    xsc[ci][:, fc * 128:(fc + 1) * 128], idb[:]),
                    r=[xsck[ci], "idb"], w=[ptk])
                P.op("vector", lambda v, fc=fc, pt_=pt_: v.tensor_copy(
                    xsT_sb[:, fc * 128:(fc + 1) * 128], pt_[:, 0:128]),
                    r=[ptk], w=[f"{xsTk}:{fc}"])
            for q in range(16):
                pw = pb[q % 2]
                pwk = f"pb{q % 2}"
                P.dma("sync", lambda e, q=q: e.dma_start(
                    out=cumfq[0:1, 0:512], in_=cumt_d.rearrange("h t -> (h t)")[q * 512:(q + 1) * 512][None, :]),
                    r=["cumt_d", pwk], w=["cumfq"], stream="cfq")
                P.op("tensor", lambda t, pw=pw: t.matmul(
                    pw[:, 0:512], onesf_r[:], cumfq[0:1, 0:512],
                    start=True, stop=False),
                    r=["cumfq", "onesf_r"], w=[pwk], inc=False)
                P.op("tensor", lambda t, q=q, pw=pw: t.matmul(
                    pw[:, 0:512], idb[:], mnegw[:, q * 512:(q + 1) * 512],
                    start=False, stop=True),
                    r=["idb", "mnegw"], w=[pwk])
                for hh in range(4):
                    h = q * 4 + hh
                    P.op("vector", lambda v, hh=hh, h=h, pw=pw: v.tensor_scalar(
                        dti[:], pw[:, hh * 128:(hh + 1) * 128], colv[:, h:h + 1], None, ALU.add),
                        r=[pwk, "colv"], w=[dtik])
                    P.op("scalar", lambda s: s.activation(edt[:], dti[:], AF.Exp),
                         r=[dtik], w=[edtk])
                    P.op("vector", lambda v: v.tensor_tensor(mts[:], pcb[:, 0:128], edt[:], ALU.mult),
                         r=["pb2", edtk], w=[mtsk])
                    P.op("tensor", lambda t, h=h: t.matmul(
                        pyd[:, 0:64], mts[:], xsT_sb[:, (h // 2) * 128 + (h % 2) * 64: (h // 2) * 128 + (h % 2) * 64 + 64],
                        start=True, stop=True),
                        r=[mtsk, f"{xsTk}:{h // 2}"], w=["pb3"])
                    P.op("vector", lambda v, h=h: v.scalar_tensor_tensor(
                        yc[:, h * 64:(h + 1) * 64],
                        xsT_sb[:, (h // 2) * 128 + (h % 2) * 64: (h // 2) * 128 + (h % 2) * 64 + 64],
                        DBt[:, h:h + 1], pyd[:, 0:64], ALU.mult, ALU.add),
                        r=[f"{xsTk}:{h // 2}", "DBt", "pb3"], w=[f"{yck}:{h}"])
            P.dma("gpsimd", lambda e, c=c: e.dma_start(
                out=y_part[c * 128:(c + 1) * 128, :], in_=yc[:, :]),
                r=[f"{yck}:{h}" for h in range(MH)], stream="so0")
            for pr in range(32):
                P.op("vector", lambda v, pr=pr: v.tensor_scalar(
                    wx[:, 0:64], xsT_sb[:, pr * 128:pr * 128 + 64], wdecc[:, 2 * pr:2 * pr + 1], None, ALU.mult),
                    r=[f"{xsTk}:{pr}", "wdecc"], w=[wxk + ":a"])
                P.op("vector", lambda v, pr=pr: v.tensor_scalar(
                    wx[:, 64:128], xsT_sb[:, pr * 128 + 64:pr * 128 + 128], wdecc[:, 2 * pr + 1:2 * pr + 2], None, ALU.mult),
                    r=[f"{xsTk}:{pr}", "wdecc"], w=[wxk + ":b"])
                for g in range(2):
                    h = 2 * pr + g
                    P.op("tensor", lambda t, g=g: t.matmul(
                        pyd[0:64, 128 + g * 128:256 + g * 128], wx[:, g * 64:(g + 1) * 64],
                        btq[:], start=True, stop=True),
                        r=[wxk + ":a", wxk + ":b", btqk], w=["pb3"])
                    P.op("vector", lambda v, g=g: v.tensor_copy(
                        stb[0:64, g * 128:(g + 1) * 128], pyd[0:64, 128 + g * 128:256 + g * 128]),
                        r=["pb3"], w=[stbk])
                    P.dma("gpsimd", lambda e, c=c, h=h, g=g: e.dma_start(
                        out=states_a[c, :, h * 128:(h + 1) * 128],
                        in_=stb[0:64, g * 128:(g + 1) * 128]),
                        r=[stbk], stream=f"so{pr % 2}")
                    P.op("vector", lambda v, h=h, g=g: v.scalar_tensor_tensor(
                        S_t[:, h * MN:(h + 1) * MN], S_t[:, h * MN:(h + 1) * MN],
                        decB[0:MP, h:h + 1], stb[0:64, g * 128:(g + 1) * 128], ALU.mult, ALU.add),
                        r=["decB", stbk, "S"], w=["S"])
        P.dma("gpsimd", lambda e: e.dma_start(out=sloc[:, :], in_=S_t[:, :]),
              r=["S"], w=["sloc_s"], stream="so0")

        if fused_halo:
            AF_.unpin("af:m0s")
        for k in ["af:tri_s", "af:cwt", "af:rows", "af:S_t", "af:small",
                  "af:cl_r", "af:dec_r", "af:decBa", "af:cumfq", "af:cumT", "af:dtr_s"]:
            AF_.unpin(k)
        AB.unpin("ab:mnegw")
        AB.unpin("ab:Bfm")
        AB.unpin("ab:Cfm")




def build_launch_c2a(T=2048):
    return _standalone(emit_c2a, T=T)


def _tri_mask():
    j = np.arange(128)[:, None]
    i = np.arange(128)[None, :]
    tri = (j <= i).astype(np.float32)
    mneg = np.where(i >= j, 0.0, -1e30).astype(np.float32)
    return tri, np.ascontiguousarray(np.tile(mneg, (1, MH)).astype(BF))


def stage_c2a(inputs, xbc_all, dtr_all, T=2048):
    tri, mnegw = _tri_mask()
    cw = np.asarray(inputs["L1_attn_conv_w"]).reshape(nXBC, 128, 4)
    cw = np.ascontiguousarray(cw.transpose(1, 0, 2).reshape(128, nXBC * 4))
    shared = dict(
        cw_d=cw,
        dtb_d=np.asarray(inputs["L1_attn_dt_bias"]).reshape(1, MH),
        alog_d=np.asarray(inputs["L1_attn_A_log"]).reshape(1, MH),
        dd_d=np.asarray(inputs["L1_attn_D"]).reshape(1, MH),
        tri_d=tri, mnegw_d=mnegw,
        identf=np.eye(128, dtype=np.float32),
        identb=np.eye(128, dtype=BF),
    )
    in_maps = []
    for c in range(8):
        hh = c % 2
        if hh == 0:
            halo = np.zeros((CONVD, 3), np.float32)
        else:
            halo = xbc_all[c - 1][:, T - 3:T]
        m = dict(shared)
        m.update(
            xbc_h=np.ascontiguousarray(np.concatenate([halo, xbc_all[c]], axis=1)),
            dt_raw=np.ascontiguousarray(dtr_all[c]),
        )
        in_maps.append(m)
    return in_maps


def emit_c2b(E, T=2048, fused_sinit=False):
    nCH = T // 128
    nTB = T // 512
    nc, P, AB, AF_ = E.nc, E.P, E.AB, E.AF
    AB.begin_phase()
    AF_.begin_phase()
    pb = E.pb
    pyo, pmisc = E.pb[2], E.px[0]
    ptq0, ptq1 = E.pt[0], E.pt[1]
    idf, idb = E.idf, E.idb
    onesf_r = E.onesf_r
    y_part = E.inp("y_part", [T, ZDIM], F32)
    states_a = E.inp("states_a", [nCH, MP, MH * MN], BF16)
    decays_d = E.inp("decays_d", [1, nCH * MH], F32)
    if fused_sinit:
        sinit_ap = E.wiring["cc2_sinit"]
        m0col = E.inp("m0col", [128, 1], F32)
    else:
        s_init = E.inp("s_init", [MP, MH * MN], F32)
    c_fm_d = E.inp("c_fm", [MN, T], BF16)
    cum_d = E.inp("cum_out", [T, MH], F32)
    z_tm = E.inp("z_tm", [T, ZDIM], F32)
    nw_row = E.inp("nw_row", [1, ZDIM], F32)
    woutT = E.inp("woutT", [ZDIM, HID], F32)
    attn1_T = E.out("attn1_T", [HID, T], F32)
    yn_fm = E.scratch("yn_fm", [ZDIM, T], BF16)
    if True:
        S_t, _, _ = AF_.alloc("S_tb", MH * MN, parts=MP, aliases=["S"])
        AF_.pin("af:S_tb")
        Cfm, _, _ = AB.alloc("Cfm", T, parts=MN, aliases=["Cfm"])
        AB.pin("ab:Cfm")
        decs, _, _ = AF_.alloc("decs", nCH * MH, parts=1, aliases=["decs"])
        AF_.pin("af:decs")
        decB, _, _ = AF_.alloc("decBb", MH, parts=MP, aliases=["decB"])
        AF_.pin("af:decBb")
        NW, _, _ = AB.alloc("NW", ZDIM, aliases=["NW"])
        AB.pin("ab:NW")
        stat, _, _ = AF_.alloc("statb", 8, aliases=["st0", "st1", "st2", "st3"])
        AF_.pin("af:statb")
        ec, _, _ = AF_.alloc("ec", MH, aliases=["ec"])
        AF_.pin("af:ec")
        cumc, _, _ = AF_.alloc("cumc", MH, aliases=["cumc"])
        AF_.pin("af:cumc")
        shb, _, _ = AB.alloc("shb", MN, parts=MP, aliases=["shb"])
        AB.pin("ab:shb")
        sthb, _, _ = AB.alloc("sthb", MP, parts=MN, aliases=["sthb"])
        AB.pin("ab:sthb")
        ynb, _, _ = AB.alloc("ynb", 128, aliases=["ynb"])
        AB.pin("ab:ynb")
        nwr, _, _ = AF_.alloc("nwrb", 512, parts=1, aliases=["nwr"])
        AF_.pin("af:nwrb")
        if fused_sinit:
            m0s, _, _ = AF_.alloc("m0s_b", 1, aliases=["m0s_b"])
            AF_.pin("af:m0s_b")
            P.dma("sync", lambda e: e.dma_start(out=m0s[:, 0:1], in_=m0col),
                  w=["m0s_b"], stream="cst")
            P.dma("sync", lambda e: e.dma_start(out=S_t[:, :], in_=sinit_ap),
                  w=["S"], stream="cst", deps=E.wiring.get("cc2_tok"))
            P.op("vector", lambda v: v.tensor_scalar(
                S_t[:, :], S_t[:, :], m0s[0:MP, 0:1], None, ALU.mult),
                r=["m0s_b", "S"], w=["S"])
        else:
            P.dma("sync", lambda e: e.dma_start(out=S_t[:, :], in_=s_init), w=["S"], stream="cst")
        P.dma("sync", lambda e: e.dma_start(out=Cfm[:, :], in_=c_fm_d), w=["Cfm"], stream="cst")
        P.dma("sync", lambda e: e.dma_start(out=decs[0:1, :], in_=decays_d), w=["decs"], stream="cst")
        for zb in range(ZDIM // 512):
            P.dma("sync", lambda e, zb=zb: e.dma_start(
                out=nwr[0:1, :], in_=nw_row[:, zb * 512:(zb + 1) * 512]),
                r=["nwr"], w=["nwr"], stream="cst")
            P.op("tensor", lambda t, zb=zb: t.matmul(
                pmisc[:, :], onesf_r[:], nwr[0:1, :],
                start=True, stop=True),
                r=["nwr", "onesf_r", "NW"], w=["px0"])
            P.op("vector", lambda v, zb=zb: v.tensor_copy(NW[:, zb * 512:(zb + 1) * 512], pmisc[:, :]),
                 r=["px0"], w=["NW"])
        stc, stck = [], []
        for i in range(2):
            s_, k_, _ = AB.alloc(f"stc{i}", MH * MN)
            stc.append(s_), stck.append(k_)
            AB.pin(k_)
        zt, ztk, _ = AF_.alloc("zt", ZDIM)
        AF_.pin("af:zt")
        yfin, yfink, _ = AF_.alloc("yfin", ZDIM)
        AF_.pin("af:yfin")
        slz, slzk, _ = AF_.alloc("slz", ZDIM)
        AF_.pin("af:slz")

        for c in range(nCH):
            ci = c % 2
            P.dma("sync", lambda e, c=c, ci=ci: e.dma_start(
                out=stc[ci][0:MP, :], in_=states_a[c]),
                w=[stck[ci]], stream=f"st{ci}")
            P.dma("sync", lambda e, c=c: e.dma_start(
                out=yfin[:], in_=y_part[c * 128:(c + 1) * 128, :]),
                r=[yfink] + [f"{yfink}:{h}" for h in range(MH)], w=[yfink], stream="yp")
            P.dma("sync", lambda e, c=c: e.dma_start(
                out=zt[:], in_=z_tm[c * 128:(c + 1) * 128, :]),
                w=[ztk], stream="zt")
            P.dma("sync", lambda e, c=c: e.dma_start(
                out=cumc[:], in_=cum_d[c * 128:(c + 1) * 128, :]),
                w=["cumc"], stream="cum")
            P.op("scalar", lambda s: s.activation(ec[:], cumc[:], AF.Exp),
                 r=["cumc"], w=["ec"])
            P.op("tensor", lambda t, c=c: t.matmul(
                pmisc[0:MP, 0:MH], onesf_r[:, 0:MP], decs[0:1, c * MH:(c + 1) * MH],
                start=True, stop=True),
                r=["decs", "onesf_r", "decB"], w=["px0"])
            P.op("vector", lambda v: v.tensor_copy(decB[0:MP, :], pmisc[0:MP, 0:MH]),
                 r=["px0"], w=["decB"])
            for h in range(MH):
                P.op("vector", lambda v, h=h: v.tensor_copy(
                    shb[0:MP, :], S_t[:, h * MN:(h + 1) * MN]),
                    r=["S"], w=["shb"])
                g = h % 2
                pt_ = ptq0 if g == 0 else ptq1
                ptk = f"pt{g}"
                P.op("tensor", lambda t, pt_=pt_: t.transpose(pt_[:, 0:MP], shb[0:MP, :], idb[0:MP, 0:MP]),
                     r=["shb", "idb"], w=[ptk])
                P.op("vector", lambda v, pt_=pt_: v.tensor_copy(sthb[0:MN, :], pt_[:, 0:MP]),
                     r=[ptk], w=["sthb"])
                P.op("tensor", lambda t, c=c: t.matmul(
                    pyo[:, 0:MP], Cfm[:, c * 128:(c + 1) * 128], sthb[0:MN, :],
                    start=True, stop=True),
                    r=["Cfm", "sthb"], w=["pb2"])
                P.op("vector", lambda v, h=h: v.scalar_tensor_tensor(
                    yfin[:, h * MP:(h + 1) * MP], pyo[:, 0:MP], ec[:, h:h + 1],
                    yfin[:, h * MP:(h + 1) * MP], ALU.mult, ALU.add),
                    r=["pb2", "ec", yfink], w=[f"{yfink}:{h}"])
            for h in range(MH):
                P.op("vector", lambda v, h=h, ci=ci: v.scalar_tensor_tensor(
                    S_t[:, h * MN:(h + 1) * MN], S_t[:, h * MN:(h + 1) * MN],
                    decB[0:MP, h:h + 1], stc[ci][0:MP, h * MN:(h + 1) * MN],
                    ALU.mult, ALU.add),
                    r=["decB", stck[ci], "S"], w=["S"])
            P.op("scalar", lambda s: s.activation(slz[:], zt[:], AF.Silu),
                 r=[ztk], w=[slzk])
            P.op("vector", lambda v: v.tensor_tensor(
                yfin[:], yfin[:], slz[:], ALU.mult),
                r=[slzk] + [f"{yfink}:{h}" for h in range(MH)], w=[yfink])
            P.op("scalar", lambda s: s.activation(slz[:], yfin[:], AF.Square,
                                                  accum_out=stat[:, 0:1]),
                 r=[yfink], w=[slzk, "st0"])
            P.op("vector", lambda v: v.tensor_scalar(stat[:, 1:2], stat[:, 0:1], 1.0 / ZDIM, 1e-5, ALU.mult, ALU.add),
                 r=["st0"], w=["st1"])
            P.op("scalar", lambda s: s.activation(stat[:, 2:3], stat[:, 1:2], AF.Sqrt),
                 r=["st1"], w=["st2"])
            P.op("vector", lambda v: v.reciprocal(stat[:, 3:4], stat[:, 2:3]),
                 r=["st2"], w=["st3"])
            P.op("vector", lambda v: v.scalar_tensor_tensor(
                yfin[:], yfin[:], stat[:, 3:4], NW[:], ALU.mult, ALU.mult),
                r=["st3", "NW", yfink], w=[yfink])
            for zc_ in range(nZC):
                g = zc_ % 2
                pt_ = ptq0 if g == 0 else ptq1
                ptk = f"pt{g}"
                P.op("vector", lambda v, zc_=zc_: v.tensor_copy(
                    ynb[:], yfin[:, zc_ * 128:(zc_ + 1) * 128]),
                    r=[yfink], w=["ynb"])
                P.op("tensor", lambda t, pt_=pt_: t.transpose(pt_[:, 0:128], ynb[:], idb[:]),
                     r=["ynb", "idb"], w=[ptk])
                P.op("vector", lambda v, pt_=pt_: v.tensor_copy(ynb[:], pt_[:, 0:128]),
                     r=[ptk], w=["ynb"])
                P.dma("gpsimd", lambda e, c=c, zc_=zc_: e.dma_start(
                    out=yn_fm[zc_ * 128:(zc_ + 1) * 128, c * 128:(c + 1) * 128],
                    in_=ynb[:]),
                    r=["ynb"], w=[f"ynf:{c}:{zc_}"], stream=f"so{zc_ % 2}")

        for k in ["af:S_tb", "af:decs", "af:decBb", "af:ec", "af:cumc",
                  "af:zt", "af:yfin", "af:slz"]:
            AF_.unpin(k)
        for k in ["ab:Cfm", "ab:NW", "ab:shb", "ab:sthb", "ab:ynb", "ab:stc0", "ab:stc1"]:
            AB.unpin(k)
        AB.begin_phase()
        AF_.begin_phase()
        ynl, ynlk = [], []
        for i in range(1):
            s_, k_, _ = AB.alloc(f"ynl{i}", nZC * 512)
            ynl.append(s_), ynlk.append(k_)
        wol, wolk = [], []
        for i in range(2):
            s_, k_, _ = AB.alloc(f"wol{i}", nZC * 128)
            wol.append(s_), wolk.append(k_)
        oev, oevk = [], []
        for i in range(2):
            s_, k_, _ = AF_.alloc(f"oev{i}", 512)
            oev.append(s_), oevk.append(k_)
        for tb in range(nTB):
            ti = 0
            P.dma("sync", lambda e, tb=tb, ti=ti: e.dma_start(
                out=ynl[ti].rearrange("p (zc t) -> p zc t", zc=nZC),
                in_=yn_fm[:, tb * 512:(tb + 1) * 512].rearrange("(zc p) t -> p zc t", p=128)),
                r=[f"ynf:{c}:{zc_}" for c in range(tb * 4, tb * 4 + 4) for zc_ in range(nZC)],
                w=[ynlk[ti]], stream=f"yn{ti}")
            for fo in range(nCC):
                wi = fo % 2
                P.dma("gpsimd", lambda e, fo=fo, wi=wi: e.dma_start(
                    out=wol[wi].rearrange("p (zc f) -> p zc f", zc=nZC),
                    in_=woutT[:, fo * 128:(fo + 1) * 128].rearrange("(zc p) f -> p zc f", p=128)),
                    w=[wolk[wi]], stream=f"wo{wi}")
                pk = f"pb{fo % 2}"
                for zc_ in range(nZC):
                    P.op("tensor", lambda t, fo=fo, zc_=zc_, ti=ti, wi=wi: t.matmul(
                        pb[fo % 2][:], wol[wi][:, zc_ * 128:(zc_ + 1) * 128],
                        ynl[ti][:, zc_ * 512:(zc_ + 1) * 512],
                        start=(zc_ == 0), stop=(zc_ == nZC - 1)),
                        r=[wolk[wi], ynlk[ti]], w=[pk], inc=(zc_ == nZC - 1))
                P.op("vector", lambda v, fo=fo: v.tensor_copy(oev[fo % 2][:], pb[fo % 2][:]),
                     r=[pk], w=[oevk[fo % 2]])
                P.dma("gpsimd", lambda e, fo=fo, tb=tb: e.dma_start(
                    out=attn1_T[fo * 128:(fo + 1) * 128, tb * 512:(tb + 1) * 512],
                    in_=oev[fo % 2][:]),
                    r=[oevk[fo % 2]], stream=f"ao{fo % 2}")

        AF_.unpin("af:statb")
        AF_.unpin("af:nwrb")
        if fused_sinit:
            AF_.unpin("af:m0s_b")


QF = 5632
nQF = QF // 128




def build_launch_c2b(T=2048):
    return _standalone(emit_c2b, T=T)


def emit_d(E, T=2048):
    nTB = T // 512
    nTT = T // 128
    nc, P, AB, AF_ = E.nc, E.P, E.AB, E.AF
    AB.begin_phase()
    AF_.begin_phase()
    pg, pu, pmisc = E.pb[2], E.pb[3], E.px[0]
    pas1 = E.px[1]
    idf = E.idf
    onesf_r, onesb_c = E.onesf_r, E.onesb_c
    xn1_tm = E.inp("xn1_tm", [T, HID], BF16)
    x1_fm = E.inp("x1_fm", [HID, T], F32)
    at1_fm = E.inp("at1_fm", [HID, T], F32)
    wqgT = E.inp("wqgT", [HID, QF], I32)
    wquT = E.inp("wquT", [HID, QF], I32)
    wqdT2 = E.inp("wqdT2", [QF, HID], I32)
    sg_r = E.inp("sg_r", [nQF, 128], F32)
    su_r = E.inp("su_r", [nQF, 128], F32)
    sd_r = E.inp("sd_r", [nCC, 128], F32)
    outT = E.out("outT", [HID, T], F32)
    act_d = E.scratch("act_d", [QF, T], BF16)
    if True:
        sgc, _, _ = AF_.alloc("sgc", nQF, aliases=["sgc"])
        AF_.pin("af:sgc")
        suc, _, _ = AF_.alloc("suc", nQF, aliases=["suc"])
        AF_.pin("af:suc")
        sdc, _, _ = AF_.alloc("sdc", nCC, aliases=["sdc"])
        AF_.pin("af:sdc")
        xbc128, _, _ = AF_.alloc("xbc128", T, aliases=["xbc128"])
        AF_.pin("af:xbc128")
        xsd128, _, _ = AF_.alloc("xsd128", T, aliases=["xsd128"])
        AF_.pin("af:xsd128")
        srow, _, _ = AF_.alloc("srow", 512, parts=1, aliases=["srow"])
        AF_.pin("af:srow")
        nwr, _, _ = AF_.alloc("nwrd", 128, parts=nQF, aliases=["nwr"])
        AF_.pin("af:nwrd")
        for src_d, dst, k, n_ in [(sg_r, sgc, "sgc", nQF), (su_r, suc, "suc", nQF), (sd_r, sdc, "sdc", nCC)]:
            P.dma("sync", lambda e, src_d=src_d, n_=n_: e.dma_start(
                out=nwr[0:n_, :], in_=src_d), r=["nwr"], w=["nwr"], stream="cst")
            P.op("tensor", lambda t, n_=n_: t.transpose(pmisc[:, 0:n_], nwr[0:n_, :], idf[0:n_, 0:n_]),
                 r=["nwr", "idf"], w=["px0"])
            P.op("vector", lambda v, dst=dst, n_=n_: v.tensor_copy(dst[:, 0:n_], pmisc[:, 0:n_]),
                 r=["px0"], w=[k])

        xnf, xnfk, _ = AB.alloc("xnfd", nCC * T)
        AB.pin(xnfk)
        xnfks = [f"{xnfk}:{cc}" for cc in range(nCC)]
        for cc in range(nCC):
            P.dma("scalar", lambda e, cc=cc: e.dma_start(
                out=xnf[:, cc * T:(cc + 1) * T],
                in_=xn1_tm[:, cc * 128:(cc + 1) * 128], transpose=True),
                w=[xnfks[cc]], stream=f"xnf{cc % 2}")
        for tb in range(nTB):
            for cc in range(nCC):
                P.op("tensor", lambda t, tb=tb, cc=cc: t.matmul(
                    pmisc[0:1, :], onesb_c[:], xnf[:, cc * T + tb * 512: cc * T + tb * 512 + 512],
                    start=(cc == 0), stop=(cc == nCC - 1)),
                    r=["onesb_c", xnfks[cc]], w=["px0"], inc=(cc == nCC - 1))
            P.op("vector", lambda v: v.tensor_copy(srow[0:1, :], pmisc[0:1, :]),
                 r=["px0"], w=["srow"])
            P.op("tensor", lambda t: t.matmul(pmisc[:, :], onesf_r[:], srow[0:1, :],
                                              start=True, stop=True),
                 r=["srow", "onesf_r"], w=["px0"])
            P.op("vector", lambda v, tb=tb: v.tensor_scalar(
                xbc128[:, tb * 512:(tb + 1) * 512], pmisc[:, :], -128.0, None, ALU.mult),
                r=["px0"], w=["xbc128"])

        wg, wgk = [], []
        wu, wuk = [], []
        for i in range(2):
            s_, k_, _ = AB.alloc(f"wg{i}", nCC * 128)
            wg.append(s_), wgk.append(k_)
            s_, k_, _ = AB.alloc(f"wu{i}", nCC * 128)
            wu.append(s_), wuk.append(k_)
        t1t, t1k, _ = AF_.alloc("t1", 512)
        t2t, t2k, _ = AF_.alloc("t2", 512)
        gvt, gvk, _ = AF_.alloc("gv", 512)
        actb, actbk, _ = AB.alloc("actb", 512)

        for fb in range(nQF):
            wi = fb % 2
            P.dma("gpsimd", lambda e, fb=fb, wi=wi: e.dma_start(
                out=wg[wi].rearrange("p (cc f) -> p cc f", cc=nCC),
                in_=wqgT[:, fb * 128:(fb + 1) * 128].rearrange("(cc p) f -> p cc f", p=128)),
                w=[wgk[wi]], stream=f"wg{wi}")
            P.dma("gpsimd", lambda e, fb=fb, wi=wi: e.dma_start(
                out=wu[wi].rearrange("p (cc f) -> p cc f", cc=nCC),
                in_=wquT[:, fb * 128:(fb + 1) * 128].rearrange("(cc p) f -> p cc f", p=128)),
                w=[wuk[wi]], stream=f"wu{wi}")
            for tb in range(nTB):
                for cc in range(nCC):
                    P.op("tensor", lambda t, cc=cc, tb=tb, wi=wi: t.matmul(
                        pg[:], wg[wi][:, cc * 128:(cc + 1) * 128],
                        xnf[:, cc * T + tb * 512: cc * T + tb * 512 + 512],
                        start=(cc == 0), stop=(cc == nCC - 1)),
                        r=[wgk[wi], xnfks[cc]], w=["pb2"], inc=(cc == nCC - 1))
                for cc in range(nCC):
                    P.op("tensor", lambda t, cc=cc, tb=tb, wi=wi: t.matmul(
                        pu[:], wu[wi][:, cc * 128:(cc + 1) * 128],
                        xnf[:, cc * T + tb * 512: cc * T + tb * 512 + 512],
                        start=(cc == 0), stop=(cc == nCC - 1)),
                        r=[wuk[wi], xnfks[cc]], w=["pb3"], inc=(cc == nCC - 1))
                P.op("vector", lambda v, tb=tb: v.tensor_tensor(
                    t1t[:], pg[:], xbc128[:, tb * 512:(tb + 1) * 512], ALU.add),
                    r=["pb2", "xbc128"], w=[t1k])
                P.op("scalar", lambda s, fb=fb: s.activation(
                    gvt[:], t1t[:], AF.Silu, scale=sgc[:, fb:fb + 1]),
                    r=[t1k, "sgc"], w=[gvk])
                P.op("vector", lambda v, tb=tb: v.tensor_tensor(
                    t2t[:], pu[:], xbc128[:, tb * 512:(tb + 1) * 512], ALU.add),
                    r=["pb3", "xbc128"], w=[t2k])
                P.op("vector", lambda v, fb=fb: v.scalar_tensor_tensor(
                    actb[:], t2t[:], suc[:, fb:fb + 1], gvt[:], ALU.mult, ALU.mult),
                    r=[t2k, "suc", gvk], w=[actbk])
                P.dma("gpsimd", lambda e, fb=fb, tb=tb: e.dma_start(
                    out=act_d[fb * 128:(fb + 1) * 128, tb * 512:(tb + 1) * 512],
                    in_=actb[:]),
                    r=[actbk], w=[f"act:{fb}:{tb}"], stream=f"so{tb % 2}")

        AB.unpin(xnfk)
        AB.begin_phase()
        AF_.begin_phase()
        acl, aclk, _ = AB.alloc("acl", nQF * 512)
        wd, wdk = [], []
        for i in range(2):
            s_, k_, _ = AB.alloc(f"wd{i}", nQF * 128)
            wd.append(s_), wdk.append(k_)
        x1c, x1ck, _ = AF_.alloc("x1c", 512)
        a1c, a1ck, _ = AF_.alloc("a1c", 512)
        otv, otk, _ = AF_.alloc("ot", 512)
        for tb in range(nTB):
            P.dma("sync", lambda e, tb=tb: e.dma_start(
                out=acl.rearrange("p (fb t) -> p fb t", fb=nQF),
                in_=act_d[:, tb * 512:(tb + 1) * 512].rearrange("(fb p) t -> p fb t", p=128)),
                r=[f"act:{fb}:{tb}" for fb in range(nQF)], w=[aclk], stream="acl")
            for fb in range(nQF):
                P.op("tensor", lambda t, fb=fb: t.matmul(
                    pas1[0:1, :], onesb_c[:], acl[:, fb * 512:(fb + 1) * 512],
                    start=(fb == 0), stop=(fb == nQF - 1)),
                    r=[aclk, "onesb_c"], w=["px1"], inc=(fb == nQF - 1))
            P.op("vector", lambda v: v.tensor_copy(srow[0:1, :], pas1[0:1, :]),
                 r=["px1", "srow"], w=["srow"])
            P.op("tensor", lambda t: t.matmul(pmisc[:, :], onesf_r[:], srow[0:1, :],
                                              start=True, stop=True),
                 r=["srow", "onesf_r"], w=["px0"])
            P.op("vector", lambda v, tb=tb: v.tensor_scalar(
                xsd128[:, tb * 512:(tb + 1) * 512], pmisc[:, :], -128.0, None, ALU.mult),
                r=["px0"], w=["xsd128"])
            for ho in range(nCC):
                wi = ho % 2
                P.dma("gpsimd", lambda e, ho=ho, wi=wi: e.dma_start(
                    out=wd[wi].rearrange("p (fb f) -> p fb f", fb=nQF),
                    in_=wqdT2[:, ho * 128:(ho + 1) * 128].rearrange("(fb p) f -> p fb f", p=128)),
                    w=[wdk[wi]], stream=f"wd{wi}")
                for fb in range(nQF):
                    P.op("tensor", lambda t, fb=fb, wi=wi: t.matmul(
                        pg[:], wd[wi][:, fb * 128:(fb + 1) * 128],
                        acl[:, fb * 512:(fb + 1) * 512],
                        start=(fb == 0), stop=(fb == nQF - 1)),
                        r=[wdk[wi], aclk], w=["pb2"], inc=(fb == nQF - 1))
                P.dma("sync", lambda e, ho=ho, tb=tb: e.dma_start(
                    out=x1c[:], in_=x1_fm[ho * 128:(ho + 1) * 128, tb * 512:(tb + 1) * 512]),
                    r=[x1ck], w=[x1ck], stream="x1")
                P.dma("sync", lambda e, ho=ho, tb=tb: e.dma_start(
                    out=a1c[:], in_=at1_fm[ho * 128:(ho + 1) * 128, tb * 512:(tb + 1) * 512]),
                    r=[a1ck], w=[a1ck], stream="a1")
                P.op("vector", lambda v, tb=tb: v.tensor_tensor(
                    t1t[:], pg[:], xsd128[:, tb * 512:(tb + 1) * 512], ALU.add),
                    r=["pb2", "xsd128"], w=[t1k])
                P.op("vector", lambda v, ho=ho: v.scalar_tensor_tensor(
                    otv[:], t1t[:], sdc[:, ho:ho + 1], x1c[:], ALU.mult, ALU.add),
                    r=[t1k, "sdc", x1ck], w=[otk])
                P.op("vector", lambda v: v.tensor_tensor(otv[:], otv[:], a1c[:], ALU.add),
                     r=[otk, a1ck], w=[otk])
                P.dma("gpsimd", lambda e, ho=ho, tb=tb: e.dma_start(
                    out=outT[ho * 128:(ho + 1) * 128, tb * 512:(tb + 1) * 512],
                    in_=otv[:]),
                    r=[otk], stream=f"oo{ho % 2}")

        for k in ["af:sgc", "af:suc", "af:sdc", "af:xbc128", "af:xsd128",
                  "af:srow", "af:nwrd"]:
            AF_.unpin(k)




def build_launch_d(T=2048):
    return _standalone(emit_d, T=T)


def stage_c2b(inputs, res_c2a, res_c1, T=2048):
    nCH = T // 128
    tr = lambda w: np.ascontiguousarray(np.asarray(w).T)
    shared = dict(
        nw_row=np.asarray(inputs["L1_attn_norm_w"]).reshape(1, ZDIM),
        woutT=tr(inputs["L1_attn_W_out"]),
        identf=np.eye(128, dtype=np.float32),
        identb=np.eye(128, dtype=BF),
    )
    in_maps = []
    for c in range(8):
        hh = c % 2
        if hh == 0:
            s_init = np.zeros((MP, MH * MN), np.float32)
        else:
            s_init = np.asarray(res_c2a[c - 1]["sloc"], np.float32)
        m = dict(shared)
        m.update(
            y_part=np.asarray(res_c2a[c]["y_part"]),
            states_a=np.asarray(res_c2a[c]["states_a"]),
            decays_d=np.ascontiguousarray(np.asarray(res_c2a[c]["decays_d"]).reshape(1, nCH * MH)),
            s_init=s_init,
            c_fm_d=np.asarray(res_c2a[c]["c_fm"]),
            cum_d=np.asarray(res_c2a[c]["cum_out"]),
            z_tm=np.asarray(res_c1[c]["z_tm"]),
        )
        in_maps.append(m)
    return in_maps


def stage_d(inputs, res_c1, res_c2b, x1, T=2048):
    S = x1.shape[1]
    So = S // 2
    tr = lambda w: np.ascontiguousarray(np.asarray(w).T)
    shared = dict(
        wqgT=tr(inputs["L1_ffn_wq_gate"]),
        wquT=tr(inputs["L1_ffn_wq_up"]),
        wqdT2=tr(inputs["L1_ffn_wq_down"]),
        sg_r=np.ascontiguousarray(np.asarray(inputs["L1_ffn_s_gate"]).reshape(nQF, 128)),
        su_r=np.ascontiguousarray(np.asarray(inputs["L1_ffn_s_up"]).reshape(nQF, 128)),
        sd_r=np.ascontiguousarray(np.asarray(inputs["L1_ffn_s_down"]).reshape(nCC, 128)),
        identf=np.eye(128, dtype=np.float32),
        identb=np.eye(128, dtype=BF),
    )
    in_maps = []
    for c in range(8):
        b, hh = c // 2, c % 2
        m = dict(shared)
        m.update(
            xn1_tm=np.asarray(res_c1[c]["xn1_out"]),
            x1_fm=np.ascontiguousarray(x1[b, hh * So: hh * So + T].T),
            at1_fm=np.asarray(res_c2b[c]["attn1_T"]),
        )
        in_maps.append(m)
    return in_maps


RG_PAIRS = [[0, 1], [2, 3], [4, 5], [6, 7]]


def build_mega(S=4096):
    T = S // 2
    nCH = T // 128
    nc = bass.Bass("TRN2", target_bir_lowering=False)
    ctx = ExitStack()
    with ctx:
        E = Env(nc, ctx, fused=True)
        P = E.P
        emit_a(E, S)
        P.fence()
        E.wiring["xn_tm"] = E._dram["xn_all"][0:T, :]
        E.wiring["attn_fm"] = E._dram["attn_outT"]
        emit_b(E, T)
        P.fence()
        emit_c1(E, T)
        cc1_in = E.scratch("cc1_in", [CONVD, 512], F32)
        cc1_out = E.scratch("cc1_out", [2 * CONVD, 512], F32)
        P.dma("gpsimd", lambda e: e.dma_start(
            out=cc1_in, in_=E._dram["xbc_raw"][:, T - 512:T]),
            r=[f"xbcs:{fc}:{T // 512 - 1}" for fc in range(nXBC)],
            w=["cc1_in"], stream="cc")
        P.cc(lambda g: g.collective_compute(
            "AllGather", ALU.bypass, replica_groups=RG_PAIRS,
            ins=[cc1_in], outs=[cc1_out]),
            r=["cc1_in"], w=["cc1_out"])
        E.wiring["cc1_halo"] = cc1_out[0:CONVD, 509:512]
        E.wiring["cc1_tok"] = P.key_toks("cc1_out")
        P.fence()
        emit_c2a(E, T, fused_halo=True)
        cc2_out = E.scratch("cc2_out", [2 * MP, MH * MN], F32)
        P.cc(lambda g: g.collective_compute(
            "AllGather", ALU.bypass, replica_groups=RG_PAIRS,
            ins=[E._dram["sloc"]], outs=[cc2_out]),
            r=["sloc_s"], w=["cc2_out"])
        E.wiring["cc2_sinit"] = cc2_out[0:MP, :]
        E.wiring["cc2_tok"] = P.key_toks("cc2_out")
        P.fence()
        emit_c2b(E, T, fused_sinit=True)
        P.fence()
        E.wiring["xn1_tm"] = E._dram["xn1_out"]
        E.wiring["x1_fm"] = E._dram["x1T"]
        E.wiring["at1_fm"] = E._dram["attn1_T"]
        emit_d(E, T)
        E.finish()
    return nc


def stage_mega(inputs, S=4096):
    T = S // 2
    x = np.asarray(inputs["L0_attn_x"])
    im_a = stage_a(inputs, S)
    tr = lambda w: np.ascontiguousarray(np.asarray(w).T)
    tri, mnegw = _tri_mask()
    cw = np.asarray(inputs["L1_attn_conv_w"]).reshape(nXBC, 128, 4)
    cw = np.ascontiguousarray(cw.transpose(1, 0, 2).reshape(128, nXBC * 4))
    shared = dict(
        gwT=tr(inputs["L0_ffn_gate_weight"]),
        guT=np.ascontiguousarray(np.asarray(inputs["L0_ffn_gate_up_weights"]).transpose(0, 2, 1)),
        dwT=np.ascontiguousarray(np.asarray(inputs["L0_ffn_down_weights"]).transpose(0, 2, 1)),
        sguT=tr(inputs["L0_ffn_shared_gate_up"]),
        sdT=tr(inputs["L0_ffn_shared_down"]),
        winT=tr(inputs["L1_attn_W_in"]),
        cw_d=cw,
        dtb_d=np.asarray(inputs["L1_attn_dt_bias"]).reshape(1, MH),
        alog_d=np.asarray(inputs["L1_attn_A_log"]).reshape(1, MH),
        dd_d=np.asarray(inputs["L1_attn_D"]).reshape(1, MH),
        tri_d=tri, mnegw_d=mnegw,
        nw_row=np.asarray(inputs["L1_attn_norm_w"]).reshape(1, ZDIM),
        woutT=tr(inputs["L1_attn_W_out"]),
        wqgT=tr(inputs["L1_ffn_wq_gate"]),
        wquT=tr(inputs["L1_ffn_wq_up"]),
        wqdT2=tr(inputs["L1_ffn_wq_down"]),
        sg_r=np.ascontiguousarray(np.asarray(inputs["L1_ffn_s_gate"]).reshape(nQF, 128)),
        su_r=np.ascontiguousarray(np.asarray(inputs["L1_ffn_s_up"]).reshape(nQF, 128)),
        sd_r=np.ascontiguousarray(np.asarray(inputs["L1_ffn_s_down"]).reshape(nCC, 128)),
    )
    in_maps = []
    for c in range(8):
        b, hh = c // 2, c % 2
        m = dict(im_a[c])
        m.update(shared)
        m.update(
            x_fm=np.ascontiguousarray(x[b, hh * T:(hh + 1) * T].T),
            m0col=np.full((128, 1), float(hh), np.float32),
        )
        in_maps.append(m)
    return in_maps


def kernel(**inputs):
    S = 4096
    T = 2048
    nc = build_mega(S)
    res = run_bass_kernel_spmd(nc, stage_mega(inputs, S), core_ids=list(range(8))).results
    out = np.zeros((4, S, HID), np.float32)
    for c in range(8):
        b, hh = c // 2, c % 2
        out[b, hh * T:(hh + 1) * T] = np.asarray(res[c]["outT"]).T
    return out


# version 05
# speedup vs baseline: 1.1476x; 1.1476x over previous
import numpy as np
import ml_dtypes
import concourse.bass as bass
import concourse.mybir as mybir
from concourse.bass_utils import run_bass_kernel_spmd
from contextlib import ExitStack

F32 = mybir.dt.float32
BF16 = mybir.dt.bfloat16
I32 = mybir.dt.int32
AF = mybir.ActivationFunctionType
ALU = mybir.AluOpType
AX = mybir.AxisListType.X
BF = ml_dtypes.bfloat16

ENGINES = ("tensor", "vector", "scalar", "gpsimd", "sync")


class Tok:
    __slots__ = ("sem_key", "cnt")

    def __init__(self, sem_key, cnt):
        self.sem_key = sem_key
        self.cnt = cnt


class Prog:

    def __init__(self, nc, ctx):
        self.nc = nc
        self.ctx = ctx
        self.sems = {e: ctx.enter_context(nc.semaphore(f"cs_{e}")) for e in ENGINES}
        self.ccnt = {e: 0 for e in ENGINES}
        self.dstream = {}
        self.pending = {e: False for e in ENGINES}
        self.stream = {e: [] for e in ENGINES}
        self.wm = {e: {} for e in ENGINES}
        self.last_w = {}
        self.readers = {}
        self.nops = 0

    def _wait(self, eng, tok):
        if tok is None:
            return
        if isinstance(tok.sem_key, tuple) and tok.sem_key[0] == "dma":
            tok = Tok(tok.sem_key, self.dstream[tok.sem_key])
        if tok.sem_key == eng and tok.cnt > self.ccnt[eng]:
            return
        if self.wm[eng].get(tok.sem_key, 0) >= tok.cnt:
            return
        self.wm[eng][tok.sem_key] = tok.cnt
        sem = self.sems[tok.sem_key]
        cnt = tok.cnt
        self.stream[eng].append(lambda e, sem=sem, cnt=cnt: e.wait_ge(sem, cnt))

    def _deps(self, eng, r, w, extra=()):
        toks = [t for t in extra if t is not None]
        for k in r:
            t = self.last_w.get(k)
            if t is not None:
                toks.append(t)
        for k in w:
            t = self.last_w.get(k)
            if t is not None:
                toks.append(t)
            toks.extend(self.readers.get(k, ()))
        best = {}
        for t in toks:
            if best.get(t.sem_key) is None or best[t.sem_key].cnt < t.cnt:
                best[t.sem_key] = t
        for t in best.values():
            self._wait(eng, t)

    def _book(self, tok, r, w):
        for k in w:
            self.last_w[k] = tok
            self.readers[k] = []
        for k in r:
            self.readers.setdefault(k, []).append(tok)

    def op(self, eng, fn, r=(), w=(), inc=True, deps=()):
        self._deps(eng, r, w, deps)
        if inc:
            self.ccnt[eng] += 1
            sem = self.sems[eng]
            self.stream[eng].append(lambda e, fn=fn, sem=sem: fn(e).then_inc(sem, 1))
            self.pending[eng] = False
            tok = Tok(eng, self.ccnt[eng])
        else:
            self.stream[eng].append(fn)
            self.pending[eng] = True
            tok = Tok(eng, self.ccnt[eng] + 1)
        self.nops += 1
        self._book(tok, r, w)
        return tok

    def dma(self, eng, fn, r=(), w=(), stream=None, deps=()):
        if stream is None:
            stream = "d_" + eng
        skey = ("dma", eng, stream)
        if skey not in self.sems:
            self.sems[skey] = self.ctx.enter_context(
                self.nc.semaphore(f"d_{eng}_{stream}"))
            self.dstream[skey] = 0
        self._deps(eng, r, w, deps)
        if self.dstream[skey] > 0:
            self._wait(eng, Tok(skey, self.dstream[skey]))
        self.dstream[skey] += 16
        sem = self.sems[skey]
        self.stream[eng].append(lambda e, fn=fn, sem=sem: fn(e).then_inc(sem, 16))
        tok = Tok(skey, self.dstream[skey])
        self.nops += 1
        self._book(tok, r, w)
        return tok

    def cc(self, fn, r=(), w=()):
        skey = ("cc", len([k for k in self.sems if isinstance(k, tuple) and k[0] == "cc"]))
        self.sems[skey] = self.ctx.enter_context(self.nc.semaphore(f"cc{skey[1]}"))
        self._deps("gpsimd", r, w)
        sem = self.sems[skey]
        self.stream["gpsimd"].append(lambda e, fn=fn, sem=sem: fn(e).then_inc(sem, 1))
        tok = Tok(skey, 1)
        self.nops += 1
        self._book(tok, r, w)
        return tok

    def key_toks(self, key):
        toks = []
        t = self.last_w.get(key)
        if t is not None:
            toks.append(t)
        toks.extend(self.readers.get(key, ()))
        return toks

    def key_toks_prefix(self, prefix):
        toks = []
        pc = prefix + ":"
        for k, t in self.last_w.items():
            if k == prefix or k.startswith(pc):
                toks.append(t)
        for k, rl in self.readers.items():
            if k == prefix or k.startswith(pc):
                toks.extend(rl)
        return toks

    def close_group(self, eng):
        if self.pending[eng]:
            self.ccnt[eng] += 1
            sem = self.sems[eng]
            self.stream[eng].append(
                lambda e, sem=sem:
                    getattr(e, "engine_nop", getattr(e, "nop", None))().then_inc(sem, 1))
            self.pending[eng] = False

    def wait_all(self, eng):
        for f in ENGINES:
            if f != eng and self.ccnt[f] > 0:
                self.close_group(f)
                self._wait(eng, Tok(f, self.ccnt[f]))
        for skey, cnt in self.dstream.items():
            if cnt > 0:
                self._wait(eng, Tok(skey, cnt))

    def fence(self):
        for e in ENGINES:
            self.close_group(e)
        for e in ENGINES:
            if self.stream[e]:
                self.wait_all(e)

    def build(self):
        for e in ENGINES:
            assert not self.pending[e], f"engine {e} has uncovered group ops"
        block = self.ctx.enter_context(self.nc.Block())
        for ename in ENGINES:
            closures = self.stream[ename]
            if not closures:
                continue

            def section(eng_obj, closures=closures):
                for c in closures:
                    c(eng_obj)

            getattr(block, ename)(section)
        return block


class Arena:

    def __init__(self, P, tensor, key, n):
        self.P = P
        self.t = tensor
        self.key = key
        self.n = n
        self.cur = 0
        self.phase = 0
        self.live = []

    def pin(self, datakey):
        self.live = [(s, e, k, True if k == datakey else p, al, ph) for (s, e, k, p, al, ph) in self.live]

    def unpin(self, datakey):
        self.live = [(s, e, k, False if k == datakey else p, al, ph) for (s, e, k, p, al, ph) in self.live]

    def begin_phase(self):
        self.phase += 1

    def reset(self):
        self.cur = 0

    def alloc(self, name, n, parts=128, aliases=(), at=None):
        if at is not None:
            start, end = at, at + n
            assert end <= self.n
            deps = []
            keep = []
            for (s, e, k, p, al, ph) in self.live:
                if s < end and e > start:
                    assert not p and ph < self.phase, (
                        f"arena {self.key}: {name}@{at} overlaps live {k}")
                    deps.extend(self.P.key_toks_prefix(k))
                    for a in al:
                        deps.extend(self.P.key_toks_prefix(a))
                else:
                    keep.append((s, e, k, p, al, ph))
            dk = f"{self.key}:{name}"
            keep.append((start, end, dk, False, tuple(aliases), self.phase))
            self.live = keep
            return self.t[0:parts, start:end], dk, deps
        tries = 0
        while True:
            tries += 1
            assert tries < 200, f"arena {self.key}: cannot place {name}({n})"
            if self.cur + n > self.n:
                self.cur = 0
            start, end = self.cur, self.cur + n
            conflict = None
            for (s, e, k, p, al, ph) in self.live:
                if (p or ph == self.phase) and s < end and e > start:
                    conflict = e
                    break
            if conflict is None:
                break
            self.cur = conflict
        self.cur = end
        deps = []
        keep = []
        for (s, e, k, p, al, ph) in self.live:
            if s < end and e > start:
                assert not p, f"arena {self.key}: {name} overlaps pinned {k}"
                assert ph < self.phase, (
                    f"arena {self.key}: {name} overlaps same-phase region {k}")
                deps.extend(self.P.key_toks_prefix(k))
                for a in al:
                    deps.extend(self.P.key_toks_prefix(a))
            else:
                keep.append((s, e, k, p, al, ph))
        dk = f"{self.key}:{name}"
        keep.append((start, end, dk, False, tuple(aliases), self.phase))
        self.live = keep
        return self.t[0:parts, start:end], dk, deps


HID = 2048
NH = 16
NOPE = 128
ROPE = 64
VD = 128
QC = 1536
KVC = 512
SM_SCALE = float((NOPE + ROPE) ** -0.5)
LN_EPS = 1e-5
RMS_EPS = 1e-6
nCC = HID // 128
nKV = KVC // 128
nQC = QC // 128


def _rope_tables(S):
    inv = 1.0 / (10000.0 ** (np.arange(0, ROPE, 2, dtype=np.float64) / ROPE))
    t = np.arange(S, dtype=np.float64)
    freqs = t[:, None] * inv[None, :]
    emb = np.concatenate([freqs, freqs], axis=-1)
    return np.cos(emb).astype(np.float32), np.sin(emb).astype(np.float32)


def _rot_mat():
    RT = np.zeros((128, 128), dtype=np.float32)
    for m in range(128):
        if (m % 64) < 32:
            RT[m + 32, m] = -1.0
        else:
            RT[m - 32, m] = 1.0
    return RT.astype(BF)




class Env:

    def __init__(self, nc, ctx, fused, ab_cols=51200, af_cols=26368, final_outs=("outT",)):
        self.nc = nc
        self.ctx = ctx
        self.fused = fused
        self.wiring = {}
        self.final_outs = final_outs
        self.P = Prog(nc, ctx)
        sb = lambda n, shp, dt: ctx.enter_context(nc.sbuf_tensor(n, shp, dt))
        ps = lambda n, shp, dt: ctx.enter_context(nc.psum_tensor(n, shp, dt))
        self.AB = Arena(self.P, sb("ab", [128, ab_cols], BF16), "ab", ab_cols)
        self.AF = Arena(self.P, sb("af", [128, af_cols], F32), "af", af_cols)
        self.pb = [ps(f"pb{i}", [128, 512], F32) for i in range(4)]
        self.pt = [ps(f"pt{i}", [128, 1024], BF16) for i in range(2)]
        self.px = [ps(f"px{i}", [128, 512], F32) for i in range(2)]
        self.onesf_r = sb("onesf_r", [1, 128], F32)
        self.ones_c = sb("ones_c", [128, 1], F32)
        self.onesb_c = sb("onesb_c", [128, 1], BF16)
        self.idf = sb("idf", [128, 128], F32)
        self.idb = sb("idb", [128, 128], BF16)
        self._dram = {}
        P = self.P
        P.op("gpsimd", lambda e: e.memset(self.onesf_r[:], 1.0), w=["onesf_r"])
        P.op("gpsimd", lambda e: e.memset(self.ones_c[:], 1.0), w=["ones_c"])
        P.op("gpsimd", lambda e: e.memset(self.onesb_c[:], 1.0), w=["onesb_c"])
        identf = self.inp("identf", [128, 128], F32)
        identb = self.inp("identb", [128, 128], BF16)
        P.dma("sync", lambda e: e.dma_start(out=self.idf[:], in_=identf), w=["idf"], stream="cst")
        P.dma("sync", lambda e: e.dma_start(out=self.idb[:], in_=identb), w=["idb"], stream="cst")

    def inp(self, name, shape, dtype):
        if name in self.wiring:
            return self.wiring[name]
        if name not in self._dram:
            self._dram[name] = self.nc.dram_tensor(name, shape, dtype, kind="ExternalInput").ap()
        return self._dram[name]

    def out(self, name, shape, dtype):
        if name not in self._dram:
            kind = "ExternalOutput" if (not self.fused or name in self.final_outs) else "Internal"
            self._dram[name] = self.nc.dram_tensor(name, shape, dtype, kind=kind).ap()
        return self._dram[name]

    def scratch(self, name, shape, dtype):
        if name not in self._dram:
            self._dram[name] = self.nc.dram_tensor(name, shape, dtype, kind="Internal").ap()
        return self._dram[name]

    def finish(self):
        self.P.wait_all("sync")
        self.P.build()


def _standalone(emit_fn, **kw):
    nc = bass.Bass("TRN2", target_bir_lowering=False)
    ctx = ExitStack()
    with ctx:
        E = Env(nc, ctx, fused=False)
        emit_fn(E, **kw)
        E.finish()
    return nc


def _wload(P, w_sb, key, deps, hbm_ap, nchunk, width, stream):
    P.dma("gpsimd", lambda e: e.dma_start(
        out=w_sb.rearrange("p (cc f) -> p cc f", cc=nchunk),
        in_=hbm_ap.rearrange("(cc p) f -> p cc f", p=128)),
        w=[key], deps=deps, stream=stream)


def emit_a(E, S=4096):
    So = S // 2
    nTT = S // 128
    nTTo = So // 128
    nTB = S // 512
    nTBo = So // 512
    nKC = S // 512
    nKC128 = S // 128
    nQT = So // 128

    nc, P, AB, AF_ = E.nc, E.P, E.AB, E.AF
    AB.begin_phase()
    AF_.begin_phase()
    pb, pt = E.pb, E.pt
    pav, pmisc = E.px[0], E.px[1]
    idf, idb = E.idf, E.idb
    onesf_r, ones_c = E.onesf_r, E.ones_c
    x_tm = E.inp("x_tm", [S, HID], F32)
    wqdT = E.inp("wqdT", [HID, QC], F32)
    wqnT = E.inp("wqnT", [QC, NH * NOPE], F32)
    wqrT = E.inp("wqrT", [QC, NH * ROPE], F32)
    wkvdT = E.inp("wkvdT", [HID, KVC], F32)
    wknT = E.inp("wknT", [KVC, NH * NOPE], F32)
    wvuT = E.inp("wvuT", [KVC, NH * VD], F32)
    wkrT = E.inp("wkrT", [HID, ROPE], F32)
    woT = E.inp("woT", [NH * VD, HID], F32)
    qnw_r = E.inp("qnw_r", [nQC, 128], F32)
    kvnw_r = E.inp("kvnw_r", [nKV, 128], F32)
    cosq2 = E.inp("cosq2", [128, So], BF16)
    sinq2 = E.inp("sinq2", [128, So], BF16)
    cosk = E.inp("cosk", [ROPE, S], BF16)
    sink = E.inp("sink", [ROPE, S], BF16)
    rotm = E.inp("rotm", [128, 128], BF16)
    xn_out = E.out("xn_out", [So, HID], BF16)
    attn_outT = E.out("attn_outT", [HID, So], F32)
    xn_all = E.scratch("xn_all", [S, HID], BF16)
    ckv_a = E.scratch("ckv_a", [KVC, S], BF16)
    krope_a = E.scratch("krope_a", [ROPE, S], BF16)
    kn_a = E.scratch("kn_a", [NH * NOPE, S], BF16)
    v_tm = E.scratch("v_tm", [S, NH * VD], BF16)
    cq_a = E.scratch("cq_a", [QC, So], BF16)
    qn_a = E.scratch("qn_a", [NH * NOPE, So], BF16)
    qr_a = E.scratch("qr_a", [NH * ROPE, So], BF16)
    attf = E.scratch("attf", [NH * VD, So], BF16)
    if True:
        xnf, xnfk = [], []
        for i in range(2):
            s_, k_, d_ = AB.alloc(f"xnf{i}", nCC * 512)
            xnf.append(s_)
            xnfk.append(k_)
            AB.pin(k_)
        rotm_s, _, _ = AB.alloc("rotm_s", 128, aliases=["rotm"], at=AB.n - 128)
        AB.pin("ab:rotm_s")
        qnw_c, _, _ = AF_.alloc("qnw_c", nQC, aliases=["qnw_c"])
        AF_.pin("af:qnw_c")
        kvnw_c, _, _ = AF_.alloc("kvnw_c", nKV, aliases=["kvnw_c"])
        AF_.pin("af:kvnw_c")
        cosq_s, _, _ = AB.alloc("cosq_s", So, aliases=["cosq"], at=AB.n - 128 - So)
        AB.pin("ab:cosq_s")
        sinq_s, _, _ = AB.alloc("sinq_s", So, aliases=["sinq"], at=AB.n - 128 - 2 * So)
        AB.pin("ab:sinq_s")
        cosk_s, cosk_k, _ = AB.alloc("cosk_s", S, parts=ROPE, aliases=["cosk"])
        AB.pin("ab:cosk_s")
        sink_s, sink_k, _ = AB.alloc("sink_s", S, parts=ROPE, aliases=["sink"])
        AB.pin("ab:sink_s")
        stat, _, _ = AF_.alloc("stat", 16, aliases=["st0", "st1", "st2", "st4", "st5", "st6", "st7", "st8"])
        AF_.pin("af:stat")
        rrow, _, _ = AF_.alloc("rrow", 512, parts=1, aliases=["rrow"])
        AF_.pin("af:rrow")
        rbc, _, _ = AF_.alloc("rbc", 512, aliases=["rbc"])
        AF_.pin("af:rbc")
        mrow, _, _ = AF_.alloc("mrow", 4, aliases=["mr", "nmax0", "nmax1"])
        AF_.pin("af:mrow")
        rsum, _, _ = AF_.alloc("rsum", 4, aliases=["rsum0", "rsum1", "rrec0", "rrec1"])
        AF_.pin("af:rsum")
        for name, dst, k in [("cosq2", cosq_s, "cosq"), ("sinq2", sinq_s, "sinq"),
                             ("cosk", cosk_s, "cosk"), ("sink", sink_s, "sink"),
                             ("rotm", rotm_s, "rotm")]:
            src = {"cosq2": cosq2, "sinq2": sinq2, "cosk": cosk, "sink": sink,
                   "rotm": rotm}[name]
            P.dma("sync", lambda e, dst=dst, src=src: e.dma_start(out=dst[0:src.tensor.shape[0], :], in_=src),
                  w=[k], stream="cst")
        nwr, _, _ = AF_.alloc("nwr", 128, parts=nQC, aliases=["nwr"])
        AF_.pin("af:nwr")
        P.dma("sync", lambda e: e.dma_start(out=nwr[0:nQC, :], in_=qnw_r), w=["nwr"], stream="cst")
        P.op("tensor", lambda t: t.transpose(pmisc[:, 0:nQC], nwr[0:nQC, :], idf[0:nQC, 0:nQC]),
             r=["nwr", "idf"], w=["px1"])
        P.op("vector", lambda v: v.tensor_copy(qnw_c[:], pmisc[:, 0:nQC]),
             r=["px1"], w=["qnw_c"])
        P.dma("sync", lambda e: e.dma_start(out=nwr[0:nKV, :], in_=kvnw_r),
              r=["nwr"], w=["nwr"], stream="cst")
        P.op("tensor", lambda t: t.transpose(pmisc[:, 0:nKV], nwr[0:nKV, :], idf[0:nKV, 0:nKV]),
             r=["nwr", "idf", "qnw_c"], w=["px1"])
        P.op("vector", lambda v: v.tensor_copy(kvnw_c[:], pmisc[:, 0:nKV]),
             r=["px1"], w=["kvnw_c"])

        xa, xak, dp = zip(*[AF_.alloc(f"xa{i}", HID) for i in range(2)])
        assert not dp[0] and not dp[1]
        dumpt, dumpk, _ = AF_.alloc("dump", HID)
        xnt, xntk = [], []
        for i in range(2):
            s_, k_, d_ = AB.alloc(f"xnt{i}", HID)
            xnt.append(s_)
            xntk.append(k_)
        AB.pin("ab:xnt0"), AB.pin("ab:xnt1")
        for tt in range(nTT):
            bi = tt % 2
            P.dma("sync", lambda e, tt=tt, bi=bi: e.dma_start(
                out=xa[bi][:], in_=x_tm[tt * 128:(tt + 1) * 128, :]),
                w=[xak[bi]], stream=f"x{bi}")
            P.op("scalar", lambda s, bi=bi: s.activation(
                dumpt[:], xa[bi][:], AF.Square, accum_out=stat[:, 0:1]),
                r=[xak[bi]], w=[dumpk, "st0"])
            P.op("vector", lambda v, bi=bi: v.reduce_sum(stat[:, 1:2], xa[bi][:], axis=AX),
                 r=[xak[bi]], w=["st1"])
            P.op("vector", lambda v: v.tensor_scalar(stat[:, 2:3], stat[:, 1:2], 1.0 / HID, None, ALU.mult),
                 r=["st1"], w=["st2"])
            P.op("vector", lambda v: v.tensor_tensor(stat[:, 4:5], stat[:, 2:3], stat[:, 2:3], ALU.mult),
                 r=["st2"], w=["st4"])
            P.op("vector", lambda v: v.scalar_tensor_tensor(
                stat[:, 5:6], stat[:, 0:1], 1.0 / HID, stat[:, 4:5], ALU.mult, ALU.subtract),
                r=["st0", "st4"], w=["st5"])
            P.op("vector", lambda v: v.tensor_scalar(stat[:, 8:9], stat[:, 5:6], LN_EPS, None, ALU.add),
                 r=["st5"], w=["st8"])
            P.op("scalar", lambda s: s.activation(stat[:, 6:7], stat[:, 8:9], AF.Sqrt),
                 r=["st8"], w=["st6"])
            P.op("vector", lambda v: v.reciprocal(stat[:, 7:8], stat[:, 6:7]),
                 r=["st6"], w=["st7"])
            P.op("vector", lambda v, bi=bi: v.tensor_scalar(
                xnt[bi][:], xa[bi][:], stat[:, 2:3], stat[:, 7:8], ALU.subtract, ALU.mult),
                r=[xak[bi], "st2", "st7"], w=[xntk[bi]])
            P.dma("gpsimd", lambda e, tt=tt, bi=bi: e.dma_start(
                out=xn_all[tt * 128:(tt + 1) * 128, :], in_=xnt[bi][:]),
                r=[xntk[bi]], w=[f"xn:{tt}"], stream=f"st{bi}")
            if tt < nTTo:
                P.dma("gpsimd", lambda e, tt=tt, bi=bi: e.dma_start(
                    out=xn_out[tt * 128:(tt + 1) * 128, :], in_=xnt[bi][:]),
                    r=[xntk[bi]], stream=f"st{bi}")
        AB.unpin("ab:xnt0"), AB.unpin("ab:xnt1")
        xn_tok = [f"xn:{t}" for t in range(nTT)]


        def load_xnf(tb, sl, extra_deps=()):
            ks = []
            for cc in range(nCC):
                t = P.dma("scalar", lambda e, tb=tb, cc=cc, sl=sl: e.dma_start(
                    out=xnf[sl][:, cc * 512:(cc + 1) * 512],
                    in_=xn_all[tb * 512:(tb + 1) * 512, cc * 128:(cc + 1) * 128],
                    transpose=True),
                    r=[xn_tok[tb * 4 + j] for j in range(4)],
                    w=[f"{xnfk[sl]}:{cc}"], stream=f"xnf{sl}", deps=extra_deps)
                ks.append(f"{xnfk[sl]}:{cc}")
            return ks

        wkvd_s, wkvd_k, dp = AB.alloc("wkvd", nCC * KVC)
        _wload(P, wkvd_s, wkvd_k, dp, wkvdT, nCC, KVC, "w0")
        wkr_s, wkr_k, dp = AB.alloc("wkr", nCC * ROPE)
        _wload(P, wkr_s, wkr_k, dp, wkrT, nCC, ROPE, "w1")
        ckvb, ckvbk, _ = AB.alloc("ckvb", nKV * 512)
        AB.pin("ab:ckvb")
        krb, krbk, _ = AB.alloc("krb", 512)
        AB.pin("ab:krb")
        krsb, krsbk, _ = AB.alloc("krsb", 512)
        AB.pin("ab:krsb")
        evt, evk = [], []
        for i in range(4):
            s_, k_, _ = AF_.alloc(f"ev{i}", 512)
            evt.append(s_)
            evk.append(k_)
        sqt, sqk, _ = AF_.alloc("sq", 512)
        krt, krtk, _ = AF_.alloc("krt", 512)

        def rms_norm_store(tb, n_f, inv_n, gcol, gkey, out_hbm, out_pref, psum_keys, st):
            for fc in range(n_f):
                P.op("scalar", lambda s, fc=fc: s.activation(evt[fc][:], pb[fc][:], AF.Square),
                     r=[psum_keys[fc]], w=[evk[fc]])
                P.op("tensor", lambda t, fc=fc: t.matmul(
                    pmisc[0:1, :], ones_c[:], evt[fc][:],
                    start=(fc == 0), stop=(fc == n_f - 1)),
                    r=[evk[fc], "ones_c"], w=["px1"], inc=(fc == n_f - 1))
            P.op("vector", lambda v: v.tensor_scalar(rrow[:], pmisc[0:1, :], inv_n, RMS_EPS, ALU.mult, ALU.add),
                 r=["px1"], w=["rrow"])
            P.op("scalar", lambda s: s.activation(rrow[:], rrow[:], AF.Sqrt),
                 r=["rrow"], w=["rrow"])
            P.op("vector", lambda v: v.reciprocal(rrow[:], rrow[:]), r=["rrow"], w=["rrow"])
            P.op("tensor", lambda t: t.matmul(pmisc[:, :], onesf_r[:], rrow[:],
                                              start=True, stop=True),
                 r=["rrow", "onesf_r", "px1"], w=["px1"])
            P.op("vector", lambda v: v.tensor_copy(rbc[:], pmisc[:, :]), r=["px1"], w=["rbc"])
            outks = []
            for fc in range(n_f):
                P.op("vector", lambda v, fc=fc: v.scalar_tensor_tensor(
                    ckvb[:, fc * 512:(fc + 1) * 512], pb[fc][:], gcol[:, fc:fc + 1],
                    rbc[:], ALU.mult, ALU.mult),
                    r=[psum_keys[fc], gkey, "rbc"], w=[f"{ckvbk}:{fc}"])
                ok = f"{out_pref}:{tb}:{fc}"
                P.dma("gpsimd", lambda e, fc=fc, tb=tb: e.dma_start(
                    out=out_hbm[fc * 128:(fc + 1) * 128, tb * 512:(tb + 1) * 512],
                    in_=ckvb[:, fc * 512:(fc + 1) * 512]),
                    r=[f"{ckvbk}:{fc}"], w=[ok], stream=f"st{fc % 2}")
                outks.append(ok)
            return outks

        for tb in range(nTB):
            sl = tb % 2
            xk = load_xnf(tb, sl)
            for fc in range(nKV):
                for cc in range(nCC):
                    P.op("tensor", lambda t, fc=fc, cc=cc, sl=sl: t.matmul(
                        pb[fc][:], wkvd_s[:, cc * KVC + fc * 128: cc * KVC + fc * 128 + 128],
                        xnf[sl][:, cc * 512:(cc + 1) * 512],
                        start=(cc == 0), stop=(cc == nCC - 1)),
                        r=[wkvd_k, xk[cc]], w=[f"pb{fc}"], inc=(cc == nCC - 1))
            rms_norm_store(tb, nKV, 1.0 / KVC, kvnw_c, "kvnw_c", ckv_a, "ckv",
                           [f"pb{i}" for i in range(nKV)], "st")
            for cc in range(nCC):
                P.op("tensor", lambda t, cc=cc, sl=sl: t.matmul(
                    pav[0:ROPE, :], wkr_s[:, cc * ROPE:(cc + 1) * ROPE],
                    xnf[sl][:, cc * 512:(cc + 1) * 512],
                    start=(cc == 0), stop=(cc == nCC - 1)),
                    r=[wkr_k, xk[cc]], w=["px0"], inc=(cc == nCC - 1))
            P.op("vector", lambda v: v.tensor_copy(krsb[0:64, :], pav[0:ROPE, :]),
                 r=["px0"], w=[f"{krsbk}"])
            P.op("tensor", lambda t: t.matmul(pmisc[0:64, :], rotm_s[0:64, 0:64],
                                              krsb[0:64, :], start=True, stop=True),
                 r=[krsbk, "rotm"], w=["px1"])
            P.op("vector", lambda v, tb=tb: v.tensor_tensor(
                krt[0:64, :], krsb[0:64, :], cosk_s[:, tb * 512:(tb + 1) * 512], ALU.mult),
                r=[krsbk, "cosk"], w=[krtk])
            P.op("vector", lambda v, tb=tb: v.tensor_tensor(
                krb[0:64, :], pmisc[0:64, :], sink_s[:, tb * 512:(tb + 1) * 512], ALU.mult),
                r=["px1", "sink"], w=[krbk])
            P.op("vector", lambda v: v.tensor_tensor(
                krb[0:64, :], krb[0:64, :], krt[0:64, :], ALU.add),
                r=[krbk, krtk], w=[krbk])
            P.dma("gpsimd", lambda e, tb=tb: e.dma_start(
                out=krope_a[:, tb * 512:(tb + 1) * 512], in_=krb[0:64, :]),
                r=[krbk], w=[f"kro:{tb}"], stream="st2")

        AB.unpin("ab:ckvb")
        AB.unpin("ab:krb")
        AB.unpin("ab:krsb")
        AB.unpin("ab:cosk_s")
        AB.unpin("ab:sink_s")
        AB.begin_phase()
        AF_.begin_phase()
        wkn_s, wkn_k, dp = AB.alloc("wkn", nKV * NH * NOPE)
        _wload(P, wkn_s, wkn_k, dp, wknT, nKV, NH * NOPE, "w0")
        wvu_s, wvu_k, dp = AB.alloc("wvu", nKV * NH * VD)
        _wload(P, wvu_s, wvu_k, dp, wvuT, nKV, NH * VD, "w1")
        ckvf, ckvfk = [], []
        for i in range(2):
            s_, k_, d_ = AB.alloc(f"ckvf{i}", nKV * 512)
            ckvf.append(s_)
            ckvfk.append(k_)
            assert not d_ or True
        knb, knbk, _ = AB.alloc("knb", 512)
        vtt, vttk, _ = AB.alloc("vtt", NH * VD)
        for tb in range(nTB):
            sl = tb % 2
            P.dma("sync", lambda e, tb=tb, sl=sl: e.dma_start(
                out=ckvf[sl].rearrange("p (cc t) -> p cc t", cc=nKV),
                in_=ckv_a[:, tb * 512:(tb + 1) * 512].rearrange("(cc p) t -> p cc t", p=128)),
                r=[f"ckv:{tb}:{fc}" for fc in range(nKV)], w=[ckvfk[sl]], stream=f"cf{sl}")
            for h in range(NH):
                pk = f"pb{h % 2}"
                for cc in range(nKV):
                    P.op("tensor", lambda t, h=h, cc=cc, sl=sl: t.matmul(
                        pb[h % 2][:], wkn_s[:, cc * NH * NOPE + h * 128: cc * NH * NOPE + h * 128 + 128],
                        ckvf[sl][:, cc * 512:(cc + 1) * 512],
                        start=(cc == 0), stop=(cc == nKV - 1)),
                        r=[wkn_k, ckvfk[sl]], w=[pk], inc=(cc == nKV - 1))
                P.op("vector", lambda v, h=h: v.tensor_copy(knb[:], pb[h % 2][:]),
                     r=[pk], w=[knbk])
                P.dma("gpsimd", lambda e, h=h, tb=tb: e.dma_start(
                    out=kn_a[h * 128:(h + 1) * 128, tb * 512:(tb + 1) * 512], in_=knb[:]),
                    r=[knbk], w=[f"kn:{tb}:{h}"], stream="st0")
            for t4 in range(4):
                for h in range(NH):
                    pk = f"pb{2 + h % 2}"
                    for cc in range(nKV):
                        P.op("tensor", lambda t, h=h, cc=cc, t4=t4, sl=sl: t.matmul(
                            pb[2 + h % 2][:, 0:128],
                            ckvf[sl][:, cc * 512 + t4 * 128: cc * 512 + t4 * 128 + 128],
                            wvu_s[:, cc * NH * VD + h * 128: cc * NH * VD + h * 128 + 128],
                            start=(cc == 0), stop=(cc == nKV - 1)),
                            r=[wvu_k, ckvfk[sl]], w=[pk], inc=(cc == nKV - 1))
                    P.op("vector", lambda v, h=h: v.tensor_copy(
                        vtt[:, h * 128:(h + 1) * 128], pb[2 + h % 2][:, 0:128]),
                        r=[pk], w=[f"{vttk}:{h}"])
                P.dma("gpsimd", lambda e, tb=tb, t4=t4: e.dma_start(
                    out=v_tm[tb * 512 + t4 * 128: tb * 512 + (t4 + 1) * 128, :],
                    in_=vtt[:]),
                    r=[f"{vttk}:{h}" for h in range(NH)], w=[f"v:{tb}:{t4}"], stream="st1")

        AB.begin_phase()
        AF_.begin_phase()
        wqd_s, wqd_k, dp = AB.alloc("wqd", nCC * QC)
        _wload(P, wqd_s, wqd_k, dp, wqdT, nCC, QC, "w0")
        cqr, cqrk, _ = AF_.alloc("cqr", nQC * 512)
        cqb, cqbk, _ = AB.alloc("cqb", 512)
        for tb in range(nTBo):
            sl = tb % 2
            xk = load_xnf(tb, sl)
            for fc in range(nQC):
                pk = f"pb{fc % 4}"
                for cc in range(nCC):
                    P.op("tensor", lambda t, fc=fc, cc=cc, sl=sl: t.matmul(
                        pb[fc % 4][:], wqd_s[:, cc * QC + fc * 128: cc * QC + fc * 128 + 128],
                        xnf[sl][:, cc * 512:(cc + 1) * 512],
                        start=(cc == 0), stop=(cc == nCC - 1)),
                        r=[wqd_k, xk[cc]], w=[pk], inc=(cc == nCC - 1))
                P.op("vector", lambda v, fc=fc: v.tensor_copy(
                    cqr[:, fc * 512:(fc + 1) * 512], pb[fc % 4][:]),
                    r=[pk], w=[f"{cqrk}:{fc}"])
                P.op("scalar", lambda s, fc=fc: s.activation(
                    sqt[:], cqr[:, fc * 512:(fc + 1) * 512], AF.Square),
                    r=[f"{cqrk}:{fc}"], w=[sqk])
                P.op("tensor", lambda t, fc=fc: t.matmul(
                    pmisc[0:1, :], ones_c[:], sqt[:],
                    start=(fc == 0), stop=(fc == nQC - 1)),
                    r=[sqk, "ones_c"], w=["px1"], inc=(fc == nQC - 1))
            P.op("vector", lambda v: v.tensor_scalar(rrow[:], pmisc[0:1, :], 1.0 / QC, RMS_EPS, ALU.mult, ALU.add),
                 r=["px1"], w=["rrow"])
            P.op("scalar", lambda s: s.activation(rrow[:], rrow[:], AF.Sqrt),
                 r=["rrow"], w=["rrow"])
            P.op("vector", lambda v: v.reciprocal(rrow[:], rrow[:]), r=["rrow"], w=["rrow"])
            P.op("tensor", lambda t: t.matmul(pmisc[:, :], onesf_r[:], rrow[:],
                                              start=True, stop=True),
                 r=["rrow", "onesf_r", "px1"], w=["px1"])
            P.op("vector", lambda v: v.tensor_copy(rbc[:], pmisc[:, :]), r=["px1"], w=["rbc"])
            for fc in range(nQC):
                P.op("vector", lambda v, fc=fc: v.scalar_tensor_tensor(
                    cqb[:], cqr[:, fc * 512:(fc + 1) * 512], qnw_c[:, fc:fc + 1],
                    rbc[:], ALU.mult, ALU.mult),
                    r=[f"{cqrk}:{fc}", "qnw_c", "rbc"], w=[cqbk])
                P.dma("gpsimd", lambda e, fc=fc, tb=tb: e.dma_start(
                    out=cq_a[fc * 128:(fc + 1) * 128, tb * 512:(tb + 1) * 512], in_=cqb[:]),
                    r=[cqbk], w=[f"cq:{tb}:{fc}"], stream="st0")

        AB.unpin(xnfk[0])
        AB.unpin(xnfk[1])
        AB.begin_phase()
        AF_.begin_phase()
        wqn_s, wqn_k, dp = AB.alloc("wqn", nQC * NH * NOPE)
        _wload(P, wqn_s, wqn_k, dp, wqnT, nQC, NH * NOPE, "w0")
        wqr_s, wqr_k, dp = AB.alloc("wqr", nQC * NH * ROPE)
        _wload(P, wqr_s, wqr_k, dp, wqrT, nQC, NH * ROPE, "w1")
        cqf, cqfk = [], []
        for i in range(1):
            s_, k_, d_ = AB.alloc(f"cqf{i}", nQC * 512)
            cqf.append(s_)
            cqfk.append(k_)
            if d_:
                pass
            AB.pin(k_)
        qnb, qnbk, _ = AB.alloc("qnb", 512)
        krs, krsk, _ = AB.alloc("krs", 512)
        qrbt, qrbk, _ = AB.alloc("qrb", 512)
        qrt, qrtk, _ = AF_.alloc("qrt", 512)
        for tb in range(nTBo):
            sl = 0
            P.dma("sync", lambda e, tb=tb, sl=sl: e.dma_start(
                out=cqf[sl].rearrange("p (cc t) -> p cc t", cc=nQC),
                in_=cq_a[:, tb * 512:(tb + 1) * 512].rearrange("(cc p) t -> p cc t", p=128)),
                r=[f"cq:{tb}:{fc}" for fc in range(nQC)], w=[cqfk[sl]], stream=f"qf{sl}")
            for h in range(NH):
                pk = f"pb{h % 2}"
                for cc in range(nQC):
                    P.op("tensor", lambda t, h=h, cc=cc, sl=sl: t.matmul(
                        pb[h % 2][:], wqn_s[:, cc * NH * NOPE + h * 128: cc * NH * NOPE + h * 128 + 128],
                        cqf[sl][:, cc * 512:(cc + 1) * 512],
                        start=(cc == 0), stop=(cc == nQC - 1)),
                        r=[wqn_k, cqfk[sl]], w=[pk], inc=(cc == nQC - 1))
                P.op("vector", lambda v, h=h: v.tensor_copy(qnb[:], pb[h % 2][:]),
                     r=[pk], w=[qnbk])
                P.dma("gpsimd", lambda e, h=h, tb=tb: e.dma_start(
                    out=qn_a[h * 128:(h + 1) * 128, tb * 512:(tb + 1) * 512], in_=qnb[:]),
                    r=[qnbk], w=[f"qn:{tb}:{h}"], stream="st0")
            for hc in range(NH * ROPE // 128):
                for cc in range(nQC):
                    P.op("tensor", lambda t, hc=hc, cc=cc, sl=sl: t.matmul(
                        pb[2][:], wqr_s[:, cc * NH * ROPE + hc * 128: cc * NH * ROPE + hc * 128 + 128],
                        cqf[sl][:, cc * 512:(cc + 1) * 512],
                        start=(cc == 0), stop=(cc == nQC - 1)),
                        r=[wqr_k, cqfk[sl]], w=["pb2"], inc=(cc == nQC - 1))
                P.op("vector", lambda v: v.tensor_copy(krs[:], pb[2][:]), r=["pb2"], w=[krsk])
                P.op("tensor", lambda t: t.matmul(pav[:, 0:512], rotm_s[:], krs[:],
                                                  start=True, stop=True),
                     r=[krsk, "rotm"], w=["px0"])
                P.op("vector", lambda v, tb=tb: v.tensor_tensor(
                    qrt[:], krs[:], cosq_s[:, tb * 512:(tb + 1) * 512], ALU.mult),
                    r=[krsk, "cosq"], w=[qrtk])
                P.op("vector", lambda v, tb=tb: v.tensor_tensor(
                    qrbt[:], pav[:, 0:512], sinq_s[:, tb * 512:(tb + 1) * 512], ALU.mult),
                    r=["px0", "sinq"], w=[qrbk])
                P.op("vector", lambda v: v.tensor_tensor(
                    qrbt[:], qrbt[:], qrt[:], ALU.add),
                    r=[qrbk, qrtk], w=[qrbk])
                P.dma("gpsimd", lambda e, hc=hc, tb=tb: e.dma_start(
                    out=qr_a[hc * 128:(hc + 1) * 128, tb * 512:(tb + 1) * 512], in_=qrbt[:]),
                    r=[qrbk], w=[f"qr:{tb}:{hc}"], stream="st1")
        AB.unpin(cqfk[0])

        AB.unpin("ab:cosq_s")
        AB.unpin("ab:sinq_s")
        AB.unpin("ab:rotm_s")
        AB.begin_phase()
        AF_.begin_phase()
        kro_s, _, _ = AB.alloc("kro_s", S, parts=ROPE, aliases=["kro_s"])
        AB.pin("ab:kro_s")
        P.dma("sync", lambda e: e.dma_start(out=kro_s[0:ROPE, :], in_=krope_a),
              r=[f"kro:{tb}" for tb in range(nTB)], w=["kro_s"], stream="cst")
        ksb, ksbk = [], []
        vsb, vsbk = [], []
        qnh, qnhk = [], []
        qrh, qrhk = [], []
        pbf, pbfk = [], []
        ptsb, ptsbk = [], []
        for i in range(2):
            s_, k_, d_ = AB.alloc(f"ksb{i}", S)
            ksb.append(s_), ksbk.append(k_)
            s_, k_, d_ = AB.alloc(f"vsb{i}", S)
            vsb.append(s_), vsbk.append(k_)
            s_, k_, d_ = AB.alloc(f"qnh{i}", So)
            qnh.append(s_), qnhk.append(k_)
            s_, k_, d_ = AB.alloc(f"qrh{i}", So)
            qrh.append(s_), qrhk.append(k_)
            s_, k_, d_ = AB.alloc(f"pbf{i}", S)
            pbf.append(s_), pbfk.append(k_)
            s_, k_, d_ = AB.alloc(f"ptsb{i}", 1024)
            ptsb.append(s_), ptsbk.append(k_)
        avb, avbk, _ = AB.alloc("avb", 128)
        attb, attbk, _ = AB.alloc("attb", 128)
        scs, scsk = [], []
        for i in range(2):
            s_, k_, d_ = AF_.alloc(f"scs{i}", S)
            scs.append(s_), scsk.append(k_)

        for h in range(NH):
            sl = h % 2
            P.dma("sync", lambda e, h=h, sl=sl: e.dma_start(
                out=ksb[sl][:], in_=kn_a[h * 128:(h + 1) * 128, :]),
                r=[f"kn:{tb}:{h}" for tb in range(nTB)], w=[ksbk[sl]], stream=f"k{sl}")
            P.dma("sync", lambda e, h=h, sl=sl: e.dma_start(
                out=vsb[sl].rearrange("p (kc d) -> p kc d", d=VD),
                in_=v_tm.rearrange("(kc p) d -> p kc d", p=128)[:, :, h * VD:(h + 1) * VD]),
                r=[f"v:{tb}:{t4}" for tb in range(nTB) for t4 in range(4)],
                w=[vsbk[sl]], stream=f"v{sl}")
            P.dma("sync", lambda e, h=h, sl=sl: e.dma_start(
                out=qnh[sl][:], in_=qn_a[h * 128:(h + 1) * 128, :]),
                r=[f"qn:{tb}:{h}" for tb in range(nTBo)], w=[qnhk[sl]], stream=f"qn{sl}")
            P.dma("sync", lambda e, h=h, sl=sl: e.dma_start(
                out=qrh[sl][0:ROPE, :],
                in_=qr_a.rearrange("(hc p) t -> hc p t", p=128)[h // 2, (h % 2) * 64:(h % 2) * 64 + 64, :]),
                r=[f"qr:{tb}:{h // 2}" for tb in range(nTBo)], w=[qrhk[sl]], stream=f"qr{sl}")
            def emit_scores(qt):
                qsl = qt % 2
                for kc in range(nKC):
                    pk = f"pb{kc % 4}"
                    P.op("tensor", lambda t, kc=kc, qt=qt, sl=sl: t.matmul(
                        pb[kc % 4][:], qnh[sl][:, qt * 128:(qt + 1) * 128],
                        ksb[sl][:, kc * 512:(kc + 1) * 512], start=True, stop=False),
                        r=[qnhk[sl], ksbk[sl]], w=[pk], inc=False)
                    P.op("tensor", lambda t, kc=kc, qt=qt, sl=sl: t.matmul(
                        pb[kc % 4][:], qrh[sl][0:ROPE, qt * 128:(qt + 1) * 128],
                        kro_s[0:ROPE, kc * 512:(kc + 1) * 512], start=False, stop=True),
                        r=[qrhk[sl], "kro_s"], w=[pk])
                    m_prev = mrow[:, (kc - 1) % 2: (kc - 1) % 2 + 1] if kc > 0 else None
                    m_cur = mrow[:, kc % 2: kc % 2 + 1]
                    P.op("vector", lambda v, kc=kc, qsl=qsl, m_prev=m_prev, m_cur=m_cur: v.tensor_scalar(
                        scs[qsl][:, kc * 512:(kc + 1) * 512], pb[kc % 4][:],
                        SM_SCALE, m_prev, ALU.mult, ALU.max, accum_out=m_cur),
                        r=[pk] + (["mr"] if kc > 0 else []), w=[f"{scsk[qsl]}:{kc}", "mr"])
                P.op("vector", lambda v, qsl=qsl: v.tensor_scalar(
                    mrow[:, 2 + qsl:3 + qsl], mrow[:, (nKC - 1) % 2:(nKC - 1) % 2 + 1], -1.0, None, ALU.mult),
                    r=["mr"], w=[f"nmax{qsl}"])
                P.op("scalar", lambda s, qsl=qsl: s.activation(
                    pbf[qsl][:], scs[qsl][:], AF.Exp, bias=mrow[:, 2 + qsl:3 + qsl],
                    accum_out=rsum[:, qsl:qsl + 1]),
                    r=[f"{scsk[qsl]}:{kc}" for kc in range(nKC)] + [f"nmax{qsl}"],
                    w=[pbfk[qsl], f"rsum{qsl}"])
                P.op("vector", lambda v, qsl=qsl: v.reciprocal(rsum[:, 2 + qsl:3 + qsl], rsum[:, qsl:qsl + 1]),
                     r=[f"rsum{qsl}"], w=[f"rrec{qsl}"])

            def emit_pv(qt, h):
                qsl = qt % 2
                for kc in range(nKC128):
                    g = (kc // 8) % 2
                    P.op("tensor", lambda t, kc=kc, qsl=qsl, g=g: t.transpose(
                        pt[g][:, (kc % 8) * 128:(kc % 8 + 1) * 128],
                        pbf[qsl][:, kc * 128:(kc + 1) * 128], idb[:]),
                        r=[pbfk[qsl], "idb", f"ptsb_done{g}"], w=[f"pt{g}:{kc % 8}"])
                    if kc % 8 == 7:
                        P.op("vector", lambda v, g=g: v.tensor_copy(ptsb[g][:], pt[g][:]),
                             r=[f"pt{g}:{j}" for j in range(8)], w=[ptsbk[g], f"ptsb_done{g}"])
                        for j in range(8):
                            kcj = (kc // 8) * 8 + j
                            P.op("tensor", lambda t, kcj=kcj, j=j, g=g, sl=sl: t.matmul(
                                pav[:, 0:128], ptsb[g][:, j * 128:(j + 1) * 128],
                                vsb[sl][:, kcj * 128:(kcj + 1) * 128],
                                start=(kcj == 0), stop=(kcj == nKC128 - 1)),
                                r=[ptsbk[g], vsbk[sl]], w=["px0"], inc=(kcj == nKC128 - 1))
                P.op("vector", lambda v, qsl=qsl: v.tensor_scalar(
                    avb[:], pav[:, 0:128], rsum[:, 2 + qsl:3 + qsl], None, ALU.mult),
                    r=["px0", f"rrec{qsl}"], w=[avbk])
                P.op("tensor", lambda t: t.transpose(pt[0][:, 0:128], avb[:], idb[:]),
                     r=[avbk, "idb"], w=["pt0:0"])
                P.op("vector", lambda v: v.tensor_copy(attb[:], pt[0][:, 0:128]),
                     r=["pt0:0"], w=[attbk])
                P.dma("gpsimd", lambda e, h=h, qt=qt: e.dma_start(
                    out=attf[h * 128:(h + 1) * 128, qt * 128:(qt + 1) * 128], in_=attb[:]),
                    r=[attbk], w=[f"att:{h}:{qt}"], stream="st0")

            for qt in range(nQT + 1):
                if qt < nQT:
                    emit_scores(qt)
                if qt > 0:
                    emit_pv(qt - 1, h)
        AB.unpin("ab:kro_s")
        AB.begin_phase()
        AF_.begin_phase()
        wo_s, wo_k, dp = AB.alloc("wo", nCC * HID)
        _wload(P, wo_s, wo_k, dp, woT, nCC, HID, "w0")
        atf, atfk = [], []
        for i in range(1):
            s_, k_, d_ = AB.alloc(f"atf{i}", nCC * 512)
            atf.append(s_), atfk.append(k_)
        oev, oevk = [], []
        for i in range(2):
            s_, k_, d_ = AF_.alloc(f"oev{i}", 512)
            oev.append(s_), oevk.append(k_)
        for tb in range(nTBo):
            sl = 0
            P.dma("sync", lambda e, tb=tb, sl=sl: e.dma_start(
                out=atf[sl].rearrange("p (cc t) -> p cc t", cc=nCC),
                in_=attf[:, tb * 512:(tb + 1) * 512].rearrange("(cc p) t -> p cc t", p=128)),
                r=[f"att:{h}:{qt}" for h in range(NH) for qt in range(tb * 4, tb * 4 + 4)],
                w=[atfk[sl]], stream="at0")
            for fc in range(nCC):
                pk = f"pb{fc % 4}"
                for cc in range(nCC):
                    P.op("tensor", lambda t, fc=fc, cc=cc, sl=sl: t.matmul(
                        pb[fc % 4][:], wo_s[:, cc * HID + fc * 128: cc * HID + fc * 128 + 128],
                        atf[sl][:, cc * 512:(cc + 1) * 512],
                        start=(cc == 0), stop=(cc == nCC - 1)),
                        r=[wo_k, atfk[sl]], w=[pk], inc=(cc == nCC - 1))
                osl = fc % 2
                P.op("vector", lambda v, fc=fc, osl=osl: v.tensor_copy(oev[osl][:], pb[fc % 4][:]),
                     r=[pk], w=[oevk[osl]])
                P.dma("gpsimd", lambda e, fc=fc, tb=tb, osl=osl: e.dma_start(
                    out=attn_outT[fc * 128:(fc + 1) * 128, tb * 512:(tb + 1) * 512],
                    in_=oev[osl][:]),
                    r=[oevk[osl]], stream=f"o{osl}")




        for k in ["af:qnw_c", "af:kvnw_c", "af:stat", "af:rrow", "af:rbc",
                  "af:mrow", "af:rsum", "af:nwr"]:
            AF_.unpin(k)



def build_launch_a(S=4096):
    return _standalone(emit_a, S=S)


def stage_a(inputs, S=4096):
    So = S // 2
    x = np.asarray(inputs["L0_attn_x"])
    Bb = x.shape[0]
    tr = lambda w: np.ascontiguousarray(np.asarray(w).T)
    shared = dict(
        wqdT=tr(inputs["L0_attn_w_q_down"]),
        wqnT=tr(inputs["L0_attn_w_q_nope"]),
        wqrT=tr(inputs["L0_attn_w_q_rope"]),
        wkvdT=tr(inputs["L0_attn_w_kv_down"]),
        wknT=tr(inputs["L0_attn_w_k_nope"]),
        wvuT=tr(inputs["L0_attn_w_v_up"]),
        wkrT=tr(inputs["L0_attn_w_k_rope"]),
        woT=tr(inputs["L0_attn_w_o"]),
        qnw_r=np.ascontiguousarray(
            np.asarray(inputs["L0_attn_q_norm_w"]).reshape(nQC, 128)),
        kvnw_r=np.ascontiguousarray(
            np.asarray(inputs["L0_attn_kv_norm_w"]).reshape(nKV, 128)),
        identf=np.eye(128, dtype=np.float32),
        identb=np.eye(128, dtype=BF),
        rotm=_rot_mat(),
    )
    cos, sin = _rope_tables(S)
    in_maps = []
    for c in range(8):
        b, hh = c // 2, c % 2
        xb = x[b].reshape(2, So, HID)
        x_rot = np.ascontiguousarray(
            np.concatenate([xb[hh], xb[1 - hh]], axis=0))
        pos_own = hh * So + np.arange(So)
        pos_rot = np.concatenate([pos_own, (1 - hh) * So + np.arange(So)])
        cq2 = np.ascontiguousarray(np.tile(cos[pos_own].T, (2, 1))).astype(BF)
        sq2 = np.ascontiguousarray(np.tile(sin[pos_own].T, (2, 1))).astype(BF)
        ck = np.ascontiguousarray(cos[pos_rot].T).astype(BF)
        sk = np.ascontiguousarray(sin[pos_rot].T).astype(BF)
        m = dict(shared)
        m.update(x_tm=x_rot, cosq2=cq2, sinq2=sq2, cosk=ck, sink=sk)
        in_maps.append(m)
    return in_maps


def unstage_a(results, S=4096):
    So = S // 2
    attn = np.zeros((4, S, HID), np.float32)
    xn = np.zeros((4, S, HID), BF)
    for c in range(8):
        b, hh = c // 2, c % 2
        attn[b, hh * So:(hh + 1) * So] = results[c]["attn_outT"].T
        xn[b, hh * So:(hh + 1) * So] = results[c]["xn_out"]
    return attn, xn





NEXP = 8
FME = 1408
nFE = FME // 128
FSH = 2816
nFS = FSH // 128


def emit_b(E, T=2048):
    nTT = T // 128
    nTB = T // 512
    nTH = 2
    TH = T // nTH
    nSB = TH // 512
    nc, P, AB, AF_ = E.nc, E.P, E.AB, E.AF
    AB.begin_phase()
    AF_.begin_phase()
    pb = E.pb
    pg, pu = E.pb[2], E.pb[3]
    pmisc, pmisc2 = E.px[0], E.px[1]
    idf = E.idf
    onesf_r = E.onesf_r
    xn_tm = E.inp("xn_tm", [T, HID], BF16)
    x_fm = E.inp("x_fm", [HID, T], F32)
    attn_fm = E.inp("attn_fm", [HID, T], F32)
    gwT = E.inp("gwT", [HID, NEXP], F32)
    guT = E.inp("guT", [NEXP, HID, 2 * FME], F32)
    dwT = E.inp("dwT", [NEXP, FME, HID], F32)
    sguT = E.inp("sguT", [HID, 2 * FSH], F32)
    sdT = E.inp("sdT", [FSH, HID], F32)
    x1T = E.out("x1T", [HID, T], F32)
    x1_tm = E.out("x1_tm", [T, HID], F32)
    cmb_d = E.scratch("cmb_d", [NEXP, T], F32)
    if True:
        gw_s, _, _ = AB.alloc("gw_s", nCC * NEXP, aliases=["gw_s"])
        AB.pin("ab:gw_s")
        st8, _, _ = AF_.alloc("st8", 12, aliases=["s0", "s1", "s2", "s3", "s4", "s5", "s6", "s7", "s8"])
        AF_.pin("af:st8")
        crow, _, _ = AF_.alloc("crow", TH, parts=1, aliases=["crow"])
        AF_.pin("af:crow")
        cbc, _, _ = AF_.alloc("cbc", TH, aliases=["cbc"])
        AF_.pin("af:cbc")
        _wload(P, gw_s, "gw_s", (), gwT, nCC, NEXP, "w0")
        xnf, xnfk, _ = AB.alloc("xnfb", nCC * TH)
        AB.pin(xnfk)
        xnfks = [f"{xnfk}:{cc}" for cc in range(nCC)]
        Y, Yk, _ = AF_.alloc("Y", nCC * TH)
        AF_.pin(Yk)
        act, actk, _ = AB.alloc("act", nFE * TH)
        AB.pin(actk)
        lfm, lfmk, _ = AF_.alloc("lfm", TH)
        cmbf, cmbfk, _ = AF_.alloc("cmbf", TH)
        ltm, ltmk, _ = AF_.alloc("ltm", 16)
        ptm, ptmk, _ = AF_.alloc("ptm", 16)
        sgt, sgtk, _ = AF_.alloc("sgt", 512)
        t2t, t2k, _ = AF_.alloc("t2t", 512)
        xre, xrek, _ = AF_.alloc("xre", TH)
        xtmt, _, _ = AF_.alloc("xtmt", HID, aliases=["xtmt"])
        AF_.pin("af:xtmt")
        are, arek, _ = AF_.alloc("are", TH)
        gbuf, ubuf, dwbuf = [], [], []
        for i in range(2):
            s_, k_, _ = AB.alloc(f"gb{i}", nCC * 128)
            gbuf.append((s_, k_))
            s_, k_, _ = AB.alloc(f"ub{i}", nCC * 128)
            ubuf.append((s_, k_))
            s_, k_, _ = AB.alloc(f"dwb{i}", nFS * 128)
            dwbuf.append((s_, k_))

        def gate_up_down(th, guT_e, dwT_e, nF, has_cmb, first):
            npass = nF // nFE
            for pas in range(npass):
                for fi in range(nFE):
                    fc = pas * nFE + fi
                    gs, gk = gbuf[fi % 2]
                    us, uk = ubuf[fi % 2]
                    P.dma("gpsimd", lambda e_, fc=fc, gs=gs: e_.dma_start(
                        out=gs.rearrange("p (cc f) -> p cc f", cc=nCC),
                        in_=guT_e[:, fc * 128:(fc + 1) * 128].rearrange(
                            "(cc p) f -> p cc f", p=128)),
                        w=[gk], stream=f"g{fi % 2}")
                    P.dma("gpsimd", lambda e_, fc=fc, us=us: e_.dma_start(
                        out=us.rearrange("p (cc f) -> p cc f", cc=nCC),
                        in_=guT_e[:, (nF + fc) * 128:(nF + fc + 1) * 128].rearrange(
                            "(cc p) f -> p cc f", p=128)),
                        w=[uk], stream=f"u{fi % 2}")
                    for sbk in range(nSB):
                        for cc in range(nCC):
                            P.op("tensor", lambda t, cc=cc, sbk=sbk, gs=gs: t.matmul(
                                pg[:], gs[:, cc * 128:(cc + 1) * 128],
                                xnf[:, cc * TH + sbk * 512: cc * TH + sbk * 512 + 512],
                                start=(cc == 0), stop=(cc == nCC - 1)),
                                r=[gk, xnfks[cc]], w=["pb2"], inc=(cc == nCC - 1))
                        for cc in range(nCC):
                            P.op("tensor", lambda t, cc=cc, sbk=sbk, us=us: t.matmul(
                                pu[:], us[:, cc * 128:(cc + 1) * 128],
                                xnf[:, cc * TH + sbk * 512: cc * TH + sbk * 512 + 512],
                                start=(cc == 0), stop=(cc == nCC - 1)),
                                r=[uk, xnfks[cc]], w=["pb3"], inc=(cc == nCC - 1))
                        P.op("scalar", lambda s: s.activation(sgt[:], pg[:], AF.Silu),
                             r=["pb2"], w=[sgtk])
                        ao = act[:, fi * TH + sbk * 512: fi * TH + sbk * 512 + 512]
                        if has_cmb:
                            P.op("vector", lambda v: v.tensor_tensor(t2t[:], sgt[:], pu[:], ALU.mult),
                                 r=[sgtk, "pb3"], w=[t2k])
                            P.op("vector", lambda v, ao=ao, sbk=sbk: v.tensor_tensor(
                                ao, t2t[:], cbc[:, sbk * 512:(sbk + 1) * 512], ALU.mult),
                                r=[t2k, "cbc"], w=[f"{actk}:{fi}"])
                        else:
                            P.op("vector", lambda v, ao=ao: v.tensor_tensor(ao, sgt[:], pu[:], ALU.mult),
                                 r=[sgtk, "pb3"], w=[f"{actk}:{fi}"])
                for hc in range(nCC):
                    ds, dk = dwbuf[hc % 2]
                    P.dma("gpsimd", lambda e_, hc=hc, pas=pas, ds=ds: e_.dma_start(
                        out=ds[:, 0:nFE * 128].rearrange("p (fc f) -> p fc f", fc=nFE),
                        in_=dwT_e[pas * FME:(pas + 1) * FME, hc * 128:(hc + 1) * 128].rearrange(
                            "(fc p) f -> p fc f", p=128)),
                        w=[dk], stream=f"dw{hc % 2}")
                    for sbk in range(nSB):
                        pk = f"pb{sbk % 2}"
                        for fi in range(nFE):
                            P.op("tensor", lambda t, fi=fi, sbk=sbk, ds=ds: t.matmul(
                                pb[sbk % 2][:], ds[:, fi * 128:(fi + 1) * 128],
                                act[:, fi * TH + sbk * 512: fi * TH + sbk * 512 + 512],
                                start=(fi == 0), stop=(fi == nFE - 1)),
                                r=[dk, f"{actk}:{fi}"], w=[pk], inc=(fi == nFE - 1))
                        yo = Y[:, hc * TH + sbk * 512: hc * TH + sbk * 512 + 512]
                        yk2 = f"{Yk}:{hc}:{sbk}"
                        if first and pas == 0:
                            P.op("vector", lambda v, yo=yo, sbk=sbk: v.tensor_copy(yo, pb[sbk % 2][:]),
                                 r=[pk], w=[yk2])
                        else:
                            P.op("vector", lambda v, yo=yo, sbk=sbk: v.tensor_tensor(
                                yo, yo, pb[sbk % 2][:], ALU.add),
                                r=[pk, yk2], w=[yk2])

        for th in range(nTH):
            for cc in range(nCC):
                P.dma("scalar", lambda e, cc=cc, th=th: e.dma_start(
                    out=xnf[:, cc * TH:(cc + 1) * TH],
                    in_=xn_tm[th * TH:(th + 1) * TH, cc * 128:(cc + 1) * 128],
                    transpose=True),
                    w=[xnfks[cc]], stream=f"xnf{cc % 2}")
            for tb in range(nSB):
                for cc in range(nCC):
                    P.op("tensor", lambda t, tb=tb, cc=cc: t.matmul(
                        pmisc[0:NEXP, 0:512], gw_s[:, cc * NEXP:(cc + 1) * NEXP],
                        xnf[:, cc * TH + tb * 512: cc * TH + tb * 512 + 512],
                        start=(cc == 0), stop=(cc == nCC - 1)),
                        r=["gw_s", xnfks[cc]], w=["px0"], inc=(cc == nCC - 1))
                P.op("vector", lambda v, tb=tb: v.tensor_copy(
                    lfm[0:8, tb * 512:(tb + 1) * 512], pmisc[0:NEXP, 0:512]),
                    r=["px0"], w=[f"{lfmk}:{tb}"])
            for tt in range(TH // 128):
                P.op("tensor", lambda t, tt=tt: t.transpose(
                    pmisc[:, 0:8], lfm[0:8, tt * 128:(tt + 1) * 128], idf[0:8, 0:8]),
                    r=[f"{lfmk}:{tt // 4}", "idf"], w=["px0"])
                P.op("vector", lambda v: v.tensor_copy(ltm[:, 0:8], pmisc[:, 0:8]),
                     r=["px0"], w=[ltmk])
                P.op("vector", lambda v: v.reduce_max(st8[:, 0:1], ltm[:, 0:8], axis=AX, negate=True),
                     r=[ltmk], w=["s0"])
                P.op("scalar", lambda s: s.activation(ptm[:, 0:8], ltm[:, 0:8], AF.Exp,
                                                      bias=st8[:, 0:1], accum_out=st8[:, 1:2]),
                     r=[ltmk, "s0"], w=[ptmk, "s1"])
                P.op("vector", lambda v: v.reciprocal(st8[:, 2:3], st8[:, 1:2]),
                     r=["s1"], w=["s2"])
                P.op("vector", lambda v: v.tensor_scalar(ptm[:, 0:8], ptm[:, 0:8], st8[:, 2:3], None, ALU.mult),
                     r=[ptmk, "s2"], w=[ptmk])
                P.op("vector", lambda v: v.reduce_max(st8[:, 3:4], ptm[:, 0:8], axis=AX),
                     r=[ptmk], w=["s3"])
                P.op("vector", lambda v: v.tensor_scalar(ptm[:, 8:16], ptm[:, 0:8], st8[:, 3:4], None, ALU.is_lt),
                     r=[ptmk, "s3"], w=["s4"])
                P.op("vector", lambda v: v.tensor_tensor(ptm[:, 8:16], ptm[:, 0:8], ptm[:, 8:16], ALU.mult),
                     r=[ptmk, "s4"], w=["s5"])
                P.op("vector", lambda v: v.reduce_max(st8[:, 4:5], ptm[:, 8:16], axis=AX),
                     r=["s5"], w=["s6"])
                P.op("vector", lambda v: v.tensor_scalar(ptm[:, 8:16], ptm[:, 0:8], st8[:, 4:5], None, ALU.is_ge),
                     r=[ptmk, "s6"], w=["s7"])
                P.op("vector", lambda v: v.tensor_tensor(ptm[:, 8:16], ptm[:, 0:8], ptm[:, 8:16], ALU.mult),
                     r=[ptmk, "s7"], w=["s8"])
                P.op("tensor", lambda t: t.transpose(pmisc2[0:8, 0:128], ptm[:, 8:16], idf[:]),
                     r=["s8", "idf"], w=["px1"])
                P.op("vector", lambda v, tt=tt: v.tensor_copy(
                    cmbf[0:8, tt * 128:(tt + 1) * 128], pmisc2[0:8, 0:128]),
                    r=["px1"], w=[f"{cmbfk}:{tt}"])
            P.dma("gpsimd", lambda e, th=th: e.dma_start(
                out=cmb_d[:, th * TH:(th + 1) * TH], in_=cmbf[0:8, 0:TH]),
                r=[f"{cmbfk}:{tt}" for tt in range(TH // 128)], w=[f"cmb_d:{th}"], stream="st0")

            for e in range(NEXP):
                P.dma("sync", lambda e_, e=e, th=th: e_.dma_start(
                    out=crow[0:1, :], in_=cmb_d[e:e + 1, th * TH:(th + 1) * TH]),
                    r=[f"cmb_d:{th}"], w=["crow"], stream="crow")
                for sbk in range(nSB):
                    P.op("tensor", lambda t, sbk=sbk: t.matmul(
                        pmisc[:, :], onesf_r[:], crow[0:1, sbk * 512:(sbk + 1) * 512],
                        start=True, stop=True),
                        r=["crow", "onesf_r"], w=["px0"])
                    P.op("vector", lambda v, sbk=sbk: v.tensor_copy(
                        cbc[:, sbk * 512:(sbk + 1) * 512], pmisc[:, :]),
                        r=["px0"], w=["cbc"])
                gate_up_down(th, guT[e], dwT[e], nFE, True, first=(e == 0))
            gate_up_down(th, sguT, sdT, nFS, False, first=False)
            for cc in range(nCC):
                P.dma("sync", lambda e_, cc=cc, th=th: e_.dma_start(
                    out=xre[:], in_=x_fm[cc * 128:(cc + 1) * 128, th * TH:(th + 1) * TH]),
                    w=[xrek], stream="xre")
                P.dma("sync", lambda e_, cc=cc, th=th: e_.dma_start(
                    out=are[:], in_=attn_fm[cc * 128:(cc + 1) * 128, th * TH:(th + 1) * TH]),
                    w=[arek], stream="are")
                for sbk in range(nSB):
                    yo = Y[:, cc * TH + sbk * 512: cc * TH + sbk * 512 + 512]
                    yk2 = f"{Yk}:{cc}:{sbk}"
                    P.op("vector", lambda v, yo=yo, sbk=sbk: v.tensor_tensor(
                        yo, yo, xre[:, sbk * 512:(sbk + 1) * 512], ALU.add),
                        r=[yk2, xrek], w=[yk2])
                    P.op("vector", lambda v, yo=yo, sbk=sbk: v.tensor_tensor(
                        yo, yo, are[:, sbk * 512:(sbk + 1) * 512], ALU.add),
                        r=[yk2, arek], w=[yk2])
                P.dma("gpsimd", lambda e_, cc=cc, th=th: e_.dma_start(
                    out=x1T[cc * 128:(cc + 1) * 128, th * TH:(th + 1) * TH],
                    in_=Y[:, cc * TH:(cc + 1) * TH]),
                    r=[f"{Yk}:{cc}:{sbk}" for sbk in range(nSB)], stream=f"o{cc % 2}")
            for lt in range(TH // 128):
                for cc in range(nCC):
                    P.op("tensor", lambda t, lt=lt, cc=cc: t.matmul(
                        pmisc2[:, 0:128], Y[:, cc * TH + lt * 128: cc * TH + lt * 128 + 128],
                        idf[:], start=True, stop=True),
                        r=[f"{Yk}:{cc}:{(lt * 128) // 512}", "idf"], w=["px1"])
                    P.op("vector", lambda v, cc=cc: v.tensor_copy(
                        xtmt[:, cc * 128:(cc + 1) * 128], pmisc2[:, 0:128]),
                        r=["px1"], w=[f"xtmt:{cc}"])
                P.dma("gpsimd", lambda e_, lt=lt, th=th: e_.dma_start(
                    out=x1_tm[th * TH + lt * 128: th * TH + (lt + 1) * 128, :], in_=xtmt[:]),
                    r=[f"xtmt:{cc}" for cc in range(nCC)], stream=f"o{lt % 2}")

        for k in ["ab:gw_s", "af:st8", "af:crow", "af:cbc", "af:xtmt"]:
            (AB if k.startswith("ab:") else AF_).unpin(k)
        AF_.unpin("af:Y")
        AB.unpin("ab:act")
        AB.unpin(xnfk)




def build_launch_b(T=2048):
    return _standalone(emit_b, T=T)


def stage_b(inputs, xn, attn, T=2048):
    x = np.asarray(inputs["L0_attn_x"])
    S = x.shape[1]
    So = S // 2
    tr = lambda w: np.ascontiguousarray(np.asarray(w).T)
    shared = dict(
        gwT=tr(inputs["L0_ffn_gate_weight"]),
        guT=np.ascontiguousarray(np.asarray(inputs["L0_ffn_gate_up_weights"]).transpose(0, 2, 1)),
        dwT=np.ascontiguousarray(np.asarray(inputs["L0_ffn_down_weights"]).transpose(0, 2, 1)),
        sguT=tr(inputs["L0_ffn_shared_gate_up"]),
        sdT=tr(inputs["L0_ffn_shared_down"]),
        identf=np.eye(128, dtype=np.float32),
        identb=np.eye(128, dtype=BF),
    )
    in_maps = []
    for c in range(8):
        b, hh = c // 2, c % 2
        sl = slice(hh * So, hh * So + T)
        m = dict(shared)
        m.update(
            xn_tm=np.ascontiguousarray(xn[b, sl]),
            x_fm=np.ascontiguousarray(x[b, sl].T),
            attn_fm=np.ascontiguousarray(attn[b, sl].T),
        )
        in_maps.append(m)
    return in_maps


DIN = 8512
ZDIM = 4096
CONVD = 4352
nZC = ZDIM // 128
nXBC = CONVD // 128


def emit_c1(E, T=2048):
    nTT = T // 128
    nTB = T // 512
    nc, P, AB, AF_ = E.nc, E.P, E.AB, E.AF
    AB.begin_phase()
    AF_.begin_phase()
    pb = E.pb
    x1_tm = E.inp("x1_tm", [T, HID], F32)
    winT = E.inp("winT", [HID, DIN], F32)
    xn1_out = E.out("xn1_out", [T, HID], BF16)
    xbc_raw = E.out("xbc_raw", [CONVD, T], F32)
    z_tm = E.out("z_tm", [T, ZDIM], F32)
    dt_raw = E.out("dt_raw", [T, 64], F32)
    if True:
        stat, _, _ = AF_.alloc("statc", 16, aliases=["st0", "st1", "st2", "st4", "st5", "st6", "st7", "st8"])
        AF_.pin("af:statc")
        xa, xak = [], []
        for i in range(2):
            s_, k_, _ = AF_.alloc(f"xa{i}", HID)
            xa.append(s_), xak.append(k_)
        dumpt, dumpk, _ = AF_.alloc("dump", HID)
        xnt, xntk = [], []
        for i in range(2):
            s_, k_, _ = AB.alloc(f"xnt{i}", HID)
            xnt.append(s_), xntk.append(k_)
        for tt in range(nTT):
            bi = tt % 2
            P.dma("sync", lambda e, tt=tt, bi=bi: e.dma_start(
                out=xa[bi][:], in_=x1_tm[tt * 128:(tt + 1) * 128, :]),
                w=[xak[bi]], stream=f"x{bi}")
            P.op("scalar", lambda s, bi=bi: s.activation(
                dumpt[:], xa[bi][:], AF.Square, accum_out=stat[:, 0:1]),
                r=[xak[bi]], w=[dumpk, "st0"])
            P.op("vector", lambda v, bi=bi: v.reduce_sum(stat[:, 1:2], xa[bi][:], axis=AX),
                 r=[xak[bi]], w=["st1"])
            P.op("vector", lambda v: v.tensor_scalar(stat[:, 2:3], stat[:, 1:2], 1.0 / HID, None, ALU.mult),
                 r=["st1"], w=["st2"])
            P.op("vector", lambda v: v.tensor_tensor(stat[:, 4:5], stat[:, 2:3], stat[:, 2:3], ALU.mult),
                 r=["st2"], w=["st4"])
            P.op("vector", lambda v: v.scalar_tensor_tensor(
                stat[:, 5:6], stat[:, 0:1], 1.0 / HID, stat[:, 4:5], ALU.mult, ALU.subtract),
                r=["st0", "st4"], w=["st5"])
            P.op("vector", lambda v: v.tensor_scalar(stat[:, 8:9], stat[:, 5:6], LN_EPS, None, ALU.add),
                 r=["st5"], w=["st8"])
            P.op("scalar", lambda s: s.activation(stat[:, 6:7], stat[:, 8:9], AF.Sqrt),
                 r=["st8"], w=["st6"])
            P.op("vector", lambda v: v.reciprocal(stat[:, 7:8], stat[:, 6:7]),
                 r=["st6"], w=["st7"])
            P.op("vector", lambda v, bi=bi: v.tensor_scalar(
                xnt[bi][:], xa[bi][:], stat[:, 2:3], stat[:, 7:8], ALU.subtract, ALU.mult),
                r=[xak[bi], "st2", "st7"], w=[xntk[bi]])
            P.dma("gpsimd", lambda e, tt=tt, bi=bi: e.dma_start(
                out=xn1_out[tt * 128:(tt + 1) * 128, :], in_=xnt[bi][:]),
                r=[xntk[bi]], w=[f"xn1:{tt}"], stream=f"st{bi}")

        xnf, xnfk, _ = AB.alloc("xnfc", nCC * T)
        AB.pin(xnfk)
        xnfks = [f"{xnfk}:{cc}" for cc in range(nCC)]
        for cc in range(nCC):
            P.dma("scalar", lambda e, cc=cc: e.dma_start(
                out=xnf[:, cc * T:(cc + 1) * T],
                in_=xn1_out[:, cc * 128:(cc + 1) * 128], transpose=True),
                r=[f"xn1:{tt}" for tt in range(nTT)],
                w=[xnfks[cc]], stream=f"xnf{cc % 2}")

        wsl = []
        for i in range(1):
            s_, k_, _ = AB.alloc(f"w{i}", nCC * 512)
            wsl.append((s_, k_))
        ev, evk = [], []
        for i in range(2):
            s_, k_, _ = AF_.alloc(f"ev{i}", 512)
            ev.append(s_), evk.append(k_)

        nFG = (nXBC + 3) // 4
        for fg in range(nFG):
            f0 = fg * 4
            nf = min(4, nXBC - f0)
            ws, wk = wsl[0]
            P.dma("gpsimd", lambda e, f0=f0, nf=nf, ws=ws: e.dma_start(
                out=ws[:, 0:nCC * nf * 128].rearrange("p (cc f) -> p cc f", cc=nCC),
                in_=winT[:, ZDIM + f0 * 128: ZDIM + (f0 + nf) * 128].rearrange(
                    "(cc p) f -> p cc f", p=128)),
                w=[wk], stream="wl0")
            for tb in range(nTB):
                for fi in range(nf):
                    pk = f"pb{fi}"
                    for cc in range(nCC):
                        P.op("tensor", lambda t, fi=fi, cc=cc, tb=tb, ws=ws, nf=nf: t.matmul(
                            pb[fi][:], ws[:, cc * nf * 128 + fi * 128: cc * nf * 128 + fi * 128 + 128],
                            xnf[:, cc * T + tb * 512: cc * T + tb * 512 + 512],
                            start=(cc == 0), stop=(cc == nCC - 1)),
                            r=[wk, xnfks[cc]], w=[pk], inc=(cc == nCC - 1))
                    P.op("vector", lambda v, fi=fi: v.tensor_copy(ev[fi % 2][:], pb[fi][:]),
                         r=[pk], w=[evk[fi % 2]])
                    P.dma("gpsimd", lambda e, f0=f0, fi=fi, tb=tb: e.dma_start(
                        out=xbc_raw[(f0 + fi) * 128:(f0 + fi + 1) * 128, tb * 512:(tb + 1) * 512],
                        in_=ev[fi % 2][:]),
                        r=[evk[fi % 2]], w=[f"xbcs:{f0 + fi}:{tb}"], stream=f"so{fi % 2}")

        for zb in range(ZDIM // 512):
            ws, wk = wsl[0]
            P.dma("gpsimd", lambda e, zb=zb, ws=ws: e.dma_start(
                out=ws.rearrange("p (cc f) -> p cc f", cc=nCC),
                in_=winT[:, zb * 512:(zb + 1) * 512].rearrange("(cc p) f -> p cc f", p=128)),
                w=[wk], stream="wl0")
            for tt in range(nTT):
                pk = f"pb{tt % 4}"
                for cc in range(nCC):
                    P.op("tensor", lambda t, tt=tt, cc=cc, ws=ws: t.matmul(
                        pb[tt % 4][:], xnf[:, cc * T + tt * 128: cc * T + tt * 128 + 128],
                        ws[:, cc * 512:(cc + 1) * 512],
                        start=(cc == 0), stop=(cc == nCC - 1)),
                        r=[wk, xnfks[cc]], w=[pk], inc=(cc == nCC - 1))
                P.op("vector", lambda v, tt=tt: v.tensor_copy(ev[tt % 2][:], pb[tt % 4][:]),
                     r=[pk], w=[evk[tt % 2]])
                P.dma("gpsimd", lambda e, zb=zb, tt=tt: e.dma_start(
                    out=z_tm[tt * 128:(tt + 1) * 128, zb * 512:(zb + 1) * 512],
                    in_=ev[tt % 2][:]),
                    r=[evk[tt % 2]], stream=f"so{tt % 2}")

        wdt, wdtk, _ = AB.alloc("wdt", nCC * 64)
        P.dma("gpsimd", lambda e: e.dma_start(
            out=wdt.rearrange("p (cc f) -> p cc f", cc=nCC),
            in_=winT[:, DIN - 64:DIN].rearrange("(cc p) f -> p cc f", p=128)),
            w=[wdtk], stream="wl0")
        for tt in range(nTT):
            pk = f"pb{tt % 4}"
            for cc in range(nCC):
                P.op("tensor", lambda t, tt=tt, cc=cc: t.matmul(
                    pb[tt % 4][:, 0:64], xnf[:, cc * T + tt * 128: cc * T + tt * 128 + 128],
                    wdt[:, cc * 64:(cc + 1) * 64],
                    start=(cc == 0), stop=(cc == nCC - 1)),
                    r=[wdtk, xnfks[cc]], w=[pk], inc=(cc == nCC - 1))
            P.op("vector", lambda v, tt=tt: v.tensor_copy(ev[tt % 2][:, 0:64], pb[tt % 4][:, 0:64]),
                 r=[pk], w=[evk[tt % 2]])
            P.dma("gpsimd", lambda e, tt=tt: e.dma_start(
                out=dt_raw[tt * 128:(tt + 1) * 128, :], in_=ev[tt % 2][:, 0:64]),
                r=[evk[tt % 2]], stream=f"so{tt % 2}")

        AF_.unpin("af:statc")
        AB.unpin(xnfk)




def build_launch_c1(T=2048):
    return _standalone(emit_c1, T=T)


def stage_c1(inputs, x1, T=2048):
    S = x1.shape[1]
    So = S // 2
    tr = lambda w: np.ascontiguousarray(np.asarray(w).T)
    shared = dict(winT=tr(inputs["L1_attn_W_in"]),
                  identf=np.eye(128, dtype=np.float32),
                  identb=np.eye(128, dtype=BF))
    in_maps = []
    for c in range(8):
        b, hh = c // 2, c % 2
        m = dict(shared)
        m.update(x1_tm=np.ascontiguousarray(x1[b, hh * So: hh * So + T]))
        in_maps.append(m)
    return in_maps


MH = 64
MP = 64
MN = 128


def emit_c2a(E, T=2048, fused_halo=False):
    nCH = T // 128
    nc, P, AB, AF_ = E.nc, E.P, E.AB, E.AF
    AB.begin_phase()
    AF_.begin_phase()
    pb, pt = E.pb, E.pt
    pcb, pyd, pmisc = E.pb[2], E.pb[3], E.px[0]
    ptq0, ptq1 = E.pt[0], E.pt[1]
    idf, idb = E.idf, E.idb
    onesf_r, ones_c = E.onesf_r, E.ones_c
    if fused_halo:
        xbc_m = E.inp("xbc_raw", [CONVD, T], F32)
        halo_ap = E.wiring["cc1_halo"]
        m0col = E.inp("m0col", [128, 1], F32)
    else:
        xbc_h = E.inp("xbc_h", [CONVD, T + 3], F32)
    dtr_d = E.inp("dt_raw", [T, MH], F32)
    cw_d = E.inp("cw_d", [128, nXBC * 4], F32)
    dtb_d = E.inp("dtb_d", [1, MH], F32)
    alog_d = E.inp("alog_d", [1, MH], F32)
    dd_d = E.inp("dd_d", [1, MH], F32)
    tri_d = E.inp("tri_d", [128, 128], F32)
    mnegw_d = E.inp("mnegw_d", [128, MH * 128], BF16)
    y_part = E.out("y_part", [T, ZDIM], F32)
    states_a = E.out("states_a", [nCH, MP, MH * MN], BF16)
    decays_d = E.out("decays_d", [1, nCH * MH], F32)
    sloc = E.out("sloc", [MP, MH * MN], F32)
    c_fm = E.out("c_fm", [MN, T], BF16)
    cum_out = E.out("cum_out", [T, MH], F32)
    xs_a = E.scratch("xs_a", [ZDIM, T], BF16)
    cumt_d = E.scratch("cumt_d", [MH, 128], F32)
    if True:
        tri_s, _, _ = AF_.alloc("tri_s", 128, aliases=["tri"])
        AF_.pin("af:tri_s")
        mnegw, _, _ = AB.alloc("mnegw", MH * 128, aliases=["mnegw"])
        AB.pin("ab:mnegw")
        cwt, _, _ = AF_.alloc("cwt", nXBC * 4, aliases=["cwt"])
        AF_.pin("af:cwt")
        rows, _, _ = AF_.alloc("rows", 4 * MH, parts=1, aliases=["r_dtb", "r_al", "r_dd"])
        AF_.pin("af:rows")
        S_t, _, _ = AF_.alloc("S_t", MH * MN, parts=MP, aliases=["S"])
        AF_.pin("af:S_t")
        Bfm, _, _ = AB.alloc("Bfm", T, aliases=["Bfm"])
        AB.pin("ab:Bfm")
        Cfm, _, _ = AB.alloc("Cfm", T, aliases=["Cfm"])
        AB.pin("ab:Cfm")
        small, _, _ = AF_.alloc("small", 1024, aliases=[
            "dtbbc", "expab", "DBt", "dtc", "lndt", "apos", "cum_sb", "colv",
            "wdecc", "tmp1", "tmp2"])
        AF_.pin("af:small")
        cl_r, _, _ = AF_.alloc("cl_r", MH, parts=1, aliases=["cl_r"])
        AF_.pin("af:cl_r")
        dec_r, _, _ = AF_.alloc("dec_r", MH, parts=1, aliases=["dec_r"])
        AF_.pin("af:dec_r")
        decB, _, _ = AF_.alloc("decBa", MH, parts=MP, aliases=["decB"])
        AF_.pin("af:decBa")
        cumfq, _, _ = AF_.alloc("cumfq", 2048, parts=1, aliases=["cumfq"])
        AF_.pin("af:cumfq")
        cumT, _, _ = AF_.alloc("cumT", 128, parts=MH, aliases=["cumT"])
        AF_.pin("af:cumT")
        dtr_s, _, _ = AF_.alloc("dtr_s", nCH * MH, aliases=["dtr"])
        AF_.pin("af:dtr_s")
        P.op("gpsimd", lambda e: e.memset(S_t[:, :], 0.0), w=["S"])
        for nm, dst, k in [("tri_d", tri_s, "tri"), ("mnegw_d", mnegw, "mnegw"),
                           ("cw_d", cwt, "cwt")]:
            src = {"tri_d": tri_d, "mnegw_d": mnegw_d, "cw_d": cw_d}[nm]
            P.dma("sync", lambda e, dst=dst, src=src: e.dma_start(out=dst[:, :], in_=src),
                  w=[k], stream="cst")
        P.dma("sync", lambda e: e.dma_start(out=rows[0:1, 0:MH], in_=dtb_d), w=["r_dtb"], stream="cst")
        P.dma("sync", lambda e: e.dma_start(out=rows[0:1, MH:2 * MH], in_=alog_d), w=["r_al"], stream="cst")
        P.dma("sync", lambda e: e.dma_start(out=rows[0:1, 2 * MH:3 * MH], in_=dd_d), w=["r_dd"], stream="cst")
        dtbbc = small[:, 0:64]
        expab = small[:, 64:128]
        DBt = small[:, 128:192]
        P.op("tensor", lambda t: t.matmul(pmisc[:, 0:MH], onesf_r[:], rows[0:1, 0:MH],
                                          start=True, stop=True),
             r=["r_dtb", "onesf_r"], w=["px0"])
        P.op("vector", lambda v: v.tensor_copy(dtbbc, pmisc[:, 0:MH]), r=["px0"], w=["dtbbc"])
        P.op("tensor", lambda t: t.matmul(pmisc[:, 0:MH], onesf_r[:], rows[0:1, MH:2 * MH],
                                          start=True, stop=True),
             r=["r_al", "onesf_r", "dtbbc"], w=["px0"])
        P.op("scalar", lambda s: s.activation(expab, pmisc[:, 0:MH], AF.Exp),
             r=["px0"], w=["expab"])
        P.op("tensor", lambda t: t.matmul(pmisc[:, 0:MH], onesf_r[:], rows[0:1, 2 * MH:3 * MH],
                                          start=True, stop=True),
             r=["r_dd", "onesf_r", "expab"], w=["px0"])
        P.op("vector", lambda v: v.tensor_copy(DBt, pmisc[:, 0:MH]), r=["px0"], w=["DBt"])

        xbr, xbrk = [], []
        for i in range(2):
            s_, k_, _ = AF_.alloc(f"xbr{i}", T + 3)
            xbr.append(s_), xbrk.append(k_)
        if fused_halo:
            m0s, _, _ = AF_.alloc("m0s", 1, aliases=["m0s"])
            AF_.pin("af:m0s")
            P.dma("sync", lambda e: e.dma_start(out=m0s[:, 0:1], in_=m0col),
                  w=["m0s"], stream="cst")
        cacc, cacck, _ = AF_.alloc("cacc", T)
        cout, coutk, _ = AB.alloc("cout", T)
        for fc in range(nXBC):
            bi = fc % 2
            if fused_halo:
                P.dma("sync", lambda e, fc=fc, bi=bi: e.dma_start(
                    out=xbr[bi][:, 0:3], in_=halo_ap[fc * 128:(fc + 1) * 128, :]),
                    w=[xbrk[bi]], stream=f"xb{bi}", deps=E.wiring.get("cc1_tok"))
                P.dma("sync", lambda e, fc=fc, bi=bi: e.dma_start(
                    out=xbr[bi][:, 3:T + 3], in_=xbc_m[fc * 128:(fc + 1) * 128, :]),
                    r=[f"xbcs:{fc}"], w=[xbrk[bi]], stream=f"xb{bi}")
                P.op("vector", lambda v, bi=bi: v.tensor_scalar(
                    xbr[bi][:, 0:3], xbr[bi][:, 0:3], m0s[:, 0:1], None, ALU.mult),
                    r=[xbrk[bi], "m0s"], w=[xbrk[bi]])
            else:
                P.dma("sync", lambda e, fc=fc, bi=bi: e.dma_start(
                    out=xbr[bi][:], in_=xbc_h[fc * 128:(fc + 1) * 128, :]),
                    w=[xbrk[bi]], stream=f"xb{bi}")
            P.op("vector", lambda v, fc=fc, bi=bi: v.tensor_scalar(
                cacc[:], xbr[bi][:, 0:T], cwt[:, fc * 4:fc * 4 + 1], None, ALU.mult),
                r=[xbrk[bi], "cwt"], w=[cacck])
            for k in range(1, 4):
                P.op("vector", lambda v, fc=fc, bi=bi, k=k: v.scalar_tensor_tensor(
                    cacc[:], xbr[bi][:, k:k + T], cwt[:, fc * 4 + k:fc * 4 + k + 1],
                    cacc[:], ALU.mult, ALU.add),
                    r=[xbrk[bi], "cwt", cacck], w=[cacck])
            P.op("scalar", lambda s: s.activation(cout[:], cacc[:], AF.Silu),
                 r=[cacck], w=[coutk])
            if fc < 32:
                P.dma("gpsimd", lambda e, fc=fc: e.dma_start(
                    out=xs_a[fc * 128:(fc + 1) * 128, :], in_=cout[:]),
                    r=[coutk], w=[f"xs:{fc}"], stream=f"so{fc % 2}")
            elif fc == 32:
                P.op("vector", lambda v: v.tensor_copy(Bfm[:, :], cout[:]),
                     r=[coutk], w=["Bfm"])
            else:
                P.op("vector", lambda v: v.tensor_copy(Cfm[:, :], cout[:]),
                     r=[coutk], w=["Cfm"])
                P.dma("gpsimd", lambda e: e.dma_start(out=c_fm[:, :], in_=Cfm[:, :]),
                      r=["Cfm"], stream="so0")

        P.dma("sync", lambda e: e.dma_start(
            out=dtr_s.rearrange("p (c h) -> p c h", c=nCH),
            in_=dtr_d.rearrange("(c p) h -> p c h", p=128)),
            w=["dtr"], stream="cst")

        dtc = small[:, 192:256]
        lndt = small[:, 256:320]
        apos = small[:, 320:384]
        cum_sb = small[:, 384:448]
        colv = small[:, 448:512]
        wdecc = small[:, 512:576]
        tmp1 = small[:, 576:640]
        tmp2 = small[:, 640:704]
        xsT_sb, xsTk, _ = AB.alloc("xsT", 32 * 128)
        xsc, xsck = [], []
        for i in range(2):
            s_, k_, _ = AB.alloc(f"xsc{i}", 32 * 128)
            xsc.append(s_), xsck.append(k_)
        btq, btqk, _ = AB.alloc("btq", 128)
        dti, dtik, _ = AF_.alloc("dti", 512)
        edt, edtk, _ = AB.alloc("edt", 512)
        mts, mtsk, _ = AB.alloc("mts", 512)
        wx, wxk, _ = AB.alloc("wx", 256)
        stb, stbk, _ = AB.alloc("stb", 512)
        sts, stsk, _ = AF_.alloc("sts", 128)
        yc, yck, _ = AF_.alloc("yc", ZDIM)

        for c in range(nCH):
            ci = c % 2
            P.dma("sync", lambda e, c=c, ci=ci: e.dma_start(
                out=xsc[ci].rearrange("p (fc t) -> p fc t", fc=32),
                in_=xs_a.rearrange("(fc p) t -> p fc t", p=128)[:, :, c * 128:(c + 1) * 128]),
                r=[f"xs:{fc}" for fc in range(32)], w=[xsck[ci]], stream=f"xs{ci}")
            dk = f"dt:{c}"
            P.op("vector", lambda v, c=c: v.tensor_tensor(
                tmp1[:, 0:64], dtr_s[:, c * MH:(c + 1) * MH], dtbbc, ALU.add),
                r=["dtr", "dtbbc"], w=["tmp1"])
            P.op("scalar", lambda s: s.activation(tmp2[:, 0:64], tmp1[:, 0:64], AF.Abs),
                 r=["tmp1"], w=["tmp2"])
            P.op("scalar", lambda s: s.activation(tmp2[:, 0:64], tmp2[:, 0:64], AF.Exp, scale=-1.0),
                 r=["tmp2"], w=["tmp2"])
            P.op("vector", lambda v: v.tensor_scalar(tmp2[:, 0:64], tmp2[:, 0:64], 1.0, None, ALU.add),
                 r=["tmp2"], w=["tmp2"])
            P.op("scalar", lambda s: s.activation(tmp2[:, 0:64], tmp2[:, 0:64], AF.Ln),
                 r=["tmp2"], w=["tmp2"])
            P.op("vector", lambda v: v.tensor_scalar(tmp1[:, 0:64], tmp1[:, 0:64], 0.0, None, ALU.max),
                 r=["tmp1"], w=["tmp1"])
            P.op("vector", lambda v: v.tensor_tensor(dtc, tmp1[:, 0:64], tmp2[:, 0:64], ALU.add),
                 r=["tmp1", "tmp2"], w=["dtc"])
            P.op("scalar", lambda s: s.activation(lndt, dtc, AF.Ln), r=["dtc"], w=["lndt"])
            P.op("vector", lambda v: v.tensor_tensor(apos, dtc, expab, ALU.mult),
                 r=["dtc", "expab"], w=["apos"])
            P.op("tensor", lambda t: t.matmul(pmisc[:, 0:64], tri_s[:], apos,
                                              start=True, stop=True),
                 r=["tri", "apos"], w=["px0"])
            P.op("vector", lambda v: v.tensor_scalar(cum_sb, pmisc[:, 0:64], -1.0, None, ALU.mult),
                 r=["px0"], w=["cum_sb"])
            P.dma("gpsimd", lambda e, c=c: e.dma_start(
                out=cum_out[c * 128:(c + 1) * 128, :], in_=cum_sb),
                r=["cum_sb"], stream="so1")
            P.op("tensor", lambda t: t.matmul(pmisc[0:1, 64:128], ones_c[:], apos,
                                              start=True, stop=True),
                 r=["ones_c", "apos", "cum_sb"], w=["px0"])
            P.op("vector", lambda v: v.tensor_scalar(cl_r[0:1, :], pmisc[0:1, 64:128], -1.0, None, ALU.mult),
                 r=["px0"], w=["cl_r"])
            P.op("scalar", lambda s: s.activation(dec_r[0:1, :], cl_r[0:1, :], AF.Exp),
                 r=["cl_r"], w=["dec_r"])
            P.dma("gpsimd", lambda e, c=c: e.dma_start(
                out=decays_d[:, c * MH:(c + 1) * MH], in_=dec_r[0:1, :]),
                r=["dec_r"], stream="so1")
            P.op("tensor", lambda t: t.matmul(pmisc[0:MP, 128:192], onesf_r[:, 0:MH].rearrange("o h -> o h"), dec_r[0:1, :],
                                              start=True, stop=True),
                 r=["dec_r", "onesf_r", "cl_r"], w=["px0"])
            P.op("vector", lambda v: v.tensor_copy(decB[0:MP, :], pmisc[0:MP, 128:192]),
                 r=["px0"], w=["decB"])
            P.op("vector", lambda v: v.tensor_tensor(colv, lndt, cum_sb, ALU.subtract),
                 r=["lndt", "cum_sb"], w=["colv"])
            P.op("tensor", lambda t: t.matmul(pmisc[:, 192:256], onesf_r[:], cl_r[0:1, :],
                                              start=True, stop=True),
                 r=["cl_r", "onesf_r", "decB"], w=["px0"])
            P.op("vector", lambda v: v.scalar_tensor_tensor(
                tmp1[:, 0:64], cum_sb, -1.0, pmisc[:, 192:256], ALU.mult, ALU.add),
                r=["cum_sb", "px0"], w=["tmp1"])
            P.op("vector", lambda v: v.tensor_tensor(tmp1[:, 0:64], tmp1[:, 0:64], lndt, ALU.add),
                 r=["tmp1", "lndt"], w=["tmp1"])
            P.op("scalar", lambda s: s.activation(wdecc, tmp1[:, 0:64], AF.Exp),
                 r=["tmp1"], w=["wdecc"])
            P.op("tensor", lambda t: t.transpose(pmisc[0:MH, 256:384], cum_sb, idf[:]),
                 r=["cum_sb", "idf", "tmp1"], w=["px0"])
            P.op("vector", lambda v: v.tensor_copy(cumT[0:MH, :], pmisc[0:MH, 256:384]),
                 r=["px0"], w=["cumT"])
            P.dma("gpsimd", lambda e: e.dma_start(out=cumt_d[:, :], in_=cumT[0:MH, :]),
                  r=["cumT"], w=["cumt_d"], stream="so1")
            P.op("tensor", lambda t, c=c: t.transpose(
                ptq0[:, 0:128], Bfm[:, c * 128:(c + 1) * 128], idb[:]),
                r=["Bfm", "idb"], w=["pt0"])
            P.op("vector", lambda v: v.tensor_copy(btq[:], ptq0[:, 0:128]),
                 r=["pt0"], w=[btqk])
            P.op("tensor", lambda t, c=c: t.matmul(
                pcb[:, 0:128], Bfm[:, c * 128:(c + 1) * 128], Cfm[:, c * 128:(c + 1) * 128],
                start=True, stop=True),
                r=["Bfm", "Cfm"], w=["pb2"])
            for g8 in range(4):
                g = g8 % 2
                pt_ = ptq0 if g == 0 else ptq1
                ptk = f"pt{g}"
                for j in range(8):
                    fc = g8 * 8 + j
                    P.op("tensor", lambda t, fc=fc, j=j, ci=ci, pt_=pt_: t.transpose(
                        pt_[:, j * 128:(j + 1) * 128],
                        xsc[ci][:, fc * 128:(fc + 1) * 128], idb[:]),
                        r=[xsck[ci], "idb"], w=[ptk], inc=(j == 7))
                P.op("vector", lambda v, g8=g8, pt_=pt_: v.tensor_copy(
                    xsT_sb[:, g8 * 1024:(g8 + 1) * 1024], pt_[:, :]),
                    r=[ptk], w=[f"{xsTk}:{g8}"])
            for q in range(16):
                pw = pb[q % 2]
                pwk = f"pb{q % 2}"
                P.dma("sync", lambda e, q=q: e.dma_start(
                    out=cumfq[0:1, 0:512], in_=cumt_d.rearrange("h t -> (h t)")[q * 512:(q + 1) * 512][None, :]),
                    r=["cumt_d", pwk], w=["cumfq"], stream="cfq")
                P.op("tensor", lambda t, pw=pw: t.matmul(
                    pw[:, 0:512], onesf_r[:], cumfq[0:1, 0:512],
                    start=True, stop=False),
                    r=["cumfq", "onesf_r"], w=[pwk], inc=False)
                P.op("tensor", lambda t, q=q, pw=pw: t.matmul(
                    pw[:, 0:512], idb[:], mnegw[:, q * 512:(q + 1) * 512],
                    start=False, stop=True),
                    r=["idb", "mnegw"], w=[pwk])
                for hh in range(4):
                    h = q * 4 + hh
                    P.op("vector", lambda v, hh=hh, h=h, pw=pw: v.tensor_scalar(
                        dti[:, hh * 128:(hh + 1) * 128], pw[:, hh * 128:(hh + 1) * 128],
                        colv[:, h:h + 1], None, ALU.add),
                        r=[pwk, "colv"], w=[dtik], inc=(hh == 3))
                for hh in range(4):
                    P.op("scalar", lambda s, hh=hh: s.activation(
                        edt[:, hh * 128:(hh + 1) * 128], dti[:, hh * 128:(hh + 1) * 128], AF.Exp),
                        r=[dtik], w=[edtk], inc=(hh == 3))
                for hh in range(4):
                    P.op("vector", lambda v, hh=hh: v.tensor_tensor(
                        mts[:, hh * 128:(hh + 1) * 128], pcb[:, 0:128],
                        edt[:, hh * 128:(hh + 1) * 128], ALU.mult),
                        r=["pb2", edtk], w=[mtsk], inc=(hh == 3))
                for hh in range(4):
                    h = q * 4 + hh
                    P.op("tensor", lambda t, hh=hh, h=h: t.matmul(
                        pyd[:, hh * 128: hh * 128 + 64], mts[:, hh * 128:(hh + 1) * 128],
                        xsT_sb[:, (h // 2) * 128 + (h % 2) * 64: (h // 2) * 128 + (h % 2) * 64 + 64],
                        start=True, stop=True),
                        r=[mtsk, f"{xsTk}:{h // 16}"], w=["pb3"], inc=(hh == 3))
                for hh in range(4):
                    h = q * 4 + hh
                    P.op("vector", lambda v, hh=hh, h=h: v.scalar_tensor_tensor(
                        yc[:, h * 64:(h + 1) * 64],
                        xsT_sb[:, (h // 2) * 128 + (h % 2) * 64: (h // 2) * 128 + (h % 2) * 64 + 64],
                        DBt[:, h:h + 1], pyd[:, hh * 128: hh * 128 + 64], ALU.mult, ALU.add),
                        r=[f"{xsTk}:{h // 16}", "DBt", "pb3"], w=[f"{yck}:{h}"], inc=(hh == 3))
            P.dma("gpsimd", lambda e, c=c: e.dma_start(
                out=y_part[c * 128:(c + 1) * 128, :], in_=yc[:, :]),
                r=[f"{yck}:{h}" for h in range(MH)], stream="so0")
            for prg in range(16):
                for j in range(2):
                    pr = prg * 2 + j
                    P.op("vector", lambda v, pr=pr, j=j: v.tensor_scalar(
                        wx[:, j * 128:j * 128 + 64], xsT_sb[:, pr * 128:pr * 128 + 64],
                        wdecc[:, 2 * pr:2 * pr + 1], None, ALU.mult),
                        r=[f"{xsTk}:{pr // 8}", "wdecc"], w=[wxk], inc=False)
                    P.op("vector", lambda v, pr=pr, j=j: v.tensor_scalar(
                        wx[:, j * 128 + 64:j * 128 + 128], xsT_sb[:, pr * 128 + 64:pr * 128 + 128],
                        wdecc[:, 2 * pr + 1:2 * pr + 2], None, ALU.mult),
                        r=[f"{xsTk}:{pr // 8}", "wdecc"], w=[wxk], inc=(j == 1))
                for q4 in range(4):
                    P.op("tensor", lambda t, q4=q4: t.matmul(
                        pyd[0:64, q4 * 128:(q4 + 1) * 128], wx[:, q4 * 64:(q4 + 1) * 64],
                        btq[:], start=True, stop=True),
                        r=[wxk, btqk], w=["pb3"], inc=(q4 == 3))
                P.op("vector", lambda v: v.tensor_copy(stb[0:64, :], pyd[0:64, :]),
                     r=["pb3"], w=[stbk])
                for q4 in range(4):
                    h = prg * 4 + q4
                    P.dma("gpsimd", lambda e, c=c, h=h, q4=q4: e.dma_start(
                        out=states_a[c, :, h * 128:(h + 1) * 128],
                        in_=stb[0:64, q4 * 128:(q4 + 1) * 128]),
                        r=[stbk], stream=f"so{prg % 2}")
                for q4 in range(4):
                    h = prg * 4 + q4
                    P.op("vector", lambda v, h=h, q4=q4: v.scalar_tensor_tensor(
                        S_t[:, h * MN:(h + 1) * MN], S_t[:, h * MN:(h + 1) * MN],
                        decB[0:MP, h:h + 1], stb[0:64, q4 * 128:(q4 + 1) * 128], ALU.mult, ALU.add),
                        r=["decB", stbk, "S"], w=["S"], inc=(q4 == 3))
        P.dma("gpsimd", lambda e: e.dma_start(out=sloc[:, :], in_=S_t[:, :]),
              r=["S"], w=["sloc_s"], stream="so0")

        if fused_halo:
            AF_.unpin("af:m0s")
        for k in ["af:tri_s", "af:cwt", "af:rows", "af:S_t", "af:small",
                  "af:cl_r", "af:dec_r", "af:decBa", "af:cumfq", "af:cumT", "af:dtr_s"]:
            AF_.unpin(k)
        AB.unpin("ab:mnegw")
        AB.unpin("ab:Bfm")
        AB.unpin("ab:Cfm")




def build_launch_c2a(T=2048):
    return _standalone(emit_c2a, T=T)


def _tri_mask():
    j = np.arange(128)[:, None]
    i = np.arange(128)[None, :]
    tri = (j <= i).astype(np.float32)
    mneg = np.where(i >= j, 0.0, -1e30).astype(np.float32)
    return tri, np.ascontiguousarray(np.tile(mneg, (1, MH)).astype(BF))


def stage_c2a(inputs, xbc_all, dtr_all, T=2048):
    tri, mnegw = _tri_mask()
    cw = np.asarray(inputs["L1_attn_conv_w"]).reshape(nXBC, 128, 4)
    cw = np.ascontiguousarray(cw.transpose(1, 0, 2).reshape(128, nXBC * 4))
    shared = dict(
        cw_d=cw,
        dtb_d=np.asarray(inputs["L1_attn_dt_bias"]).reshape(1, MH),
        alog_d=np.asarray(inputs["L1_attn_A_log"]).reshape(1, MH),
        dd_d=np.asarray(inputs["L1_attn_D"]).reshape(1, MH),
        tri_d=tri, mnegw_d=mnegw,
        identf=np.eye(128, dtype=np.float32),
        identb=np.eye(128, dtype=BF),
    )
    in_maps = []
    for c in range(8):
        hh = c % 2
        if hh == 0:
            halo = np.zeros((CONVD, 3), np.float32)
        else:
            halo = xbc_all[c - 1][:, T - 3:T]
        m = dict(shared)
        m.update(
            xbc_h=np.ascontiguousarray(np.concatenate([halo, xbc_all[c]], axis=1)),
            dt_raw=np.ascontiguousarray(dtr_all[c]),
        )
        in_maps.append(m)
    return in_maps


def emit_c2b(E, T=2048, fused_sinit=False):
    nCH = T // 128
    nTB = T // 512
    nc, P, AB, AF_ = E.nc, E.P, E.AB, E.AF
    AB.begin_phase()
    AF_.begin_phase()
    pb = E.pb
    pyo, pmisc = E.pb[2], E.px[0]
    ptq0, ptq1 = E.pt[0], E.pt[1]
    idf, idb = E.idf, E.idb
    onesf_r = E.onesf_r
    y_part = E.inp("y_part", [T, ZDIM], F32)
    states_a = E.inp("states_a", [nCH, MP, MH * MN], BF16)
    decays_d = E.inp("decays_d", [1, nCH * MH], F32)
    if fused_sinit:
        sinit_ap = E.wiring["cc2_sinit"]
        m0col = E.inp("m0col", [128, 1], F32)
    else:
        s_init = E.inp("s_init", [MP, MH * MN], F32)
    c_fm_d = E.inp("c_fm", [MN, T], BF16)
    cum_d = E.inp("cum_out", [T, MH], F32)
    z_tm = E.inp("z_tm", [T, ZDIM], F32)
    nw_row = E.inp("nw_row", [1, ZDIM], F32)
    woutT = E.inp("woutT", [ZDIM, HID], F32)
    attn1_T = E.out("attn1_T", [HID, T], F32)
    yn_fm = E.scratch("yn_fm", [ZDIM, T], BF16)
    if True:
        S_t, _, _ = AF_.alloc("S_tb", MH * MN, parts=MP, aliases=["S"])
        AF_.pin("af:S_tb")
        Cfm, _, _ = AB.alloc("Cfm", T, parts=MN, aliases=["Cfm"])
        AB.pin("ab:Cfm")
        decs, _, _ = AF_.alloc("decs", nCH * MH, parts=1, aliases=["decs"])
        AF_.pin("af:decs")
        decB, _, _ = AF_.alloc("decBb", MH, parts=MP, aliases=["decB"])
        AF_.pin("af:decBb")
        NW, _, _ = AB.alloc("NW", ZDIM, aliases=["NW"])
        AB.pin("ab:NW")
        stat, _, _ = AF_.alloc("statb", 8, aliases=["st0", "st1", "st2", "st3"])
        AF_.pin("af:statb")
        ec, _, _ = AF_.alloc("ec", MH, aliases=["ec"])
        AF_.pin("af:ec")
        cumc, _, _ = AF_.alloc("cumc", MH, aliases=["cumc"])
        AF_.pin("af:cumc")
        shb, _, _ = AB.alloc("shb", 4 * MN, parts=MP, aliases=["shb"])
        AB.pin("ab:shb")
        sthb, _, _ = AB.alloc("sthb", 256, parts=MN, aliases=["sthb"])
        AB.pin("ab:sthb")
        ynb, _, _ = AB.alloc("ynb", 1024, aliases=["ynb"])
        ynb2, _, _ = AB.alloc("ynb2", 1024, aliases=["ynb2"])
        AB.pin("ab:ynb2")
        AB.pin("ab:ynb")
        nwr, _, _ = AF_.alloc("nwrb", 512, parts=1, aliases=["nwr"])
        AF_.pin("af:nwrb")
        if fused_sinit:
            m0s, _, _ = AF_.alloc("m0s_b", 1, aliases=["m0s_b"])
            AF_.pin("af:m0s_b")
            P.dma("sync", lambda e: e.dma_start(out=m0s[:, 0:1], in_=m0col),
                  w=["m0s_b"], stream="cst")
            P.dma("sync", lambda e: e.dma_start(out=S_t[:, :], in_=sinit_ap),
                  w=["S"], stream="cst", deps=E.wiring.get("cc2_tok"))
            P.op("vector", lambda v: v.tensor_scalar(
                S_t[:, :], S_t[:, :], m0s[0:MP, 0:1], None, ALU.mult),
                r=["m0s_b", "S"], w=["S"])
        else:
            P.dma("sync", lambda e: e.dma_start(out=S_t[:, :], in_=s_init), w=["S"], stream="cst")
        P.dma("sync", lambda e: e.dma_start(out=Cfm[:, :], in_=c_fm_d), w=["Cfm"], stream="cst")
        P.dma("sync", lambda e: e.dma_start(out=decs[0:1, :], in_=decays_d), w=["decs"], stream="cst")
        for zb in range(ZDIM // 512):
            P.dma("sync", lambda e, zb=zb: e.dma_start(
                out=nwr[0:1, :], in_=nw_row[:, zb * 512:(zb + 1) * 512]),
                r=["nwr"], w=["nwr"], stream="cst")
            P.op("tensor", lambda t, zb=zb: t.matmul(
                pmisc[:, :], onesf_r[:], nwr[0:1, :],
                start=True, stop=True),
                r=["nwr", "onesf_r", "NW"], w=["px0"])
            P.op("vector", lambda v, zb=zb: v.tensor_copy(NW[:, zb * 512:(zb + 1) * 512], pmisc[:, :]),
                 r=["px0"], w=["NW"])
        stc, stck = [], []
        for i in range(2):
            s_, k_, _ = AB.alloc(f"stc{i}", MH * MN)
            stc.append(s_), stck.append(k_)
            AB.pin(k_)
        zt, ztk, _ = AF_.alloc("zt", ZDIM)
        AF_.pin("af:zt")
        yfin, yfink, _ = AF_.alloc("yfin", ZDIM)
        AF_.pin("af:yfin")
        slz, slzk, _ = AF_.alloc("slz", ZDIM)
        AF_.pin("af:slz")

        for c in range(nCH):
            ci = c % 2
            P.dma("sync", lambda e, c=c, ci=ci: e.dma_start(
                out=stc[ci][0:MP, :], in_=states_a[c]),
                w=[stck[ci]], stream=f"st{ci}")
            P.dma("sync", lambda e, c=c: e.dma_start(
                out=yfin[:], in_=y_part[c * 128:(c + 1) * 128, :]),
                r=[yfink] + [f"{yfink}:{h}" for h in range(MH)], w=[yfink], stream="yp")
            P.dma("sync", lambda e, c=c: e.dma_start(
                out=zt[:], in_=z_tm[c * 128:(c + 1) * 128, :]),
                w=[ztk], stream="zt")
            P.dma("sync", lambda e, c=c: e.dma_start(
                out=cumc[:], in_=cum_d[c * 128:(c + 1) * 128, :]),
                w=["cumc"], stream="cum")
            P.op("scalar", lambda s: s.activation(ec[:], cumc[:], AF.Exp),
                 r=["cumc"], w=["ec"])
            P.op("tensor", lambda t, c=c: t.matmul(
                pmisc[0:MP, 0:MH], onesf_r[:, 0:MP], decs[0:1, c * MH:(c + 1) * MH],
                start=True, stop=True),
                r=["decs", "onesf_r", "decB"], w=["px0"])
            P.op("vector", lambda v: v.tensor_copy(decB[0:MP, :], pmisc[0:MP, 0:MH]),
                 r=["px0"], w=["decB"])
            for hg in range(MH // 4):
                P.op("vector", lambda v, hg=hg: v.tensor_copy(
                    shb[0:MP, :], S_t[:, hg * 4 * MN:(hg + 1) * 4 * MN]),
                    r=["S"], w=["shb"])
                for j in range(4):
                    g = j % 2
                    pt_ = ptq0 if g == 0 else ptq1
                    P.op("tensor", lambda t, j=j, pt_=pt_: t.transpose(
                        pt_[:, (j // 2) * 64:(j // 2) * 64 + 64],
                        shb[0:MP, j * MN:(j + 1) * MN], idb[0:MP, 0:MP]),
                        r=["shb", "idb"], w=[f"pt{g}"], inc=(j >= 2))
                P.op("vector", lambda v: v.tensor_copy(sthb[0:MN, 0:128], ptq0[:, 0:128]),
                     r=["pt0"], w=["sthb"], inc=False)
                P.op("vector", lambda v: v.tensor_copy(sthb[0:MN, 128:256], ptq1[:, 0:128]),
                     r=["pt1"], w=["sthb"])
                for j in range(4):
                    P.op("tensor", lambda t, c=c, j=j: t.matmul(
                        pyo[:, j * 128: j * 128 + MP], Cfm[:, c * 128:(c + 1) * 128],
                        sthb[0:MN, ((j % 2) * 128 + (j // 2) * 64):((j % 2) * 128 + (j // 2) * 64 + 64)],
                        start=True, stop=True),
                        r=["Cfm", "sthb"], w=["pb2"], inc=(j == 3))
                for j in range(4):
                    h = hg * 4 + j
                    P.op("vector", lambda v, h=h, j=j: v.scalar_tensor_tensor(
                        yfin[:, h * MP:(h + 1) * MP], pyo[:, j * 128: j * 128 + MP], ec[:, h:h + 1],
                        yfin[:, h * MP:(h + 1) * MP], ALU.mult, ALU.add),
                        r=["pb2", "ec", yfink], w=[f"{yfink}:{h}"], inc=(j == 3))
            for h in range(MH):
                P.op("vector", lambda v, h=h, ci=ci: v.scalar_tensor_tensor(
                    S_t[:, h * MN:(h + 1) * MN], S_t[:, h * MN:(h + 1) * MN],
                    decB[0:MP, h:h + 1], stc[ci][0:MP, h * MN:(h + 1) * MN],
                    ALU.mult, ALU.add),
                    r=["decB", stck[ci], "S"], w=["S"], inc=(h % 8 == 7))
            P.op("scalar", lambda s: s.activation(slz[:], zt[:], AF.Silu),
                 r=[ztk], w=[slzk])
            P.op("vector", lambda v: v.tensor_tensor(
                yfin[:], yfin[:], slz[:], ALU.mult),
                r=[slzk] + [f"{yfink}:{h}" for h in range(MH)], w=[yfink])
            P.op("scalar", lambda s: s.activation(slz[:], yfin[:], AF.Square,
                                                  accum_out=stat[:, 0:1]),
                 r=[yfink], w=[slzk, "st0"])
            P.op("vector", lambda v: v.tensor_scalar(stat[:, 1:2], stat[:, 0:1], 1.0 / ZDIM, 1e-5, ALU.mult, ALU.add),
                 r=["st0"], w=["st1"])
            P.op("scalar", lambda s: s.activation(stat[:, 2:3], stat[:, 1:2], AF.Sqrt),
                 r=["st1"], w=["st2"])
            P.op("vector", lambda v: v.reciprocal(stat[:, 3:4], stat[:, 2:3]),
                 r=["st2"], w=["st3"])
            P.op("vector", lambda v: v.scalar_tensor_tensor(
                yfin[:], yfin[:], stat[:, 3:4], NW[:], ALU.mult, ALU.mult),
                r=["st3", "NW", yfink], w=[yfink])
            for zg in range(nZC // 8):
                g = zg % 2
                pt_ = ptq0 if g == 0 else ptq1
                for j in range(8):
                    zc_ = zg * 8 + j
                    P.op("vector", lambda v, zc_=zc_, j=j: v.tensor_copy(
                        ynb[:, j * 128:(j + 1) * 128], yfin[:, zc_ * 128:(zc_ + 1) * 128]),
                        r=[yfink], w=["ynb"], inc=(j == 7))
                for j in range(8):
                    P.op("tensor", lambda t, j=j, pt_=pt_: t.transpose(
                        pt_[:, j * 128:(j + 1) * 128], ynb[:, j * 128:(j + 1) * 128], idb[:]),
                        r=["ynb", "idb"], w=[f"pt{g}"], inc=(j == 7))
                P.op("vector", lambda v, pt_=pt_: v.tensor_copy(ynb2[:, :], pt_[:, :]),
                     r=[f"pt{g}"], w=["ynb2"])
                for j in range(8):
                    zc_ = zg * 8 + j
                    P.dma("gpsimd", lambda e, c=c, zc_=zc_, j=j: e.dma_start(
                        out=yn_fm[zc_ * 128:(zc_ + 1) * 128, c * 128:(c + 1) * 128],
                        in_=ynb2[:, j * 128:(j + 1) * 128]),
                        r=["ynb2"], w=[f"ynf:{c}:{zc_}"], stream=f"so{zc_ % 2}")
        for k in ["af:S_tb", "af:decs", "af:decBb", "af:ec", "af:cumc",
                  "af:zt", "af:yfin", "af:slz"]:
            AF_.unpin(k)
        for k in ["ab:Cfm", "ab:NW", "ab:shb", "ab:sthb", "ab:ynb", "ab:ynb2", "ab:stc0", "ab:stc1"]:
            AB.unpin(k)
        AB.begin_phase()
        AF_.begin_phase()
        ynl, ynlk = [], []
        for i in range(1):
            s_, k_, _ = AB.alloc(f"ynl{i}", nZC * 512)
            ynl.append(s_), ynlk.append(k_)
        wol, wolk = [], []
        for i in range(2):
            s_, k_, _ = AB.alloc(f"wol{i}", nZC * 128)
            wol.append(s_), wolk.append(k_)
        oev, oevk = [], []
        for i in range(2):
            s_, k_, _ = AF_.alloc(f"oev{i}", 512)
            oev.append(s_), oevk.append(k_)
        for tb in range(nTB):
            ti = 0
            P.dma("sync", lambda e, tb=tb, ti=ti: e.dma_start(
                out=ynl[ti].rearrange("p (zc t) -> p zc t", zc=nZC),
                in_=yn_fm[:, tb * 512:(tb + 1) * 512].rearrange("(zc p) t -> p zc t", p=128)),
                r=[f"ynf:{c}:{zc_}" for c in range(tb * 4, tb * 4 + 4) for zc_ in range(nZC)],
                w=[ynlk[ti]], stream=f"yn{ti}")
            for fo in range(nCC):
                wi = fo % 2
                P.dma("gpsimd", lambda e, fo=fo, wi=wi: e.dma_start(
                    out=wol[wi].rearrange("p (zc f) -> p zc f", zc=nZC),
                    in_=woutT[:, fo * 128:(fo + 1) * 128].rearrange("(zc p) f -> p zc f", p=128)),
                    w=[wolk[wi]], stream=f"wo{wi}")
                pk = f"pb{fo % 2}"
                for zc_ in range(nZC):
                    P.op("tensor", lambda t, fo=fo, zc_=zc_, ti=ti, wi=wi: t.matmul(
                        pb[fo % 2][:], wol[wi][:, zc_ * 128:(zc_ + 1) * 128],
                        ynl[ti][:, zc_ * 512:(zc_ + 1) * 512],
                        start=(zc_ == 0), stop=(zc_ == nZC - 1)),
                        r=[wolk[wi], ynlk[ti]], w=[pk], inc=(zc_ == nZC - 1))
                P.op("vector", lambda v, fo=fo: v.tensor_copy(oev[fo % 2][:], pb[fo % 2][:]),
                     r=[pk], w=[oevk[fo % 2]])
                P.dma("gpsimd", lambda e, fo=fo, tb=tb: e.dma_start(
                    out=attn1_T[fo * 128:(fo + 1) * 128, tb * 512:(tb + 1) * 512],
                    in_=oev[fo % 2][:]),
                    r=[oevk[fo % 2]], stream=f"ao{fo % 2}")

        AF_.unpin("af:statb")
        AF_.unpin("af:nwrb")
        if fused_sinit:
            AF_.unpin("af:m0s_b")


QF = 5632
nQF = QF // 128




def build_launch_c2b(T=2048):
    return _standalone(emit_c2b, T=T)


def emit_d(E, T=2048):
    nTB = T // 512
    nTT = T // 128
    nc, P, AB, AF_ = E.nc, E.P, E.AB, E.AF
    AB.begin_phase()
    AF_.begin_phase()
    pg, pu, pmisc = E.pb[2], E.pb[3], E.px[0]
    pas1 = E.px[1]
    idf = E.idf
    onesf_r, onesb_c = E.onesf_r, E.onesb_c
    xn1_tm = E.inp("xn1_tm", [T, HID], BF16)
    x1_fm = E.inp("x1_fm", [HID, T], F32)
    at1_fm = E.inp("at1_fm", [HID, T], F32)
    wqgT = E.inp("wqgT", [HID, QF], I32)
    wquT = E.inp("wquT", [HID, QF], I32)
    wqdT2 = E.inp("wqdT2", [QF, HID], I32)
    sg_r = E.inp("sg_r", [nQF, 128], F32)
    su_r = E.inp("su_r", [nQF, 128], F32)
    sd_r = E.inp("sd_r", [nCC, 128], F32)
    outT = E.out("outT", [HID, T], F32)
    act_d = E.scratch("act_d", [QF, T], BF16)
    if True:
        sgc, _, _ = AF_.alloc("sgc", nQF, aliases=["sgc"])
        AF_.pin("af:sgc")
        suc, _, _ = AF_.alloc("suc", nQF, aliases=["suc"])
        AF_.pin("af:suc")
        sdc, _, _ = AF_.alloc("sdc", nCC, aliases=["sdc"])
        AF_.pin("af:sdc")
        xbc128, _, _ = AF_.alloc("xbc128", T, aliases=["xbc128"])
        AF_.pin("af:xbc128")
        xsd128, _, _ = AF_.alloc("xsd128", T, aliases=["xsd128"])
        AF_.pin("af:xsd128")
        srow, _, _ = AF_.alloc("srow", 512, parts=1, aliases=["srow"])
        AF_.pin("af:srow")
        nwr, _, _ = AF_.alloc("nwrd", 128, parts=nQF, aliases=["nwr"])
        AF_.pin("af:nwrd")
        for src_d, dst, k, n_ in [(sg_r, sgc, "sgc", nQF), (su_r, suc, "suc", nQF), (sd_r, sdc, "sdc", nCC)]:
            P.dma("sync", lambda e, src_d=src_d, n_=n_: e.dma_start(
                out=nwr[0:n_, :], in_=src_d), r=["nwr"], w=["nwr"], stream="cst")
            P.op("tensor", lambda t, n_=n_: t.transpose(pmisc[:, 0:n_], nwr[0:n_, :], idf[0:n_, 0:n_]),
                 r=["nwr", "idf"], w=["px0"])
            P.op("vector", lambda v, dst=dst, n_=n_: v.tensor_copy(dst[:, 0:n_], pmisc[:, 0:n_]),
                 r=["px0"], w=[k])

        xnf, xnfk, _ = AB.alloc("xnfd", nCC * T)
        AB.pin(xnfk)
        xnfks = [f"{xnfk}:{cc}" for cc in range(nCC)]
        for cc in range(nCC):
            P.dma("scalar", lambda e, cc=cc: e.dma_start(
                out=xnf[:, cc * T:(cc + 1) * T],
                in_=xn1_tm[:, cc * 128:(cc + 1) * 128], transpose=True),
                w=[xnfks[cc]], stream=f"xnf{cc % 2}")
        for tb in range(nTB):
            for cc in range(nCC):
                P.op("tensor", lambda t, tb=tb, cc=cc: t.matmul(
                    pmisc[0:1, :], onesb_c[:], xnf[:, cc * T + tb * 512: cc * T + tb * 512 + 512],
                    start=(cc == 0), stop=(cc == nCC - 1)),
                    r=["onesb_c", xnfks[cc]], w=["px0"], inc=(cc == nCC - 1))
            P.op("vector", lambda v: v.tensor_copy(srow[0:1, :], pmisc[0:1, :]),
                 r=["px0"], w=["srow"])
            P.op("tensor", lambda t: t.matmul(pmisc[:, :], onesf_r[:], srow[0:1, :],
                                              start=True, stop=True),
                 r=["srow", "onesf_r"], w=["px0"])
            P.op("vector", lambda v, tb=tb: v.tensor_scalar(
                xbc128[:, tb * 512:(tb + 1) * 512], pmisc[:, :], -128.0, None, ALU.mult),
                r=["px0"], w=["xbc128"])

        wg, wgk = [], []
        wu, wuk = [], []
        for i in range(2):
            s_, k_, _ = AB.alloc(f"wg{i}", nCC * 128)
            wg.append(s_), wgk.append(k_)
            s_, k_, _ = AB.alloc(f"wu{i}", nCC * 128)
            wu.append(s_), wuk.append(k_)
        t1t, t1k, _ = AF_.alloc("t1", 512)
        t2t, t2k, _ = AF_.alloc("t2", 512)
        gvt, gvk, _ = AF_.alloc("gv", 512)
        actb, actbk, _ = AB.alloc("actb", 512)

        for fb in range(nQF):
            wi = fb % 2
            P.dma("gpsimd", lambda e, fb=fb, wi=wi: e.dma_start(
                out=wg[wi].rearrange("p (cc f) -> p cc f", cc=nCC),
                in_=wqgT[:, fb * 128:(fb + 1) * 128].rearrange("(cc p) f -> p cc f", p=128)),
                w=[wgk[wi]], stream=f"wg{wi}")
            P.dma("gpsimd", lambda e, fb=fb, wi=wi: e.dma_start(
                out=wu[wi].rearrange("p (cc f) -> p cc f", cc=nCC),
                in_=wquT[:, fb * 128:(fb + 1) * 128].rearrange("(cc p) f -> p cc f", p=128)),
                w=[wuk[wi]], stream=f"wu{wi}")
            for tb in range(nTB):
                for cc in range(nCC):
                    P.op("tensor", lambda t, cc=cc, tb=tb, wi=wi: t.matmul(
                        pg[:], wg[wi][:, cc * 128:(cc + 1) * 128],
                        xnf[:, cc * T + tb * 512: cc * T + tb * 512 + 512],
                        start=(cc == 0), stop=(cc == nCC - 1)),
                        r=[wgk[wi], xnfks[cc]], w=["pb2"], inc=(cc == nCC - 1))
                for cc in range(nCC):
                    P.op("tensor", lambda t, cc=cc, tb=tb, wi=wi: t.matmul(
                        pu[:], wu[wi][:, cc * 128:(cc + 1) * 128],
                        xnf[:, cc * T + tb * 512: cc * T + tb * 512 + 512],
                        start=(cc == 0), stop=(cc == nCC - 1)),
                        r=[wuk[wi], xnfks[cc]], w=["pb3"], inc=(cc == nCC - 1))
                P.op("vector", lambda v, tb=tb: v.tensor_tensor(
                    t1t[:], pg[:], xbc128[:, tb * 512:(tb + 1) * 512], ALU.add),
                    r=["pb2", "xbc128"], w=[t1k])
                P.op("scalar", lambda s, fb=fb: s.activation(
                    gvt[:], t1t[:], AF.Silu, scale=sgc[:, fb:fb + 1]),
                    r=[t1k, "sgc"], w=[gvk])
                P.op("vector", lambda v, tb=tb: v.tensor_tensor(
                    t2t[:], pu[:], xbc128[:, tb * 512:(tb + 1) * 512], ALU.add),
                    r=["pb3", "xbc128"], w=[t2k])
                P.op("vector", lambda v, fb=fb: v.scalar_tensor_tensor(
                    actb[:], t2t[:], suc[:, fb:fb + 1], gvt[:], ALU.mult, ALU.mult),
                    r=[t2k, "suc", gvk], w=[actbk])
                P.dma("gpsimd", lambda e, fb=fb, tb=tb: e.dma_start(
                    out=act_d[fb * 128:(fb + 1) * 128, tb * 512:(tb + 1) * 512],
                    in_=actb[:]),
                    r=[actbk], w=[f"act:{fb}:{tb}"], stream=f"so{tb % 2}")

        AB.unpin(xnfk)
        AB.begin_phase()
        AF_.begin_phase()
        acl, aclk, _ = AB.alloc("acl", nQF * 512)
        wd, wdk = [], []
        for i in range(2):
            s_, k_, _ = AB.alloc(f"wd{i}", nQF * 128)
            wd.append(s_), wdk.append(k_)
        x1c, x1ck, _ = AF_.alloc("x1c", 512)
        a1c, a1ck, _ = AF_.alloc("a1c", 512)
        otv, otk, _ = AF_.alloc("ot", 512)
        for tb in range(nTB):
            P.dma("sync", lambda e, tb=tb: e.dma_start(
                out=acl.rearrange("p (fb t) -> p fb t", fb=nQF),
                in_=act_d[:, tb * 512:(tb + 1) * 512].rearrange("(fb p) t -> p fb t", p=128)),
                r=[f"act:{fb}:{tb}" for fb in range(nQF)], w=[aclk], stream="acl")
            for fb in range(nQF):
                P.op("tensor", lambda t, fb=fb: t.matmul(
                    pas1[0:1, :], onesb_c[:], acl[:, fb * 512:(fb + 1) * 512],
                    start=(fb == 0), stop=(fb == nQF - 1)),
                    r=[aclk, "onesb_c"], w=["px1"], inc=(fb == nQF - 1))
            P.op("vector", lambda v: v.tensor_copy(srow[0:1, :], pas1[0:1, :]),
                 r=["px1", "srow"], w=["srow"])
            P.op("tensor", lambda t: t.matmul(pmisc[:, :], onesf_r[:], srow[0:1, :],
                                              start=True, stop=True),
                 r=["srow", "onesf_r"], w=["px0"])
            P.op("vector", lambda v, tb=tb: v.tensor_scalar(
                xsd128[:, tb * 512:(tb + 1) * 512], pmisc[:, :], -128.0, None, ALU.mult),
                r=["px0"], w=["xsd128"])
            for ho in range(nCC):
                wi = ho % 2
                P.dma("gpsimd", lambda e, ho=ho, wi=wi: e.dma_start(
                    out=wd[wi].rearrange("p (fb f) -> p fb f", fb=nQF),
                    in_=wqdT2[:, ho * 128:(ho + 1) * 128].rearrange("(fb p) f -> p fb f", p=128)),
                    w=[wdk[wi]], stream=f"wd{wi}")
                for fb in range(nQF):
                    P.op("tensor", lambda t, fb=fb, wi=wi: t.matmul(
                        pg[:], wd[wi][:, fb * 128:(fb + 1) * 128],
                        acl[:, fb * 512:(fb + 1) * 512],
                        start=(fb == 0), stop=(fb == nQF - 1)),
                        r=[wdk[wi], aclk], w=["pb2"], inc=(fb == nQF - 1))
                P.dma("sync", lambda e, ho=ho, tb=tb: e.dma_start(
                    out=x1c[:], in_=x1_fm[ho * 128:(ho + 1) * 128, tb * 512:(tb + 1) * 512]),
                    r=[x1ck], w=[x1ck], stream="x1")
                P.dma("sync", lambda e, ho=ho, tb=tb: e.dma_start(
                    out=a1c[:], in_=at1_fm[ho * 128:(ho + 1) * 128, tb * 512:(tb + 1) * 512]),
                    r=[a1ck], w=[a1ck], stream="a1")
                P.op("vector", lambda v, tb=tb: v.tensor_tensor(
                    t1t[:], pg[:], xsd128[:, tb * 512:(tb + 1) * 512], ALU.add),
                    r=["pb2", "xsd128"], w=[t1k])
                P.op("vector", lambda v, ho=ho: v.scalar_tensor_tensor(
                    otv[:], t1t[:], sdc[:, ho:ho + 1], x1c[:], ALU.mult, ALU.add),
                    r=[t1k, "sdc", x1ck], w=[otk])
                P.op("vector", lambda v: v.tensor_tensor(otv[:], otv[:], a1c[:], ALU.add),
                     r=[otk, a1ck], w=[otk])
                P.dma("gpsimd", lambda e, ho=ho, tb=tb: e.dma_start(
                    out=outT[ho * 128:(ho + 1) * 128, tb * 512:(tb + 1) * 512],
                    in_=otv[:]),
                    r=[otk], stream=f"oo{ho % 2}")

        for k in ["af:sgc", "af:suc", "af:sdc", "af:xbc128", "af:xsd128",
                  "af:srow", "af:nwrd"]:
            AF_.unpin(k)




def build_launch_d(T=2048):
    return _standalone(emit_d, T=T)


def stage_c2b(inputs, res_c2a, res_c1, T=2048):
    nCH = T // 128
    tr = lambda w: np.ascontiguousarray(np.asarray(w).T)
    shared = dict(
        nw_row=np.asarray(inputs["L1_attn_norm_w"]).reshape(1, ZDIM),
        woutT=tr(inputs["L1_attn_W_out"]),
        identf=np.eye(128, dtype=np.float32),
        identb=np.eye(128, dtype=BF),
    )
    in_maps = []
    for c in range(8):
        hh = c % 2
        if hh == 0:
            s_init = np.zeros((MP, MH * MN), np.float32)
        else:
            s_init = np.asarray(res_c2a[c - 1]["sloc"], np.float32)
        m = dict(shared)
        m.update(
            y_part=np.asarray(res_c2a[c]["y_part"]),
            states_a=np.asarray(res_c2a[c]["states_a"]),
            decays_d=np.ascontiguousarray(np.asarray(res_c2a[c]["decays_d"]).reshape(1, nCH * MH)),
            s_init=s_init,
            c_fm_d=np.asarray(res_c2a[c]["c_fm"]),
            cum_d=np.asarray(res_c2a[c]["cum_out"]),
            z_tm=np.asarray(res_c1[c]["z_tm"]),
        )
        in_maps.append(m)
    return in_maps


def stage_d(inputs, res_c1, res_c2b, x1, T=2048):
    S = x1.shape[1]
    So = S // 2
    tr = lambda w: np.ascontiguousarray(np.asarray(w).T)
    shared = dict(
        wqgT=tr(inputs["L1_ffn_wq_gate"]),
        wquT=tr(inputs["L1_ffn_wq_up"]),
        wqdT2=tr(inputs["L1_ffn_wq_down"]),
        sg_r=np.ascontiguousarray(np.asarray(inputs["L1_ffn_s_gate"]).reshape(nQF, 128)),
        su_r=np.ascontiguousarray(np.asarray(inputs["L1_ffn_s_up"]).reshape(nQF, 128)),
        sd_r=np.ascontiguousarray(np.asarray(inputs["L1_ffn_s_down"]).reshape(nCC, 128)),
        identf=np.eye(128, dtype=np.float32),
        identb=np.eye(128, dtype=BF),
    )
    in_maps = []
    for c in range(8):
        b, hh = c // 2, c % 2
        m = dict(shared)
        m.update(
            xn1_tm=np.asarray(res_c1[c]["xn1_out"]),
            x1_fm=np.ascontiguousarray(x1[b, hh * So: hh * So + T].T),
            at1_fm=np.asarray(res_c2b[c]["attn1_T"]),
        )
        in_maps.append(m)
    return in_maps


RG_PAIRS = [[0, 1], [2, 3], [4, 5], [6, 7]]


def build_mega(S=4096):
    T = S // 2
    nCH = T // 128
    nc = bass.Bass("TRN2", target_bir_lowering=False)
    ctx = ExitStack()
    with ctx:
        E = Env(nc, ctx, fused=True)
        P = E.P
        emit_a(E, S)
        P.fence()
        E.wiring["xn_tm"] = E._dram["xn_all"][0:T, :]
        E.wiring["attn_fm"] = E._dram["attn_outT"]
        emit_b(E, T)
        P.fence()
        emit_c1(E, T)
        cc1_in = E.scratch("cc1_in", [CONVD, 512], F32)
        cc1_out = E.scratch("cc1_out", [2 * CONVD, 512], F32)
        P.dma("gpsimd", lambda e: e.dma_start(
            out=cc1_in, in_=E._dram["xbc_raw"][:, T - 512:T]),
            r=[f"xbcs:{fc}:{T // 512 - 1}" for fc in range(nXBC)],
            w=["cc1_in"], stream="cc")
        P.cc(lambda g: g.collective_compute(
            "AllGather", ALU.bypass, replica_groups=RG_PAIRS,
            ins=[cc1_in], outs=[cc1_out]),
            r=["cc1_in"], w=["cc1_out"])
        E.wiring["cc1_halo"] = cc1_out[0:CONVD, 509:512]
        E.wiring["cc1_tok"] = P.key_toks("cc1_out")
        P.fence()
        emit_c2a(E, T, fused_halo=True)
        cc2_out = E.scratch("cc2_out", [2 * MP, MH * MN], F32)
        P.cc(lambda g: g.collective_compute(
            "AllGather", ALU.bypass, replica_groups=RG_PAIRS,
            ins=[E._dram["sloc"]], outs=[cc2_out]),
            r=["sloc_s"], w=["cc2_out"])
        E.wiring["cc2_sinit"] = cc2_out[0:MP, :]
        E.wiring["cc2_tok"] = P.key_toks("cc2_out")
        P.fence()
        emit_c2b(E, T, fused_sinit=True)
        P.fence()
        E.wiring["xn1_tm"] = E._dram["xn1_out"]
        E.wiring["x1_fm"] = E._dram["x1T"]
        E.wiring["at1_fm"] = E._dram["attn1_T"]
        emit_d(E, T)
        E.finish()
    return nc


def stage_mega(inputs, S=4096):
    T = S // 2
    x = np.asarray(inputs["L0_attn_x"])
    im_a = stage_a(inputs, S)
    tr = lambda w: np.ascontiguousarray(np.asarray(w).T)
    tri, mnegw = _tri_mask()
    cw = np.asarray(inputs["L1_attn_conv_w"]).reshape(nXBC, 128, 4)
    cw = np.ascontiguousarray(cw.transpose(1, 0, 2).reshape(128, nXBC * 4))
    shared = dict(
        gwT=tr(inputs["L0_ffn_gate_weight"]),
        guT=np.ascontiguousarray(np.asarray(inputs["L0_ffn_gate_up_weights"]).transpose(0, 2, 1)),
        dwT=np.ascontiguousarray(np.asarray(inputs["L0_ffn_down_weights"]).transpose(0, 2, 1)),
        sguT=tr(inputs["L0_ffn_shared_gate_up"]),
        sdT=tr(inputs["L0_ffn_shared_down"]),
        winT=tr(inputs["L1_attn_W_in"]),
        cw_d=cw,
        dtb_d=np.asarray(inputs["L1_attn_dt_bias"]).reshape(1, MH),
        alog_d=np.asarray(inputs["L1_attn_A_log"]).reshape(1, MH),
        dd_d=np.asarray(inputs["L1_attn_D"]).reshape(1, MH),
        tri_d=tri, mnegw_d=mnegw,
        nw_row=np.asarray(inputs["L1_attn_norm_w"]).reshape(1, ZDIM),
        woutT=tr(inputs["L1_attn_W_out"]),
        wqgT=tr(inputs["L1_ffn_wq_gate"]),
        wquT=tr(inputs["L1_ffn_wq_up"]),
        wqdT2=tr(inputs["L1_ffn_wq_down"]),
        sg_r=np.ascontiguousarray(np.asarray(inputs["L1_ffn_s_gate"]).reshape(nQF, 128)),
        su_r=np.ascontiguousarray(np.asarray(inputs["L1_ffn_s_up"]).reshape(nQF, 128)),
        sd_r=np.ascontiguousarray(np.asarray(inputs["L1_ffn_s_down"]).reshape(nCC, 128)),
    )
    in_maps = []
    for c in range(8):
        b, hh = c // 2, c % 2
        m = dict(im_a[c])
        m.update(shared)
        m.update(
            x_fm=np.ascontiguousarray(x[b, hh * T:(hh + 1) * T].T),
            m0col=np.full((128, 1), float(hh), np.float32),
        )
        in_maps.append(m)
    return in_maps


def kernel(**inputs):
    S = 4096
    T = 2048
    nc = build_mega(S)
    res = run_bass_kernel_spmd(nc, stage_mega(inputs, S), core_ids=list(range(8))).results
    out = np.zeros((4, S, HID), np.float32)
    for c in range(8):
        b, hh = c // 2, c % 2
        out[b, hh * T:(hh + 1) * T] = np.asarray(res[c]["outT"]).T
    return out
